# version 14
# baseline (speedup 1.0000x reference)
import math
import numpy as np

import concourse.bass as bass
import concourse.mybir as mybir
from concourse.bass_utils import run_bass_kernel_spmd

F32 = mybir.dt.float32
BF16 = mybir.dt.bfloat16
AF = mybir.ActivationFunctionType
ALU = mybir.AluOpType
AX = mybir.AxisListType.X
AXY = mybir.AxisListType.XY

T, D, HID = 4096, 1024, 4096
NT, DC, SC, HC = T // 128, D // 128, T // 256, HID // 128
MAGIC = 1.5 * 2.0 ** 23
INV1024 = 1.0 / 1024.0
NW = 1.0 / (HID * D)

N1_FULL, N2_FULL = 8 * T * D, 8 * T * HID
_p1 = 0.995 * (N1_FULL - 1)
I1 = math.floor(_p1)
FRAC1, R1 = _p1 - I1, N1_FULL - I1
_p2 = 0.995 * (N2_FULL - 1)
I2 = math.floor(_p2)
FRAC2, R2 = _p2 - I2, N2_FULL - I2
N1_CAND, N2_CAND = 8 * 128 * 1024, 8 * 128 * 4096
NR = 11
NR1 = 8
DBG_Q1 = False
RG = [list(range(8))]

B_LO, B_HI, B_W, B_NB, B_IDX = 0, 1, 2, 3, 4
B_T0, B_T1, B_CHI, B_AMAX, B_S, B_GAM, B_GS = 5, 6, 7, 8, 9, 10, 11
B_CNT, B_PRED, B_THR, B_NTH = 16, 24, 32, 40
B_FRC3, B_FRC7 = 48, 52


def build_program(stop=None):
    nc = bass.Bass("TRN2", target_bir_lowering=False)

    x_d = nc.dram_tensor("L0_attn_x", [T, D], F32, kind="ExternalInput").ap()
    W_d = [nc.dram_tensor(f"L0_attn_W{i}", [D, D], F32, kind="ExternalInput").ap()
           for i in (1, 2, 3, 4)]
    b_d = [nc.dram_tensor(f"L0_attn_b{i}", [D], F32, kind="ExternalInput").ap()
           for i in (1, 2, 3, 4)]
    lng_d = nc.dram_tensor("L0_ffn_ln_g", [D], F32, kind="ExternalInput").ap()
    lnb_d = nc.dram_tensor("L0_ffn_ln_b", [D], F32, kind="ExternalInput").ap()
    w1T_d = nc.dram_tensor("w1T", [D, HID], F32, kind="ExternalInput").ap()
    w2T_d = nc.dram_tensor("w2T", [HID, D], F32, kind="ExternalInput").ap()
    out_d = nc.dram_tensor("out", [T, D], F32, kind="ExternalOutput").ap()

    QT_d = nc.dram_tensor("QT_d", [SC, D, 256], BF16).ap()
    KT_d = nc.dram_tensor("KT_d", [D, T], BF16).ap()
    V_d = nc.dram_tensor("V_d", [T, D], BF16).ap()
    gT_d = nc.dram_tensor("gT_d", [NT, HID, 128], BF16).ap()
    AP_d = nc.dram_tensor("AP_d", [T, D], BF16).ap()
    cc_in_t = nc.dram_tensor("cc_in_d", [8192], F32)
    cc_out_t = nc.dram_tensor("cc_out_d", [8192], F32)
    row_t = nc.dram_tensor("row_d", [1, 64], F32, kind="ExternalOutput")
    dbg_t = nc.dram_tensor("dbg_d", [16, 32], F32, kind="ExternalOutput")
    dbg2_t = nc.dram_tensor("dbg2_d", [128, 1024], F32, kind="ExternalOutput")

    live = {}

    def enter(name, shape, dtype, kind="sbuf", side="left"):
        if kind == "sbuf":
            cm = nc.sbuf_tensor(name, shape, dtype, side=side)
        else:
            cm = nc.psum_tensor(name, shape, dtype)
        h = cm.__enter__()
        live[name] = cm
        return h

    def free(*names):
        for n in names:
            live.pop(n).__exit__(None, None, None)

    from contextlib import ExitStack
    es = ExitStack()

    def sem(name):
        return es.enter_context(nc.semaphore(name))

    ident = enter("ident", [128, 128], BF16, side="right")
    pidx = enter("pidx", [128, 1], F32, side="right")
    ones_col = enter("ones_col", [128, 1], BF16, side="right")
    ones_row = enter("ones_row", [1, 128], F32, side="right")
    ones_rowb = enter("ones_rowb", [1, 128], BF16, side="right")
    b_sb = enter("b_sb", [128, 4 * DC], F32, side="right")
    b4_bc = enter("b4_bc", [128, D], F32, side="right")

    bis = enter("bis", [1, 64], F32, side="right")
    bisb = enter("bisb", [128, 8], F32, side="right")
    srow = enter("srow", [1, 1024], F32, side="right")
    cstat = enter("cstat", [128, 64], F32, side="right")
    sc1 = enter("sc1", [128, 8], F32, side="right")
    jk_f = enter("jk_f", [128, D], F32, side="right")
    mk_f = enter("mk_f", [128, 256], F32, side="right")

    pT1 = enter("pT1", [128, 1024], BF16, "psum")
    pT2 = enter("pT2", [128, 1024], BF16, "psum")
    pA = enter("pA", [128, 512], F32, "psum")
    pB = enter("pB", [128, 512], F32, "psum")
    pR = enter("pR", [128, 512], F32, "psum")
    pO1 = enter("pO1", [128, 512], F32, "psum")
    pO2 = enter("pO2", [128, 512], F32, "psum")
    pJ = enter("pJ", [128, 512], F32, "psum")

    m32 = enter("m32", [1, T], BF16, side="right")
    big1 = enter("big1", [128, DC * T], BF16, side="right")
    big2 = enter("big2", [128, DC * T], BF16, side="right")
    xnT = xqT = KTs = w2q = big1
    ln2T = w1q = Vs = big2

    q1cand = enter("q1cand", [128, 1024], F32)
    identf = enter("identf", [128, 128], F32)
    iot = enter("iot", [128, 128], F32)
    xt = [enter(f"xt{i}", [128, D], F32) for i in range(2)]
    xn_f = [enter(f"xn_f{i}", [128, D], F32) for i in range(2)]
    l2_f = enter("l2_f", [128, D], F32)
    xnb = [enter(f"xnb{i}", [128, D], BF16) for i in range(2)]
    l2b = [enter(f"l2b{i}", [128, D], BF16) for i in range(2)]
    g_bc = enter("g_bc", [128, D], F32)
    bgb_bc = enter("bgb_bc", [128, D], F32)
    st_sum = enter("st_sum", [128, NT], F32)
    st_sq = enter("st_sq", [128, NT], F32)
    rstd = enter("rstd", [128, NT], F32)
    negmr = enter("negmr", [128, NT], F32)
    rs2 = enter("rs2", [128, NT], F32)
    stt1 = enter("stt1", [128, NT], F32)
    stt2 = enter("stt2", [128, NT], F32)

    s_x = sem("s_x")
    s_w = sem("s_w")
    s_a = sem("s_a")
    s_d = sem("s_d")
    s_p = sem("s_p")
    s_sp = sem("s_sp")
    s_rl = sem("s_rl")
    s_cc = sem("s_cc")
    s_bp = sem("s_bp")
    s_bv = sem("s_bv")
    s_g = sem("s_g")
    s_dbg = sem("s_dbg")
    s_st = sem("s_st")
    s_wg = sem("s_wg")

    C = dict(x=0, w=0, a=0, d=0, p=0, sp=0, rl=0, cc=0, bp=0, bv=0, g=0, st=0,
             wg=0)


    def finish_early():
        fin = enter("fin_sb", [128, 64], F32)
        with nc.Block() as blk:
            @blk.vector
            def _(v):
                v.memset(fin[:, :], 1.0).then_inc(s_d, 1)

            @blk.sync
            def _(sync):
                sync.wait_ge(s_d, C["d"] + 1)
                sync.dma_start(out=out_d[0:128, 0:64], in_=fin[:, :]).then_inc(s_st, 16)
                sync.wait_ge(s_st, C["st"] + 16)
        es.close()
        return nc

    def vsp(v):
        return v.tensor_scalar(jk_f[:, 512:576], jk_f[:, 512:576], 1.0, None,
                               op0=ALU.mult)

    def make_bisect(tag, cand, ncol, Ntot, rank, frac, s_slot, gs_slot, base,
                    ro=8, dbg=None, nthr=3, nrnd=None, frc=None):
        nch = ncol // 512
        if nrnd is None:
            nrnd = NR
        if frc is None:
            frc = B_FRC3
        inv = 1.0 / (nthr + 1)
        ncc = nthr * nch
        act_steps, dve_steps = [], []

        def dve_init(v):
            v.reduce_max(cstat[:, 0:1], cand[:, :], axis=AX)
            vsp(v).then_inc(s_bv, 1)
        dve_steps.append(dve_init)

        def dve_init2(v):
            v.wait_ge(s_bp, base["bp"] + 16 * 3)
            v.reduce_max(bis[0:1, B_HI:B_HI + 1], srow[0:1, 0:128], axis=AX)
            vsp(v)
            v.memset(bis[0:1, B_LO:B_LO + 1], 0.0)
            vsp(v).then_inc(s_bv, 1)
        dve_steps.append(dve_init2)

        def mk_dve_round(j):
            def f(v):
                if j > 0:
                    v.wait_ge(s_bp, base["bp"] + 16 * (3 + 4 * j))
                    for k in range(nthr):
                        v.tensor_reduce(
                            bis[0:1, B_CNT + k:B_CNT + k + 1],
                            srow[0:1, 0:128 * nthr].rearrange(
                                "q (p c) -> q p c", p=128)[:, :, k:k + 1],
                            axis=AXY, op=ALU.add)
                    if j <= nrnd:
                        vsp(v)
                        v.tensor_scalar(bis[0:1, B_CNT:B_CNT + nthr], bis[0:1, B_CNT:B_CNT + nthr],
                                        0.5, Ntot * 0.5, op0=ALU.mult, op1=ALU.add)
                        vsp(v)
                        v.tensor_scalar(bis[0:1, B_PRED:B_PRED + nthr], bis[0:1, B_CNT:B_CNT + nthr],
                                        float(rank) - 0.5, None, op0=ALU.is_lt)
                        vsp(v)
                        v.reduce_sum(bis[0:1, B_NB:B_NB + 1], bis[0:1, B_PRED:B_PRED + nthr], axis=AX)
                        vsp(v)
                        v.tensor_scalar(bis[0:1, B_IDX:B_IDX + 1], bis[0:1, B_NB:B_NB + 1],
                                        -1.0, float(nthr), op0=ALU.mult, op1=ALU.add)
                        v.tensor_tensor(bis[0:1, B_W:B_W + 1], bis[0:1, B_HI:B_HI + 1],
                                        bis[0:1, B_LO:B_LO + 1], op=ALU.subtract)
                        vsp(v)
                        v.tensor_tensor(bis[0:1, B_T0:B_T0 + 1], bis[0:1, B_W:B_W + 1],
                                        bis[0:1, B_IDX:B_IDX + 1], op=ALU.mult)
                        vsp(v)
                        v.tensor_scalar(bis[0:1, B_T0:B_T0 + 1], bis[0:1, B_T0:B_T0 + 1],
                                        inv, None, op0=ALU.mult)
                        vsp(v)
                        v.tensor_tensor(bis[0:1, B_LO:B_LO + 1], bis[0:1, B_LO:B_LO + 1],
                                        bis[0:1, B_T0:B_T0 + 1], op=ALU.add)
                        v.tensor_scalar(bis[0:1, B_T1:B_T1 + 1], bis[0:1, B_W:B_W + 1],
                                        inv, None, op0=ALU.mult)
                        vsp(v)
                        v.tensor_tensor(bis[0:1, B_HI:B_HI + 1], bis[0:1, B_LO:B_LO + 1],
                                        bis[0:1, B_T1:B_T1 + 1], op=ALU.add)
                        vsp(v)
                if j <= nrnd - 1:
                    v.tensor_tensor(bis[0:1, B_W:B_W + 1], bis[0:1, B_HI:B_HI + 1],
                                    bis[0:1, B_LO:B_LO + 1], op=ALU.subtract)
                    vsp(v)
                    v.tensor_scalar(bis[0:1, B_THR:B_THR + nthr], bis[0:1, frc:frc + nthr],
                                    bis[0:1, B_W:B_W + 1], None, op0=ALU.mult)
                    vsp(v)
                    v.tensor_scalar(bis[0:1, B_THR:B_THR + nthr], bis[0:1, B_THR:B_THR + nthr],
                                    bis[0:1, B_LO:B_LO + 1], None, op0=ALU.add)
                elif j == nrnd:
                    v.tensor_scalar(bis[0:1, B_THR:B_THR + nthr], bis[0:1, frc:frc + nthr],
                                    0.0, None, op0=ALU.mult)
                    vsp(v)
                    v.tensor_scalar(bis[0:1, B_THR:B_THR + nthr], bis[0:1, B_THR:B_THR + nthr],
                                    bis[0:1, B_HI:B_HI + 1], None, op0=ALU.add)
                vsp(v)
                v.tensor_scalar(bis[0:1, B_NTH:B_NTH + nthr], bis[0:1, B_THR:B_THR + nthr],
                                -1.0, None, op0=ALU.mult)
                vsp(v).then_inc(s_bv, 1)
            return f

        for j in range(nrnd + 1):
            dve_steps.append(mk_dve_round(j))

        def mk_act_round(j):
            def f(act):
                act.wait_ge(s_bp, base["bp"] + 16 * (3 + 4 * j + 2))
                mm = None
                for k in range(nthr):
                    for c in range(nch):
                        mm = act.activation(
                            pJ[:, :], cand[:, c * 512:(c + 1) * 512], AF.Sign,
                            bias=bisb[:, k:k + 1],
                            accum_out=cstat[:, j % 2 * 32 + k * nch + c:
                                            j % 2 * 32 + k * nch + c + 1])
                mm.then_inc(s_a, 1)
            return f

        for j in range(nrnd + 1):
            act_steps.append(mk_act_round(j))

        def dve_extract(v):
            v.wait_ge(s_bp, base["bp"] + 16 * (3 + 4 * (nrnd + 1)))
            v.tensor_reduce(
                bis[0:1, B_CHI:B_CHI + 1],
                srow[0:1, 0:128 * nthr].rearrange("q (p c) -> q p c", p=128)[:, :, 0:1],
                axis=AXY, op=ALU.add)
            vsp(v)
            v.tensor_scalar(bis[0:1, B_CHI:B_CHI + 1], bis[0:1, B_CHI:B_CHI + 1],
                            0.5, Ntot * 0.5, op0=ALU.mult, op1=ALU.add)
            v.tensor_scalar(bisb[:, 7:8], bisb[:, 0:1], -1.0, None, op0=ALU.mult)
            vsp(v)
            v.tensor_scalar(jk_f[:, 0:512], cand[:, 0:512], bisb[:, 7:8], -1e30,
                            op0=ALU.is_gt, op1=ALU.mult)
            v.tensor_tensor(jk_f[:, 0:512], jk_f[:, 0:512], cand[:, 0:512], op=ALU.add)
            v.reduce_max(cstat[:, 0:1], jk_f[:, 0:512], axis=AX)
            for c in range(1, nch):
                v.tensor_scalar(jk_f[:, 0:512], cand[:, c * 512:(c + 1) * 512],
                                bisb[:, 7:8], -1e30, op0=ALU.is_gt, op1=ALU.mult)
                v.tensor_tensor(jk_f[:, 0:512], jk_f[:, 0:512],
                                cand[:, c * 512:(c + 1) * 512], op=ALU.add)
                v.reduce_max(cstat[:, 1:2], jk_f[:, 0:512], axis=AX)
                vsp(v)
                v.tensor_tensor(cstat[:, 0:1], cstat[:, 0:1], cstat[:, 1:2], op=ALU.max)
            v.tensor_scalar(jk_f[:, 0:512], cand[:, 0:512], bisb[:, 7:8], 1e30,
                            op0=ALU.is_le, op1=ALU.mult)
            v.tensor_tensor(jk_f[:, 0:512], jk_f[:, 0:512], cand[:, 0:512], op=ALU.add)
            v.tensor_reduce(cstat[:, 2:3], jk_f[:, 0:512], axis=AX, op=ALU.min)
            for c in range(1, nch):
                v.tensor_scalar(jk_f[:, 0:512], cand[:, c * 512:(c + 1) * 512],
                                bisb[:, 7:8], 1e30, op0=ALU.is_le, op1=ALU.mult)
                v.tensor_tensor(jk_f[:, 0:512], jk_f[:, 0:512],
                                cand[:, c * 512:(c + 1) * 512], op=ALU.add)
                v.tensor_reduce(cstat[:, 3:4], jk_f[:, 0:512], axis=AX, op=ALU.min)
                vsp(v)
                v.tensor_tensor(cstat[:, 2:3], cstat[:, 2:3], cstat[:, 3:4],
                                op=ALU.min)
            vsp(v).then_inc(s_bv, 1)
        dve_steps.append(dve_extract)

        def dve_final(v):
            v.wait_ge(s_bp, base["bp"] + 16 * (3 + 4 * (nrnd + 1) + 4))
            v.tensor_reduce(bis[0:1, B_T0:B_T0 + 1],
                            srow[0:1, 0:256].rearrange("q (p c) -> q p c", p=128)[:, :, 0:1],
                            axis=AXY, op=ALU.max)
            v.tensor_reduce(bis[0:1, B_T1:B_T1 + 1],
                            srow[0:1, 256:512].rearrange("q (p c) -> q p c", p=128)[:, :, 0:1],
                            axis=AXY, op=ALU.min)
            v.tensor_scalar(bis[0:1, B_PRED:B_PRED + 1], bis[0:1, B_CHI:B_CHI + 1],
                            float(rank - 1), None, op0=ALU.is_equal)
            vsp(v)
            v.tensor_tensor(bis[0:1, B_T1:B_T1 + 1], bis[0:1, B_T1:B_T1 + 1],
                            bis[0:1, B_T0:B_T0 + 1], op=ALU.subtract)
            vsp(v)
            v.tensor_tensor(bis[0:1, B_T1:B_T1 + 1], bis[0:1, B_T1:B_T1 + 1],
                            bis[0:1, B_PRED:B_PRED + 1], op=ALU.mult)
            vsp(v)
            v.tensor_scalar(bis[0:1, B_T1:B_T1 + 1], bis[0:1, B_T1:B_T1 + 1],
                            float(frac), None, op0=ALU.mult)
            vsp(v)
            v.tensor_tensor(bis[0:1, B_AMAX:B_AMAX + 1], bis[0:1, B_T0:B_T0 + 1],
                            bis[0:1, B_T1:B_T1 + 1], op=ALU.add)
            vsp(v)
            v.tensor_scalar(bis[0:1, B_AMAX:B_AMAX + 1], bis[0:1, B_AMAX:B_AMAX + 1],
                            1e-5, None, op0=ALU.max)
            vsp(v)
            v.reciprocal(bis[0:1, B_S:B_S + 1], bis[0:1, B_AMAX:B_AMAX + 1])
            vsp(v)
            v.tensor_scalar(bis[0:1, B_S:B_S + 1], bis[0:1, B_S:B_S + 1],
                            127.0, None, op0=ALU.mult)
            v.tensor_tensor(bis[0:1, B_GS:B_GS + 1], bis[0:1, B_GAM:B_GAM + 1],
                            bis[0:1, B_AMAX:B_AMAX + 1], op=ALU.mult)
            vsp(v)
            v.tensor_scalar(bis[0:1, B_GS:B_GS + 1], bis[0:1, B_GS:B_GS + 1],
                            1.0 / 127.0, None, op0=ALU.mult)
            vsp(v).then_inc(s_bv, 1)
        dve_steps.append(dve_final)

        def emit_gpsimd(gp):
            bp = base["bp"]
            cc = base["cc"]
            bv = base["bv"]
            gp.wait_ge(s_bv, bv + 1)
            gp.dma_start(out=cc_in_t[0:128].rearrange("(p c) -> p c", p=128),
                         in_=cstat[:, 0:1]).then_inc(s_bp, 16)
            gp.wait_ge(s_bp, bp + 16)
            gp.collective_compute("AllReduce", ALU.max, replica_groups=RG,
                                  ins=[cc_in_t[0:128].opt()],
                                  outs=[cc_out_t[0:128].opt()]).then_inc(s_cc, 1)
            gp.wait_ge(s_cc, cc + 1)
            gp.dma_start(out=srow[0:1, 0:128],
                         in_=cc_out_t[0:128].rearrange("(q n) -> q n", q=1)).then_inc(s_bp, 16)
            gp.dma_start(out=row_t.ap()[0:1, 0:1], in_=bis[0:1, 0:1]).then_inc(s_bp, 16)
            for j in range(nrnd + 1):
                gp.wait_ge(s_bv, bv + 3 + j)
                if dbg is not None:
                    gp.dma_start(out=dbg[j:j + 1, :],
                                 in_=bis[0:1, 0:32]).then_inc(s_dbg, 16)
                gp.dma_start(out=row_t.ap()[0:1, 0:nthr],
                             in_=bis[0:1, B_NTH:B_NTH + nthr]).then_inc(s_bp, 16)
                gp.wait_ge(s_bp, bp + 16 * (3 + 4 * j + 1))
                gp.dma_start(out=bisb[:, 0:nthr],
                             in_=row_t.ap()[0:1, 0:nthr].partition_broadcast(128)
                             ).then_inc(s_bp, 16)
                gp.wait_ge(s_a, base["a_of_round"](j))
                b0 = j % 2 * 32
                roff = 1024 + 1024 * (j % 2)
                h = nch
                while h > 1:
                    h //= 2
                    for k in range(nthr):
                        gp.tensor_tensor(cstat[:, b0 + k * nch:b0 + k * nch + h],
                                         cstat[:, b0 + k * nch:b0 + k * nch + h],
                                         cstat[:, b0 + k * nch + h:b0 + k * nch + 2 * h],
                                         op=ALU.add)
                gp.dma_start(out=cc_in_t[roff:roff + 128 * nthr].rearrange(
                                 "(p k c) -> p k c", p=128, k=nthr),
                             in_=cstat[:, b0:b0 + nthr * nch].rearrange(
                                 "p (k c) -> p k c", k=nthr)[:, :, 0:1],
                             allow_slow_non_contiguous=True).then_inc(s_bp, 16)
                gp.wait_ge(s_bp, bp + 16 * (3 + 4 * j + 3))
                gp.collective_compute("AllReduce", ALU.add, replica_groups=RG,
                                      ins=[cc_in_t[roff:roff + 128 * nthr].opt()],
                                      outs=[cc_out_t[0:128 * nthr].opt()]).then_inc(s_cc, 1)
                gp.wait_ge(s_cc, cc + 2 + j)
                gp.dma_start(out=srow[0:1, 0:128 * nthr],
                             in_=cc_out_t[0:128 * nthr].rearrange("(q n) -> q n", q=1)
                             ).then_inc(s_bp, 16)
            gp.wait_ge(s_bv, bv + 2 + (nrnd + 1) + 1)
            if dbg is not None:
                gp.dma_start(out=dbg[nrnd + 1:nrnd + 2, :],
                             in_=bis[0:1, 0:32]).then_inc(s_dbg, 16)
                gp.dma_start(out=dbg2_t.ap()[:, 0:ncol],
                             in_=cand[:, :]).then_inc(s_dbg, 16)
            gp.dma_start(out=cc_in_t[3072:3328].rearrange("(p c) -> p c", p=128),
                         in_=cstat[:, 0:2]).then_inc(s_bp, 16)
            gp.dma_start(out=cc_in_t[3328:3584].rearrange("(p c) -> p c", p=128),
                         in_=cstat[:, 2:4]).then_inc(s_bp, 16)
            gp.wait_ge(s_bp, bp + 16 * (3 + 4 * (nrnd + 1) + 2))
            gp.collective_compute("AllReduce", ALU.max, replica_groups=RG,
                                  ins=[cc_in_t[3072:3328].opt()],
                                  outs=[cc_out_t[0:256].opt()]).then_inc(s_cc, 1)
            gp.wait_ge(s_cc, cc + 1 + (nrnd + 1) + 1)
            gp.collective_compute("AllReduce", ALU.min, replica_groups=RG,
                                  ins=[cc_in_t[3328:3584].opt()],
                                  outs=[cc_out_t[256:512].opt()]).then_inc(s_cc, 1)
            gp.wait_ge(s_cc, cc + 1 + (nrnd + 1) + 2)
            gp.dma_start(out=srow[0:1, 0:256],
                         in_=cc_out_t[0:256].rearrange("(q n) -> q n", q=1)).then_inc(s_bp, 16)
            gp.dma_start(out=srow[0:1, 256:512],
                         in_=cc_out_t[256:512].rearrange("(q n) -> q n", q=1)).then_inc(s_bp, 16)
            gp.wait_ge(s_bv, bv + 2 + (nrnd + 1) + 2)
            gp.dma_start(out=row_t.ap()[0:1, ro:ro + 4],
                         in_=bis[0:1, B_AMAX:B_AMAX + 4]).then_inc(s_bp, 16)
            gp.wait_ge(s_bp, bp + 16 * (3 + 4 * (nrnd + 1) + 5))
            gp.dma_start(out=bisb[:, 4:8],
                         in_=row_t.ap()[0:1, ro:ro + 4].partition_broadcast(128)
                         ).then_inc(s_bp, 16)
            gp.wait_ge(s_bp, bp + 16 * (3 + 4 * (nrnd + 1) + 6))
            if dbg is not None:
                gp.wait_ge(s_dbg, 16 * (nrnd + 3))
            gp.tensor_copy(sc1[:, s_slot:s_slot + 1], bisb[:, 5:6])
            gp.tensor_copy(sc1[:, gs_slot:gs_slot + 1], bisb[:, 7:8]).then_inc(s_g, 1)

        n_bp = 16 * (3 + 4 * (nrnd + 1) + 6)
        n_cc = 1 + (nrnd + 1) + 2
        n_bv = 2 + (nrnd + 1) + 2
        n_act = nrnd + 1
        n_g = 1
        return act_steps, dve_steps, emit_gpsimd, dict(bp=n_bp, cc=n_cc, bv=n_bv,
                                                       a=n_act, g=n_g)


    with nc.Block() as blk:
        @blk.sync
        def _(sync):
            for i, bd in enumerate(b_d):
                sync.dma_start(out=b_sb[:, i * DC:(i + 1) * DC],
                               in_=bd.rearrange("(c p) -> p c", p=128),
                               allow_slow_non_contiguous=True).then_inc(s_w, 16)
            sync.dma_start(out=g_bc[:, :], in_=lng_d.partition_broadcast(128)).then_inc(s_w, 16)
            sync.dma_start(out=bgb_bc[:, :], in_=lnb_d.partition_broadcast(128)).then_inc(s_w, 16)
            sync.dma_start(out=b4_bc[:, :], in_=b_d[3].partition_broadcast(128)).then_inc(s_w, 16)
            for i in range(NT):
                if i >= 2:
                    sync.wait_ge(s_a, 2 * (i - 1))
                sync.dma_start(out=xt[i % 2][:, :],
                               in_=x_d[i * 128:(i + 1) * 128, :]).then_inc(s_x, 16)

        @blk.gpsimd
        def _(gp):
            gp.iota(iot[:, :], [[1, 128]], channel_multiplier=0,
                    allow_small_or_imprecise_dtypes=True)
            gp.iota(pidx[:, :], [[0, 1]], channel_multiplier=1,
                    allow_small_or_imprecise_dtypes=True).then_inc(s_g, 1)

        @blk.scalar
        def _(act):
            for i in range(NT):
                act.wait_ge(s_x, 16 * (i + 1))
                act.activation(jk_f[:, :], xt[i % 2][:, :], AF.Identity,
                               accum_out=st_sum[:, i:i + 1])
                act.activation(jk_f[:, :], xt[i % 2][:, :], AF.Square,
                               accum_out=st_sq[:, i:i + 1]).then_inc(s_a, 2)

        @blk.vector
        def _(v):
            v.memset(ones_col[:, :], 1.0)
            v.memset(ones_row[:, :], 1.0)
            v.memset(ones_rowb[:, :], 1.0)
            v.memset(bis[:, :], 0.0)
            for k in range(3):
                v.memset(bis[0:1, B_FRC3 + k:B_FRC3 + k + 1], (k + 1) / 4.0)
            for k in range(7):
                v.memset(bis[0:1, B_FRC7 + k:B_FRC7 + k + 1], (k + 1) / 8.0)
            v.wait_ge(s_g, 1)
            v.tensor_scalar(identf[:, :], iot[:, :], pidx[:, :], None, op0=ALU.is_equal)
            v.tensor_copy(ident[:, :], identf[:, :]).then_inc(s_d, 1)
    C["x"] += NT * 16
    C["w"] += 7 * 16
    C["g"] += 1
    C["a"] += 2 * NT
    C["d"] += 1

    a0, d0 = C["a"], C["d"]
    with nc.Block() as blk:
        @blk.vector
        def _(v):
            v.wait_ge(s_a, a0)
            v.tensor_scalar(stt1[:, :], st_sum[:, :], INV1024, None, op0=ALU.mult)
            vsp(v)
            v.tensor_tensor(stt2[:, :], stt1[:, :], stt1[:, :], op=ALU.mult)
            v.tensor_scalar(rstd[:, :], st_sq[:, :], INV1024, 1e-5, op0=ALU.mult, op1=ALU.add)
            vsp(v)
            v.tensor_tensor(rstd[:, :], rstd[:, :], stt2[:, :], op=ALU.subtract)
            vsp(v).then_inc(s_d, 1)

        @blk.scalar
        def _(act):
            act.wait_ge(s_d, d0 + 1)
            act.activation(negmr[:, :], rstd[:, :], AF.Sqrt).then_inc(s_a, 1)

        @blk.vector
        def _(v):
            v.wait_ge(s_a, a0 + 1)
            v.reciprocal(rs2[:, :], negmr[:, :])
            vsp(v)
            v.tensor_tensor(stt2[:, :], rs2[:, :], rs2[:, :], op=ALU.mult)
            v.tensor_scalar(rstd[:, :], rstd[:, :], -1e-5, None, op0=ALU.add)
            vsp(v)
            v.tensor_tensor(stt2[:, :], stt2[:, :], rstd[:, :], op=ALU.mult)
            vsp(v)
            v.tensor_scalar(stt2[:, :], stt2[:, :], 1e-5, None, op0=ALU.add)
            v.tensor_scalar(stt1[:, :], stt1[:, :], -1.0, None, op0=ALU.mult)
            vsp(v)
            v.tensor_tensor(negmr[:, :], stt1[:, :], rs2[:, :], op=ALU.mult)
            v.tensor_copy(rstd[:, :], rs2[:, :])
            vsp(v).then_inc(s_d, 1)

        @blk.scalar
        def _(act):
            act.wait_ge(s_d, d0 + 2)
            act.activation(stt2[:, :], stt2[:, :], AF.Sqrt).then_inc(s_a, 1)

        @blk.vector
        def _(v):
            v.wait_ge(s_a, a0 + 2)
            v.reciprocal(rs2[:, :], stt2[:, :])
            vsp(v).then_inc(s_d, 1)
    C["a"] += 2
    C["d"] += 3

    x1, a1, d1, p1 = C["x"], C["a"], C["d"], C["p"]
    with nc.Block() as blk:
        @blk.sync
        def _(sync):
            for i in range(NT):
                if i >= 2:
                    sync.wait_ge(s_a, a1 + (i - 1))
                sync.dma_start(out=xt[i % 2][:, :],
                               in_=x_d[i * 128:(i + 1) * 128, :]).then_inc(s_x, 16)

        @blk.scalar
        def _(act):
            for i in range(NT):
                act.wait_ge(s_x, x1 + 16 * (i + 1))
                if i >= 2:
                    act.wait_ge(s_d, d1 + (i - 1))
                    act.wait_ge(s_p, p1 + (i - 1))
                act.activation(xnb[i % 2][:, :], xt[i % 2][:, :], AF.Identity,
                               bias=negmr[:, i:i + 1],
                               scale=rstd[:, i:i + 1]).then_inc(s_a, 1)

        @blk.tensor
        def _(pe):
            for i in range(NT):
                pe.wait_ge(s_d, d1 + i + 1)
                mm = None
                for dc in range(DC):
                    pe.transpose(pT1[:, dc * 128:(dc + 1) * 128],
                                 xnb[i % 2][:, dc * 128:(dc + 1) * 128], ident[:, :])
                for dc in range(DC):
                    mm = pe.transpose(pT2[:, dc * 128:(dc + 1) * 128],
                                      l2b[i % 2][:, dc * 128:(dc + 1) * 128], ident[:, :])
                mm.then_inc(s_p, 1)

        @blk.vector
        def _(v):
            for i in range(NT):
                v.wait_ge(s_a, a1 + (i + 1))
                v.scalar_tensor_tensor(l2b[i % 2][:, :], xnb[i % 2][:, :], rs2[:, i:i + 1],
                                       g_bc[:, :], op0=ALU.mult, op1=ALU.mult)
                v.tensor_tensor(l2b[i % 2][:, :], l2b[i % 2][:, :], bgb_bc[:, :],
                                op=ALU.add).then_inc(s_d, 1)
                v.scalar_tensor_tensor(jk_f[:, :], l2b[i % 2][:, :], -1.0, l2b[i % 2][:, :],
                                       op0=ALU.mult, op1=ALU.max)
                for ch in range(4):
                    v.max(q1cand[:, i * 32 + ch * 8:i * 32 + ch * 8 + 8],
                          jk_f[:, ch * 256:(ch + 1) * 256])
                v.wait_ge(s_p, p1 + i + 1)
                v.tensor_copy(
                    xnT.ap().rearrange("p (dc t) -> p dc t", dc=DC)[:, :, i * 128:(i + 1) * 128],
                    pT1[:, :].rearrange("p (dc t) -> p dc t", dc=DC))
                v.tensor_copy(
                    ln2T.ap().rearrange("p (dc t) -> p dc t", dc=DC)[:, :, i * 128:(i + 1) * 128],
                    pT2[:, :].rearrange("p (dc t) -> p dc t", dc=DC))
    C["x"] += NT * 16
    C["a"] += NT
    C["p"] += NT
    C["d"] += NT

    if stop == 1:
        return finish_early()

    for nm in ("stt2", "stt1", "rs2", "negmr", "rstd", "st_sq", "st_sum",
               "bgb_bc", "g_bc", "l2b1", "l2b0", "xnb1", "xnb0", "l2_f",
               "xn_f1", "xn_f0", "xt1", "xt0", "iot", "identf"):
        free(nm)
    qkvst = [enter(f"qkvst{i}", [128, 512], BF16) for i in range(2)]
    Wq_sb = enter("Wq_sb", [128, DC * D], BF16)
    Wk_sb = enter("Wk_sb", [128, DC * D], BF16)
    with nc.Block() as blk:
        @blk.gpsimd
        def _(gp):
            for wi, Wsb in enumerate((Wq_sb, Wk_sb)):
                for dc in range(DC):
                    gp.dma_start(out=Wsb[:, dc * D:(dc + 1) * D],
                                 in_=W_d[wi][dc * 128:(dc + 1) * 128, :]).then_inc(s_wg, 16)
    C["wg"] += 16 * 16

    d3, a3, p3, sp3, w3qk = C["d"], C["a"], C["p"], C["sp"], C["wg"]

    def qk_spills(u):
        if u < 0:
            return 0
        q = min(u + 1, 64)
        k = max(0, u + 1 - 64)
        return 2 * q + k

    with nc.Block() as blk:
        @blk.tensor
        def _(pe):
            pe.wait_ge(s_wg, w3qk)
            for proj in range(2):
                Wsb = (Wq_sb, Wk_sb)[proj]
                for ts in range(8):
                    for ec in range(DC):
                        u = (proj * 8 + ts) * DC + ec
                        psum = (pA, pB, pO1, pO2)[u % 4]
                        if u >= 4:
                            pe.wait_ge(s_a, a3 + (u - 3))
                        mm = None
                        for dc in range(DC):
                            mm = pe.matmul(psum[:, :],
                                           Wsb[:, dc * D + ec * 128:dc * D + (ec + 1) * 128],
                                           xnT[:, dc * T + ts * 512:dc * T + (ts + 1) * 512],
                                           start=(dc == 0), stop=(dc == DC - 1))
                        mm.then_inc(s_p, 1)

        @blk.scalar
        def _(act):
            for proj in range(2):
                for ts in range(8):
                    for ec in range(DC):
                        u = (proj * 8 + ts) * DC + ec
                        act.wait_ge(s_p, p3 + u + 1)
                        if u >= 2:
                            act.wait_ge(s_sp, sp3 + 16 * qk_spills(u - 2))
                        act.activation(qkvst[u % 2][:, :],
                                       (pA, pB, pO1, pO2)[u % 4][:, :], AF.Identity,
                                       bias=b_sb[:, proj * DC + ec:proj * DC + ec + 1]
                                       ).then_inc(s_a, 1)

        @blk.sync
        def _(sync):
            for proj in range(2):
                for ts in range(8):
                    for ec in range(DC):
                        u = (proj * 8 + ts) * DC + ec
                        sync.wait_ge(s_a, a3 + u + 1)
                        st = qkvst[u % 2]
                        if proj == 0:
                            for h in range(2):
                                sc = ts * 2 + h
                                sync.dma_start(
                                    out=QT_d[sc, ec * 128:(ec + 1) * 128, :],
                                    in_=st[:, h * 256:(h + 1) * 256]).then_inc(s_sp, 16)
                        else:
                            sync.dma_start(
                                out=KT_d[ec * 128:(ec + 1) * 128, ts * 512:(ts + 1) * 512],
                                in_=st[:, :]).then_inc(s_sp, 16)
    C["p"] += 128
    C["a"] += 128
    C["sp"] += 192 * 16

    if stop == 2:
        return finish_early()

    free("Wk_sb", "Wq_sb")
    Wv_sb = enter("Wv_sb", [128, DC * D], BF16)
    b3h = enter("b3h", [128, D], F32)
    d4, a4, p4, sp4, w4c = C["d"], C["a"], C["p"], C["sp"], C["wg"]
    with nc.Block() as blk:
        @blk.gpsimd
        def _(gp):
            gp.dma_start(out=b3h[:, :], in_=b_d[2].partition_broadcast(128)).then_inc(s_wg, 16)
            for dc in range(DC):
                gp.dma_start(out=Wv_sb[:, dc * D:(dc + 1) * D],
                             in_=W_d[2][dc * 128:(dc + 1) * 128, :]).then_inc(s_wg, 16)

        @blk.tensor
        def _(pe):
            pe.wait_ge(s_a, a4)
            pe.wait_ge(s_wg, w4c + 16 * (DC + 1))
            for tt in range(NT):
                for eh in range(2):
                    u = tt * 2 + eh
                    psum = (pA, pB, pO1, pO2)[u % 4]
                    if u >= 4:
                        pe.wait_ge(s_d, d4 + (u - 3))
                    mm = None
                    for dc in range(DC):
                        mm = pe.matmul(psum[:, :],
                                       xnT[:, dc * T + tt * 128:dc * T + (tt + 1) * 128],
                                       Wv_sb[:, dc * D + eh * 512:dc * D + (eh + 1) * 512],
                                       start=(dc == 0), stop=(dc == DC - 1))
                    mm.then_inc(s_p, 1)

        @blk.vector
        def _(v):
            v.wait_ge(s_wg, w4c + 16)
            for tt in range(NT):
                for eh in range(2):
                    u = tt * 2 + eh
                    v.wait_ge(s_p, p4 + u + 1)
                    if u < 2:
                        v.wait_ge(s_sp, sp4)
                    else:
                        v.wait_ge(s_sp, sp4 + 16 * (u - 1))
                    v.tensor_tensor(qkvst[u % 2][:, :], (pA, pB, pO1, pO2)[u % 4][:, :],
                                    b3h[:, eh * 512:(eh + 1) * 512], op=ALU.add).then_inc(s_d, 1)

        @blk.sync
        def _(sync):
            for tt in range(NT):
                for eh in range(2):
                    u = tt * 2 + eh
                    sync.wait_ge(s_d, d4 + u + 1)
                    sync.dma_start(out=V_d[tt * 128:(tt + 1) * 128, eh * 512:(eh + 1) * 512],
                                   in_=qkvst[u % 2][:, :]).then_inc(s_sp, 16)
    C["wg"] += 16 + DC * 16
    C["p"] += 64
    C["d"] += 64
    C["sp"] += 64 * 16

    if stop == 3:
        return finish_early()

    free("b3h", "Wv_sb", "qkvst1", "qkvst0")
    wstA = enter("wstA", [128, 1024], F32)
    wstA2 = enter("wstA2", [128, 1024], F32)
    qk_rl = enter("qk_rl", [128, T], BF16)
    nsq = enter("nsq", [128, T], BF16)
    normQ = enter("normQ", [1, T], BF16)
    normK = enter("normK", [1, T], BF16)
    rl5, a5, p5, d5, w5, bp5, g5 = (C["rl"], C["a"], C["p"], C["d"], C["w"],
                                    C["bp"], C["g"])
    with nc.Block() as blk:
        @blk.sync
        def _(sync):
            sync.wait_ge(s_sp, sp3 + 16 * 192)
            for proj in range(2):
                for ec in range(DC):
                    k = proj * DC + ec
                    if k >= 1:
                        sync.wait_ge(s_a, a5 + k)
                    if proj == 0:
                        for sc in range(SC):
                            sync.dma_start(out=qk_rl[:, sc * 256:(sc + 1) * 256],
                                           in_=QT_d[sc, ec * 128:(ec + 1) * 128, :]
                                           ).then_inc(s_rl, 16)
                    else:
                        sync.dma_start(out=qk_rl[:, :],
                                       in_=KT_d[ec * 128:(ec + 1) * 128, :]).then_inc(s_rl, 16)
            for pc in range(32):
                if pc >= 2:
                    sync.wait_ge(s_a, a5 + 16 + pc - 1)
                sync.dma_start(out=(wstA, wstA2)[pc % 2][:, :],
                               in_=w1T_d[(pc // 4) * 128:(pc // 4 + 1) * 128,
                                         (pc % 4) * 1024:(pc % 4 + 1) * 1024]
                               ).then_inc(s_w, 16)

        @blk.scalar
        def _(act):
            nds = 0
            for proj in range(2):
                for ec in range(DC):
                    k = proj * DC + ec
                    nds += SC if proj == 0 else 1
                    act.wait_ge(s_rl, rl5 + 16 * nds)
                    if k >= 1:
                        act.wait_ge(s_d, d5 + 8 * k)
                    act.activation(nsq[:, :], qk_rl[:, :], AF.Square).then_inc(s_a, 1)
            for pc in range(32):
                act.wait_ge(s_w, w5 + 16 * (pc + 1))
                act.activation(jk_f[:, 0:1024], (wstA, wstA2)[pc % 2][:, :], AF.Abs,
                               accum_out=cstat[:, 32 + pc:33 + pc]).then_inc(s_a, 1)

        @blk.tensor
        def _(pe):
            for k in range(16):
                pe.wait_ge(s_a, a5 + k + 1)
                for ts in range(8):
                    j = k * 8 + ts
                    if j >= 2:
                        pe.wait_ge(s_d, d5 + j - 1)
                    pe.matmul((pR, pO1)[j % 2][0:1, 0:512], ones_col[:, :],
                              nsq[:, ts * 512:(ts + 1) * 512],
                              start=True, stop=True).then_inc(s_p, 1)

        @blk.vector
        def _(v):
            v.memset(normQ[:, :], 0.0)
            v.memset(normK[:, :], 0.0)
            for k in range(16):
                row = normQ if k < 8 else normK
                for ts in range(8):
                    j = k * 8 + ts
                    v.wait_ge(s_p, p5 + j + 1)
                    v.tensor_tensor(row[0:1, ts * 512:(ts + 1) * 512],
                                    row[0:1, ts * 512:(ts + 1) * 512],
                                    (pR, pO1)[j % 2][0:1, 0:512], op=ALU.add).then_inc(s_d, 1)
            v.wait_ge(s_a, a5 + 16 + 32)
            v.reduce_sum(cstat[:, 0:1], cstat[:, 32:64], axis=AX)
            vsp(v).then_inc(s_d, 1)

        @blk.gpsimd
        def _(gp):
            gp.wait_ge(s_d, d5 + 129)
            gp.dma_start(out=cc_in_t[0:128].rearrange("(p c) -> p c", p=128),
                         in_=cstat[:, 0:1]).then_inc(s_bp, 16)
            gp.wait_ge(s_bp, bp5 + 16)
            gp.dma_start(out=srow[0:1, 0:128],
                         in_=cc_in_t[0:128].rearrange("(q n) -> q n", q=1)).then_inc(s_bp, 16)
            gp.wait_ge(s_bp, bp5 + 32)
            gp.engine_nop().then_inc(s_g, 1)

        @blk.vector
        def _(v):
            v.wait_ge(s_g, g5 + 1)
            v.reduce_sum(bis[0:1, B_GAM:B_GAM + 1], srow[0:1, 0:128], axis=AX)
            vsp(v)
            v.tensor_scalar(bis[0:1, B_GAM:B_GAM + 1], bis[0:1, B_GAM:B_GAM + 1],
                            NW, 1e-5, op0=ALU.mult, op1=ALU.add)
            vsp(v)
            v.reciprocal(bis[0:1, B_T0:B_T0 + 1], bis[0:1, B_GAM:B_GAM + 1])
            vsp(v).then_inc(s_d, 1)

        @blk.gpsimd
        def _(gp):
            gp.wait_ge(s_d, d5 + 130)
            gp.dma_start(out=row_t.ap()[0:1, 4:5],
                         in_=bis[0:1, B_T0:B_T0 + 1]).then_inc(s_bp, 16)
            gp.wait_ge(s_bp, bp5 + 48)
            gp.dma_start(out=sc1[:, 4:5],
                         in_=row_t.ap()[0:1, 4:5].partition_broadcast(128)).then_inc(s_bp, 16)
            gp.wait_ge(s_bp, bp5 + 64)
            gp.engine_nop().then_inc(s_g, 1)
    C["rl"] += (8 * SC + 8) * 16
    C["w"] += 32 * 16
    C["a"] += 16 + 32
    C["p"] += 128
    C["d"] += 128 + 1 + 1
    C["bp"] += 64
    C["g"] += 2

    a6i, d6i = C["a"], C["d"]
    with nc.Block() as blk:
        @blk.scalar
        def _(act):
            act.wait_ge(s_d, d6i - 2)
            act.activation(normQ[0:1, :], normQ[0:1, :], AF.Sqrt).then_inc(s_a, 1)

        @blk.vector
        def _(v):
            v.wait_ge(s_a, a6i + 1)
            v.reduce_max(bis[0:1, B_T1:B_T1 + 1], normK[0:1, :], axis=AX).then_inc(s_d, 1)

        @blk.scalar
        def _(act):
            act.wait_ge(s_d, d6i + 1)
            act.activation(bis[0:1, B_T1:B_T1 + 1], bis[0:1, B_T1:B_T1 + 1],
                           AF.Sqrt).then_inc(s_a, 1)

        @blk.vector
        def _(v):
            v.wait_ge(s_a, a6i + 2)
            v.tensor_scalar(m32[0:1, :], normQ[0:1, :], bis[0:1, B_T1:B_T1 + 1],
                            -1.0, op0=ALU.mult, op1=ALU.mult).then_inc(s_d, 1)
    C["a"] += 2
    C["d"] += 2
    free("normK", "normQ")

    if stop == 4:
        return finish_early()

    base1 = dict(a=C["a"], d=C["d"], bp=C["bp"], cc=C["cc"], bv=C["bv"])
    base1["a_of_round"] = lambda j, _a=C["a"]: _a + j + 1
    acts1, dves1, gp1, n1 = make_bisect("q1", q1cand, 1024, N1_CAND, R1, FRAC1,
                                        s_slot=0, gs_slot=1, base=base1, ro=8,
                                        dbg=dbg_t.ap() if DBG_Q1 else None,
                                        nthr=7, nrnd=NR1, frc=B_FRC7)
    with nc.Block() as blk:
        @blk.vector
        def _(v):
            for f in dves1:
                f(v)

        @blk.scalar
        def _(act):
            for f in acts1:
                f(act)

        @blk.gpsimd
        def _(gp):
            gp1(gp)
    C["a"] += n1["a"]
    C["bv"] += n1["bv"]
    C["bp"] += n1["bp"]
    C["cc"] += n1["cc"]
    C["g"] += n1["g"]

    if stop == 5:
        return finish_early()

    w7g, a7g, d7g, bp7g, g7g = C["w"], C["a"], C["d"], C["bp"], C["g"]
    with nc.Block() as blk:
        @blk.sync
        def _(sync):
            for ch in range(32):
                if ch >= 2:
                    sync.wait_ge(s_a, a7g + ch - 1)
                sync.dma_start(out=(wstA, wstA2)[ch % 2][:, :],
                               in_=w2T_d[ch * 128:(ch + 1) * 128, :]).then_inc(s_w, 16)

        @blk.scalar
        def _(act):
            for ch in range(32):
                act.wait_ge(s_w, w7g + 16 * (ch + 1))
                act.activation(jk_f[:, 0:1024], (wstA, wstA2)[ch % 2][:, :], AF.Abs,
                               accum_out=cstat[:, 32 + ch:33 + ch]).then_inc(s_a, 1)

        @blk.vector
        def _(v):
            v.wait_ge(s_a, a7g + 32)
            v.reduce_sum(cstat[:, 0:1], cstat[:, 32:64], axis=AX)
            vsp(v).then_inc(s_d, 1)

        @blk.gpsimd
        def _(gp):
            gp.wait_ge(s_d, d7g + 1)
            gp.dma_start(out=cc_in_t[0:128].rearrange("(p c) -> p c", p=128),
                         in_=cstat[:, 0:1]).then_inc(s_bp, 16)
            gp.wait_ge(s_bp, bp7g + 16)
            gp.dma_start(out=srow[0:1, 256:384],
                         in_=cc_in_t[0:128].rearrange("(q n) -> q n", q=1)).then_inc(s_bp, 16)
            gp.wait_ge(s_bp, bp7g + 32)
            gp.engine_nop().then_inc(s_g, 1)

        @blk.vector
        def _(v):
            v.wait_ge(s_g, g7g + 1)
            v.reduce_sum(bis[0:1, B_GAM:B_GAM + 1], srow[0:1, 256:384], axis=AX)
            vsp(v)
            v.tensor_scalar(bis[0:1, B_GAM:B_GAM + 1], bis[0:1, B_GAM:B_GAM + 1],
                            NW, 1e-5, op0=ALU.mult, op1=ALU.add)
            vsp(v)
            v.reciprocal(bis[0:1, B_T1:B_T1 + 1], bis[0:1, B_GAM:B_GAM + 1])
            vsp(v).then_inc(s_d, 1)

        @blk.gpsimd
        def _(gp):
            gp.wait_ge(s_d, d7g + 2)
            gp.dma_start(out=row_t.ap()[0:1, 5:6],
                         in_=bis[0:1, B_T1:B_T1 + 1]).then_inc(s_bp, 16)
            gp.wait_ge(s_bp, bp7g + 48)
            gp.dma_start(out=sc1[:, 5:6],
                         in_=row_t.ap()[0:1, 5:6].partition_broadcast(128)).then_inc(s_bp, 16)
            gp.wait_ge(s_bp, bp7g + 64)
            gp.engine_nop().then_inc(s_g, 1)
    C["w"] += 32 * 16
    C["a"] += 32
    C["d"] += 2
    C["bp"] += 64
    C["g"] += 2

    d7, g7b, w7 = C["d"], C["g"], C["w"]
    wst7 = [wstA, wstA2]
    with nc.Block() as blk:
        @blk.vector
        def _(v):
            v.wait_ge(s_g, g7b)
            for ts in range(8):
                for dc in range(DC):
                    k = dc * 8 + ts
                    sl_in = ln2T[:, k * 512:(k + 1) * 512]
                    sl_out = xqT[:, k * 512:(k + 1) * 512]
                    v.tensor_scalar(jk_f[:, 0:512], sl_in, sc1[:, 0:1], None, op0=ALU.mult)
                    v.tensor_scalar(jk_f[:, 0:512], jk_f[:, 0:512], MAGIC, MAGIC,
                                    op0=ALU.add, op1=ALU.subtract)
                    mm = v.tensor_scalar(sl_out, jk_f[:, 0:512], 127.0, -128.0,
                                         op0=ALU.min, op1=ALU.max)
                mm.then_inc(s_d, 1)

        @blk.sync
        def _(sync):
            sync.wait_ge(s_g, g7b)
            for pq in range(32):
                q, dc = pq // 8, pq % 8
                if pq >= 2:
                    sync.wait_ge(s_d, d7 + 8 + pq - 1)
                sync.dma_start(out=wst7[pq % 2][:, :],
                               in_=w1T_d[dc * 128:(dc + 1) * 128,
                                         q * 1024:(q + 1) * 1024]
                               ).then_inc(s_w, 16)

        @blk.vector
        def _(v):
            for pq in range(32):
                q, dc = pq // 8, pq % 8
                v.wait_ge(s_w, w7 + 16 * (pq + 1))
                v.tensor_scalar(jk_f[:, :], wst7[pq % 2][:, :], sc1[:, 4:5], None,
                                op0=ALU.mult)
                v.tensor_scalar(jk_f[:, :], jk_f[:, :], MAGIC, MAGIC,
                                op0=ALU.add, op1=ALU.subtract)
                v.tensor_scalar(w1q[:, dc * HID + q * 1024:dc * HID + (q + 1) * 1024],
                                jk_f[:, :], 1.0, -1.0, op0=ALU.min,
                                op1=ALU.max).then_inc(s_d, 1)
    C["d"] += 8 + 32
    C["w"] += 32 * 16
    free("nsq", "qk_rl", "wstA2", "wstA", "q1cand")
    q2cand = enter("q2cand", [128, 4096], F32)

    if stop == 6:
        return finish_early()

    gstage = enter("gstage", [128, HID], BF16)
    gst = [enter(f"gst{i}", [128, 512], F32) for i in range(2)]
    ab_f = [enter(f"ab_f{i}", [128, 512], F32) for i in range(2)]
    p8, a8, d8, gd8 = C["p"], C["a"], C["d"], C["sp"]
    with nc.Block() as blk:
        @blk.tensor
        def _(pe):
            for hc in range(HC):
                for ts in range(8):
                    u = hc * 8 + ts
                    if ts == 0:
                        pe.wait_ge(s_d, d8 - 32 + (hc // 8 + 1) * 8)
                    psum = (pA, pB, pO1, pO2)[u % 4]
                    if u >= 4:
                        pe.wait_ge(s_a, a8 + 2 * (u - 3))
                    mm = None
                    for dc in range(DC):
                        mm = pe.matmul(psum[:, :],
                                       w1q[:, dc * T + hc * 128:dc * T + (hc + 1) * 128],
                                       xqT[:, dc * T + ts * 512:dc * T + (ts + 1) * 512],
                                       start=(dc == 0), stop=(dc == DC - 1))
                    mm.then_inc(s_p, 1)

        @blk.scalar
        def _(act):
            for hc in range(HC):
                for ts in range(8):
                    u = hc * 8 + ts
                    act.wait_ge(s_p, p8 + u + 1)
                    if u >= 2:
                        act.wait_ge(s_d, d8 + (u - 1))
                    act.activation(gst[u % 2][:, :],
                                   (pA, pB, pO1, pO2)[u % 4][:, :], AF.Gelu,
                                   scale=sc1[:, 1:2])
                    act.activation(ab_f[u % 2][:, :], gst[u % 2][:, :],
                                   AF.Abs).then_inc(s_a, 2)

        @blk.vector
        def _(v):
            for hc in range(HC):
                if hc >= 1:
                    v.wait_ge(s_sp, gd8 + 16 * hc)
                for ts in range(8):
                    u = hc * 8 + ts
                    v.wait_ge(s_a, a8 + 2 * (u + 1))
                    cb = u * 16
                    v.max(q2cand[:, cb:cb + 8], ab_f[u % 2][:, 0:256])
                    v.max(q2cand[:, cb + 8:cb + 16], ab_f[u % 2][:, 256:512])
                    v.tensor_copy(gstage[:, ts * 512:(ts + 1) * 512],
                                  gst[u % 2][:, :]).then_inc(s_d, 1)

        @blk.sync
        def _(sync):
            for hc in range(HC):
                sync.wait_ge(s_d, d8 + (hc + 1) * 8)
                sync.dma_start(
                    out=gT_d[:, hc * 128:(hc + 1) * 128, :].rearrange("tt h t -> h tt t"),
                    in_=gstage[:, :].rearrange("p (tt t) -> p tt t", tt=NT)
                ).then_inc(s_sp, 16)
    C["p"] += 256
    C["a"] += 512
    C["d"] += 256
    C["sp"] += HC * 16
    free("ab_f1", "ab_f0", "gst1", "gst0", "gstage")

    if stop == 7:
        return finish_early()

    W4_sb = enter("W4_sb", [128, DC * D], BF16)
    w9, rl9, d9, sp9 = C["w"], C["rl"], C["d"], C["sp"]
    p9pe = C["p"]
    qt_sb = enter("qt_sb", [128, DC * 256], BF16)
    A_sb = enter("A_sb", [128, NT * 256], BF16)
    OT_sb = enter("OT_sb", [128, DC * 256], BF16)
    cb_sb = jk_f[:, 768:1024]
    crow = mk_f[0:1, 0:256]
    apst = enter("apst", [128, 512], BF16)
    with nc.Block() as blk:
        @blk.sync
        def _(sync):
            sync.wait_ge(s_sp, sp3 + 16 * 192 + 64 * 16)
            sync.wait_ge(s_p, p9pe)
            for ec in range(DC):
                sync.dma_start(out=KTs[:, ec * T:(ec + 1) * T],
                               in_=KT_d[ec * 128:(ec + 1) * 128, :]).then_inc(s_rl, 16)

        @blk.gpsimd
        def _(gp):
            gp.wait_ge(s_sp, sp9)
            for dc in range(DC):
                gp.dma_start(out=W4_sb[:, dc * D:(dc + 1) * D],
                             in_=W_d[3][dc * 128:(dc + 1) * 128, :]).then_inc(s_wg, 16)
    C["rl"] += DC * 16
    C["wg"] += DC * 16

    if stop == 8:
        return finish_early()

    PE_PC = 32 + 32 + 1 + DC + 4
    w10v = C["w"]
    ACT_PC = 32
    DVE_PC = 1 + 1 + DC + 4
    p10, a10, d10, rl10, sp10 = C["p"], C["a"], C["p"] * 0 + C["d"], C["rl"], C["sp"]

    def act_rounds_before(c):
        return max(0, min(c - 1, NR + 1))

    def _a_of_round(j):
        c = j + 1
        return a10 + (c + 1) * ACT_PC + act_rounds_before(c) + 1

    base2 = dict(a=a10, d=d10, bp=C["bp"], cc=C["cc"], bv=C["bv"],
                 a_of_round=_a_of_round)
    acts2, dves2, gp2, n2 = make_bisect("q2", q2cand, 4096, N2_CAND, R2, FRAC2,
                                        s_slot=2, gs_slot=3, base=base2, ro=12,
                                        nthr=3, nrnd=NR, frc=B_FRC3)
    with nc.Block() as blk:
        @blk.sync
        def _(sync):
            sync.wait_ge(s_rl, rl9 + DC * 16)
            for sc in range(SC):
                if sc >= 1:
                    sync.wait_ge(s_p, p10 + (sc - 1) * PE_PC + 32)
                sync.dma_start(out=qt_sb[:, :].rearrange("p (ec t) -> p ec t", ec=DC),
                               in_=QT_d[sc, :, :].rearrange("(ec p) t -> p ec t", p=128)
                               ).then_inc(s_rl, 16)
                if sc == 0:
                    for tt in range(NT):
                        sync.dma_start(out=Vs[:, tt * D:(tt + 1) * D],
                                       in_=V_d[tt * 128:(tt + 1) * 128, :]
                                       ).then_inc(s_w, 16)
                for g in range(4):
                    sync.wait_ge(s_d, d10 + sc * DVE_PC + 2 + DC + g + 1)
                    st, dh = g // 2, g % 2
                    sync.dma_start(
                        out=AP_d[sc * 256 + st * 128: sc * 256 + (st + 1) * 128,
                                 dh * 512:(dh + 1) * 512],
                        in_=apst[:, :]).then_inc(s_sp, 16)

        @blk.tensor
        def _(pe):
            pe.wait_ge(s_wg, C["wg"])
            for sc in range(SC):
                pe.wait_ge(s_rl, rl10 + 16 * (sc + 1))
                for tt in range(NT):
                    psum = (pA, pB)[tt % 2]
                    if tt < 2 and sc >= 1:
                        pe.wait_ge(s_d, d10 + sc * DVE_PC)
                    if tt >= 2:
                        pe.wait_ge(s_a, a10 + sc * ACT_PC + act_rounds_before(sc) + (tt - 1))
                    mm = None
                    for ec in range(DC):
                        mm = pe.matmul(psum[:, 0:256],
                                       KTs[:, ec * T + tt * 128:ec * T + (tt + 1) * 128],
                                       qt_sb[:, ec * 256:(ec + 1) * 256],
                                       start=(ec == 0), stop=False)
                    mm = pe.matmul(psum[:, 0:256], ones_rowb[:, :],
                                   m32[0:1, sc * 256:(sc + 1) * 256],
                                   start=False, stop=True)
                    mm.then_inc(s_p, 1)
                pe.wait_ge(s_a, a10 + (sc + 1) * ACT_PC + act_rounds_before(sc))
                for tt in range(NT):
                    pe.matmul(pR[0:1, 0:256], ones_col[:, :],
                              A_sb[:, tt * 256:(tt + 1) * 256],
                              start=(tt == 0), stop=(tt == NT - 1)).then_inc(s_p, 1)
                pe.wait_ge(s_d, d10 + sc * DVE_PC + 1)
                pe.matmul(pR[:, 256:512], ones_row[:, :], crow,
                          start=True, stop=True).then_inc(s_p, 1)
                for ec in range(DC):
                    psum = (pO1, pO2)[ec % 2]
                    if sc == 0 and ec == 0:
                        pe.wait_ge(s_w, w10v + NT * 16)
                    if ec >= 2:
                        pe.wait_ge(s_d, d10 + sc * DVE_PC + 2 + (ec - 1))
                    mm = None
                    for tt in range(NT):
                        mm = pe.matmul(psum[:, 0:256],
                                       Vs[:, tt * D + ec * 128:tt * D + (ec + 1) * 128],
                                       A_sb[:, tt * 256:(tt + 1) * 256],
                                       start=(tt == 0), stop=(tt == NT - 1))
                    mm.then_inc(s_p, 1)
                for g in range(4):
                    st, dh = g // 2, g % 2
                    psum = (pA, pB)[g % 2]
                    if g == 0:
                        pe.wait_ge(s_d, d10 + sc * DVE_PC + 2 + DC)
                    if g >= 2:
                        pe.wait_ge(s_d, d10 + sc * DVE_PC + 2 + DC + (g - 1))
                    mm = None
                    for ec in range(DC):
                        mm = pe.matmul(psum[:, :],
                                       OT_sb[:, ec * 256 + st * 128:ec * 256 + (st + 1) * 128],
                                       W4_sb[:, ec * D + dh * 512:ec * D + (dh + 1) * 512],
                                       start=(ec == 0), stop=(ec == DC - 1))
                    mm.then_inc(s_p, 1)

        @blk.scalar
        def _(act):
            for sc in range(SC):
                for tt in range(NT):
                    act.wait_ge(s_p, p10 + sc * PE_PC + tt + 1)
                    act.activation(A_sb[:, tt * 256:(tt + 1) * 256],
                                   (pA, pB)[tt % 2][:, 0:256], AF.Exp,
                                   scale=0.03125).then_inc(s_a, 1)
                if 1 <= sc <= NR + 1:
                    acts2[sc - 1](act)

        @blk.vector
        def _(v):
            dves2[0](v)
            dves2[1](v)
            di = 2
            for sc in range(SC):
                v.wait_ge(s_p, p10 + sc * PE_PC + 32 + 32)
                v.reciprocal(crow, pR[0:1, 0:256]).then_inc(s_d, 1)
                v.wait_ge(s_p, p10 + sc * PE_PC + 32 + 32 + 1)
                v.tensor_copy(cb_sb, pR[:, 256:512]).then_inc(s_d, 1)
                for ec in range(DC):
                    v.wait_ge(s_p, p10 + sc * PE_PC + 65 + ec + 1)
                    v.tensor_tensor(OT_sb[:, ec * 256:(ec + 1) * 256],
                                    (pO1, pO2)[ec % 2][:, 0:256], cb_sb,
                                    op=ALU.mult).then_inc(s_d, 1)
                for g in range(4):
                    v.wait_ge(s_p, p10 + sc * PE_PC + 65 + DC + g + 1)
                    if sc * 4 + g >= 1:
                        v.wait_ge(s_sp, sp10 + 16 * (sc * 4 + g))
                    st, dh = g // 2, g % 2
                    v.tensor_tensor(apst[:, :], (pA, pB)[g % 2][:, :],
                                    b4_bc[:, dh * 512:(dh + 1) * 512],
                                    op=ALU.add).then_inc(s_d, 1)
                if di < len(dves2) and sc <= NR:
                    dves2[di](v)
                    di += 1
            while di < len(dves2):
                dves2[di](v)
                di += 1

        @blk.gpsimd
        def _(gp):
            gp2(gp)
    C["p"] += SC * PE_PC
    C["w"] += NT * 16
    C["a"] += SC * ACT_PC + n2["a"]
    C["d"] += SC * DVE_PC
    C["bv"] += n2["bv"]
    C["rl"] += SC * 16
    C["sp"] += SC * 4 * 16
    C["bp"] += n2["bp"]
    C["cc"] += n2["cc"]
    C["g"] += n2["g"]
    free("apst", "OT_sb", "A_sb", "qt_sb", "W4_sb", "q2cand")

    if stop == 9:
        return finish_early()

    wstB = [enter(f"wstB{i}", [128, 1024], F32) for i in range(2)]
    w11, d11 = C["w"], C["d"]
    with nc.Block() as blk:
        @blk.sync
        def _(sync):
            for ch in range(HC):
                if ch >= 2:
                    sync.wait_ge(s_d, d11 + ch - 1)
                sync.dma_start(out=wstB[ch % 2][:, :],
                               in_=w2T_d[ch * 128:(ch + 1) * 128, :]).then_inc(s_w, 16)

        @blk.vector
        def _(v):
            for ch in range(HC):
                v.wait_ge(s_w, w11 + 16 * (ch + 1))
                v.tensor_scalar(jk_f[:, :], wstB[ch % 2][:, :], sc1[:, 5:6], None,
                                op0=ALU.mult)
                v.tensor_scalar(jk_f[:, :], jk_f[:, :], MAGIC, MAGIC,
                                op0=ALU.add, op1=ALU.subtract)
                v.tensor_scalar(w2q[:, ch * D:(ch + 1) * D], jk_f[:, :],
                                1.0, -1.0, op0=ALU.min, op1=ALU.max).then_inc(s_d, 1)
    C["w"] += HC * 16
    C["d"] += HC

    if stop == 10:
        return finish_early()

    free("wstB1", "wstB0")
    gload = enter("gload", [128, HID], BF16)
    gq = [enter(f"gq{i}", [128, HID], BF16) for i in range(2)]
    gs1_f = enter("gs1_f", [128, HID // 2], F32)
    xt2 = [enter(f"xt2{i}", [128, D], F32) for i in range(2)]
    apt = [enter(f"apt{i}", [128, D], BF16) for i in range(2)]
    outst = enter("outst", [128, D], F32)
    rl12, d12, p12, st12, sp12b = C["rl"], C["d"], C["p"], C["st"], C["sp"]
    wg12 = C["wg"]
    with nc.Block() as blk:
        @blk.sync
        def _(sync):
            sync.wait_ge(s_sp, sp12b)
            sync.wait_ge(s_sp, C["sp"])
            sync.wait_ge(s_d, d12)
            for tt in range(NT):
                if tt >= 1:
                    sync.wait_ge(s_d, d12 + (tt - 1) * 4 + 1)
                sync.dma_start(out=gload[:, :].rearrange("p (hc t) -> p hc t", hc=HC),
                               in_=gT_d[tt, :, :].rearrange("(hc p) t -> p hc t", p=128)
                               ).then_inc(s_rl, 16)
                if tt >= 2:
                    sync.wait_ge(s_d, d12 + (tt - 2) * 4 + 4)
                sync.dma_start(out=xt2[tt % 2][:, :],
                               in_=x_d[tt * 128:(tt + 1) * 128, :]).then_inc(s_rl, 16)
                sync.dma_start(out=apt[tt % 2][:, :],
                               in_=AP_d[tt * 128:(tt + 1) * 128, :]).then_inc(s_rl, 16)


        @blk.vector
        def _(v):
            for tt in range(NT):
                v.wait_ge(s_rl, rl12 + 16 * (3 * tt + 1))
                if tt >= 2:
                    v.wait_ge(s_p, p12 + 2 * (tt - 1))
                for hh in range(2):
                    sl = slice(hh * (HID // 2), (hh + 1) * (HID // 2))
                    v.tensor_scalar(gs1_f[:, :], gload[:, sl], sc1[:, 2:3], None,
                                    op0=ALU.mult)
                    v.tensor_scalar(gs1_f[:, :], gs1_f[:, :], MAGIC, MAGIC,
                                    op0=ALU.add, op1=ALU.subtract)
                    mm = v.tensor_scalar(gq[tt % 2][:, sl], gs1_f[:, :], 127.0, -128.0,
                                         op0=ALU.min, op1=ALU.max)
                mm.then_inc(s_d, 1)
                if tt >= 1:
                    v.wait_ge(s_st, st12 + 16 * tt)
                for dh in range(2):
                    v.wait_ge(s_p, p12 + tt * 2 + dh + 1)
                    psum = (pA, pB, pO1, pO2)[(tt * 2 + dh) % 4]
                    v.tensor_scalar(outst[:, dh * 512:(dh + 1) * 512],
                                    psum[:, :], sc1[:, 3:4], None, op0=ALU.mult).then_inc(s_d, 1)
                v.wait_ge(s_rl, rl12 + 16 * (3 * tt + 3))
                v.tensor_tensor(outst[:, :], outst[:, :],
                                xt2[tt % 2][:, :], op=ALU.add)
                v.tensor_tensor(outst[:, :], outst[:, :],
                                apt[tt % 2][:, :], op=ALU.add).then_inc(s_d, 1)

        @blk.tensor
        def _(pe):
            pe.wait_ge(s_d, d12)
            for tt in range(NT):
                pe.wait_ge(s_d, d12 + tt * 4 + 1)
                for dh in range(2):
                    psum = (pA, pB, pO1, pO2)[(tt * 2 + dh) % 4]
                    if tt >= 2:
                        pe.wait_ge(s_d, d12 + (tt - 2) * 4 + 2 + dh)
                    mm = None
                    for hc in range(HC):
                        mm = pe.matmul(psum[:, :],
                                       gq[tt % 2][:, hc * 128:(hc + 1) * 128],
                                       w2q[:, hc * D + dh * 512:hc * D + (dh + 1) * 512],
                                       start=(hc == 0), stop=(hc == HC - 1))
                    mm.then_inc(s_p, 1)

        @blk.scalar
        def _(act):
            for tt in range(NT):
                act.wait_ge(s_d, d12 + tt * 4 + 4)
                act.dma_start(out=out_d[tt * 128:(tt + 1) * 128, :],
                              in_=outst[:, :]).then_inc(s_st, 16)
            act.wait_ge(s_st, st12 + NT * 16)
    C["rl"] += NT * 3 * 16
    C["d"] += NT * 4
    C["p"] += NT * 2
    C["st"] += NT * 16

    es.close()
    return nc


_NC_CACHE = {}


def kernel(**inputs):
    x = np.ascontiguousarray(np.asarray(inputs["L0_attn_x"], dtype=np.float32))
    w1T = np.ascontiguousarray(np.asarray(inputs["L0_ffn_w1"], dtype=np.float32).T)
    w2T = np.ascontiguousarray(np.asarray(inputs["L0_ffn_w2"], dtype=np.float32).T)
    common = {
        "w1T": w1T,
        "w2T": w2T,
        "L0_ffn_ln_g": np.ascontiguousarray(np.asarray(inputs["L0_ffn_ln_g"], np.float32)),
        "L0_ffn_ln_b": np.ascontiguousarray(np.asarray(inputs["L0_ffn_ln_b"], np.float32)),
    }
    for i in (1, 2, 3, 4):
        common[f"L0_attn_W{i}"] = np.ascontiguousarray(
            np.asarray(inputs[f"L0_attn_W{i}"], np.float32))
        common[f"L0_attn_b{i}"] = np.ascontiguousarray(
            np.asarray(inputs[f"L0_attn_b{i}"], np.float32))

    if "nc" not in _NC_CACHE:
        _NC_CACHE["nc"] = build_program()
    nc = _NC_CACHE["nc"]

    in_maps = []
    for c in range(8):
        m = dict(common)
        m["L0_attn_x"] = np.ascontiguousarray(x[c])
        in_maps.append(m)
    res = run_bass_kernel_spmd(nc, in_maps, core_ids=list(range(8)))
    return np.stack([r["out"] for r in res.results], axis=0)



# version 15 (final)
# speedup vs baseline: 1.0128x; 1.0128x over previous
import math
import numpy as np

import concourse.bass as bass
import concourse.mybir as mybir
from concourse.bass_utils import run_bass_kernel_spmd

F32 = mybir.dt.float32
BF16 = mybir.dt.bfloat16
AF = mybir.ActivationFunctionType
ALU = mybir.AluOpType
AX = mybir.AxisListType.X
AXY = mybir.AxisListType.XY

T, D, HID = 4096, 1024, 4096
NT, DC, SC, HC = T // 128, D // 128, T // 256, HID // 128
MAGIC = 1.5 * 2.0 ** 23
INV1024 = 1.0 / 1024.0
NW = 1.0 / (HID * D)

N1_FULL, N2_FULL = 8 * T * D, 8 * T * HID
_p1 = 0.995 * (N1_FULL - 1)
I1 = math.floor(_p1)
FRAC1, R1 = _p1 - I1, N1_FULL - I1
_p2 = 0.995 * (N2_FULL - 1)
I2 = math.floor(_p2)
FRAC2, R2 = _p2 - I2, N2_FULL - I2
N1_CAND, N2_CAND = 8 * 128 * 1024, 8 * 128 * 4096
NR = 11
NR1 = 8
DBG_Q1 = False
RG = [list(range(8))]

B_LO, B_HI, B_W, B_NB, B_IDX = 0, 1, 2, 3, 4
B_T0, B_T1, B_CHI, B_AMAX, B_S, B_GAM, B_GS = 5, 6, 7, 8, 9, 10, 11
B_CNT, B_PRED, B_THR, B_NTH = 16, 24, 32, 40
B_FRC3, B_FRC7 = 48, 52


def build_program(stop=None):
    nc = bass.Bass("TRN2", target_bir_lowering=False)

    x_d = nc.dram_tensor("L0_attn_x", [T, D], F32, kind="ExternalInput").ap()
    W_d = [nc.dram_tensor(f"L0_attn_W{i}", [D, D], F32, kind="ExternalInput").ap()
           for i in (1, 2, 3, 4)]
    b_d = [nc.dram_tensor(f"L0_attn_b{i}", [D], F32, kind="ExternalInput").ap()
           for i in (1, 2, 3, 4)]
    lng_d = nc.dram_tensor("L0_ffn_ln_g", [D], F32, kind="ExternalInput").ap()
    lnb_d = nc.dram_tensor("L0_ffn_ln_b", [D], F32, kind="ExternalInput").ap()
    w1T_d = nc.dram_tensor("w1T", [D, HID], F32, kind="ExternalInput").ap()
    w2T_d = nc.dram_tensor("w2T", [HID, D], F32, kind="ExternalInput").ap()
    out_d = nc.dram_tensor("out", [T, D], F32, kind="ExternalOutput").ap()

    QT_d = nc.dram_tensor("QT_d", [SC, D, 256], BF16).ap()
    KT_d = nc.dram_tensor("KT_d", [D, T], BF16).ap()
    V_d = nc.dram_tensor("V_d", [T, D], BF16).ap()
    gT_d = nc.dram_tensor("gT_d", [NT, HID, 128], BF16).ap()
    AP_d = nc.dram_tensor("AP_d", [T, D], BF16).ap()
    cc_in_t = nc.dram_tensor("cc_in_d", [8192], F32)
    cc_out_t = nc.dram_tensor("cc_out_d", [8192], F32)
    row_t = nc.dram_tensor("row_d", [1, 64], F32, kind="ExternalOutput")
    dbg_t = nc.dram_tensor("dbg_d", [16, 32], F32, kind="ExternalOutput")
    dbg2_t = nc.dram_tensor("dbg2_d", [128, 1024], F32, kind="ExternalOutput")

    live = {}

    def enter(name, shape, dtype, kind="sbuf", side="left"):
        if kind == "sbuf":
            cm = nc.sbuf_tensor(name, shape, dtype, side=side)
        else:
            cm = nc.psum_tensor(name, shape, dtype)
        h = cm.__enter__()
        live[name] = cm
        return h

    def free(*names):
        for n in names:
            live.pop(n).__exit__(None, None, None)

    from contextlib import ExitStack
    es = ExitStack()

    def sem(name):
        return es.enter_context(nc.semaphore(name))

    ident = enter("ident", [128, 128], BF16, side="right")
    pidx = enter("pidx", [128, 1], F32, side="right")
    ones_col = enter("ones_col", [128, 1], BF16, side="right")
    ones_row = enter("ones_row", [1, 128], F32, side="right")
    ones_rowb = enter("ones_rowb", [1, 128], BF16, side="right")
    b_sb = enter("b_sb", [128, 4 * DC], F32, side="right")
    b4_bc = enter("b4_bc", [128, D], F32, side="right")

    bis = enter("bis", [1, 64], F32, side="right")
    bisb = enter("bisb", [128, 8], F32, side="right")
    srow = enter("srow", [1, 1024], F32, side="right")
    cstat = enter("cstat", [128, 64], F32, side="right")
    sc1 = enter("sc1", [128, 8], F32, side="right")
    jk_f = enter("jk_f", [128, D], F32, side="right")
    mk_f = enter("mk_f", [128, 256], F32, side="right")

    pT1 = enter("pT1", [128, 1024], BF16, "psum")
    pT2 = enter("pT2", [128, 1024], BF16, "psum")
    pA = enter("pA", [128, 512], F32, "psum")
    pB = enter("pB", [128, 512], F32, "psum")
    pR = enter("pR", [128, 512], F32, "psum")
    pO1 = enter("pO1", [128, 512], F32, "psum")
    pO2 = enter("pO2", [128, 512], F32, "psum")
    pJ = enter("pJ", [128, 512], F32, "psum")

    m32 = enter("m32", [1, T], BF16, side="right")
    big1 = enter("big1", [128, DC * T], BF16, side="right")
    big2 = enter("big2", [128, DC * T], BF16, side="right")
    xnT = xqT = KTs = w2q = big1
    ln2T = w1q = Vs = big2

    q1cand = enter("q1cand", [128, 1024], F32)
    identf = enter("identf", [128, 128], F32)
    iot = enter("iot", [128, 128], F32)
    xt = [enter(f"xt{i}", [128, D], F32) for i in range(2)]
    xn_f = [enter(f"xn_f{i}", [128, D], F32) for i in range(2)]
    l2_f = enter("l2_f", [128, D], F32)
    xnb = [enter(f"xnb{i}", [128, D], BF16) for i in range(2)]
    l2b = [enter(f"l2b{i}", [128, D], BF16) for i in range(2)]
    g_bc = enter("g_bc", [128, D], F32)
    bgb_bc = enter("bgb_bc", [128, D], F32)
    st_sum = enter("st_sum", [128, NT], F32)
    st_sq = enter("st_sq", [128, NT], F32)
    rstd = enter("rstd", [128, NT], F32)
    negmr = enter("negmr", [128, NT], F32)
    rs2 = enter("rs2", [128, NT], F32)
    stt1 = enter("stt1", [128, NT], F32)
    stt2 = enter("stt2", [128, NT], F32)

    s_x = sem("s_x")
    s_w = sem("s_w")
    s_a = sem("s_a")
    s_d = sem("s_d")
    s_p = sem("s_p")
    s_sp = sem("s_sp")
    s_rl = sem("s_rl")
    s_cc = sem("s_cc")
    s_bp = sem("s_bp")
    s_bv = sem("s_bv")
    s_g = sem("s_g")
    s_dbg = sem("s_dbg")
    s_st = sem("s_st")
    s_wg = sem("s_wg")

    C = dict(x=0, w=0, a=0, d=0, p=0, sp=0, rl=0, cc=0, bp=0, bv=0, g=0, st=0,
             wg=0)


    def finish_early():
        fin = enter("fin_sb", [128, 64], F32)
        with nc.Block() as blk:
            @blk.vector
            def _(v):
                v.memset(fin[:, :], 1.0).then_inc(s_d, 1)

            @blk.sync
            def _(sync):
                sync.wait_ge(s_d, C["d"] + 1)
                sync.dma_start(out=out_d[0:128, 0:64], in_=fin[:, :]).then_inc(s_st, 16)
                sync.wait_ge(s_st, C["st"] + 16)
        es.close()
        return nc

    def vsp(v):
        return v.tensor_scalar(jk_f[:, 512:576], jk_f[:, 512:576], 1.0, None,
                               op0=ALU.mult)

    def make_bisect(tag, cand, ncol, Ntot, rank, frac, s_slot, gs_slot, base,
                    ro=8, dbg=None, nthr=3, nrnd=None, frc=None):
        nch = ncol // 512
        if nrnd is None:
            nrnd = NR
        if frc is None:
            frc = B_FRC3
        inv = 1.0 / (nthr + 1)
        ncc = nthr * nch
        act_steps, dve_steps = [], []

        def dve_init(v):
            v.reduce_max(cstat[:, 0:1], cand[:, :], axis=AX)
            vsp(v).then_inc(s_bv, 1)
        dve_steps.append(dve_init)

        def dve_init2(v):
            v.wait_ge(s_bp, base["bp"] + 16 * 3)
            v.reduce_max(bis[0:1, B_HI:B_HI + 1], srow[0:1, 0:128], axis=AX)
            vsp(v)
            v.memset(bis[0:1, B_LO:B_LO + 1], 0.0)
            vsp(v).then_inc(s_bv, 1)
        dve_steps.append(dve_init2)

        def mk_dve_round(j):
            def f(v):
                if j > 0:
                    v.wait_ge(s_bp, base["bp"] + 16 * (3 + 4 * j))
                    for k in range(nthr):
                        v.tensor_reduce(
                            bis[0:1, B_CNT + k:B_CNT + k + 1],
                            srow[0:1, 0:128 * nthr].rearrange(
                                "q (p c) -> q p c", p=128)[:, :, k:k + 1],
                            axis=AXY, op=ALU.add)
                    if j <= nrnd:
                        vsp(v)
                        v.tensor_scalar(bis[0:1, B_CNT:B_CNT + nthr], bis[0:1, B_CNT:B_CNT + nthr],
                                        0.5, Ntot * 0.5, op0=ALU.mult, op1=ALU.add)
                        vsp(v)
                        v.tensor_scalar(bis[0:1, B_PRED:B_PRED + nthr], bis[0:1, B_CNT:B_CNT + nthr],
                                        float(rank) - 0.5, None, op0=ALU.is_lt)
                        vsp(v)
                        v.reduce_sum(bis[0:1, B_NB:B_NB + 1], bis[0:1, B_PRED:B_PRED + nthr], axis=AX)
                        vsp(v)
                        v.tensor_scalar(bis[0:1, B_IDX:B_IDX + 1], bis[0:1, B_NB:B_NB + 1],
                                        -1.0, float(nthr), op0=ALU.mult, op1=ALU.add)
                        v.tensor_tensor(bis[0:1, B_W:B_W + 1], bis[0:1, B_HI:B_HI + 1],
                                        bis[0:1, B_LO:B_LO + 1], op=ALU.subtract)
                        vsp(v)
                        v.tensor_tensor(bis[0:1, B_T0:B_T0 + 1], bis[0:1, B_W:B_W + 1],
                                        bis[0:1, B_IDX:B_IDX + 1], op=ALU.mult)
                        vsp(v)
                        v.tensor_scalar(bis[0:1, B_T0:B_T0 + 1], bis[0:1, B_T0:B_T0 + 1],
                                        inv, None, op0=ALU.mult)
                        vsp(v)
                        v.tensor_tensor(bis[0:1, B_LO:B_LO + 1], bis[0:1, B_LO:B_LO + 1],
                                        bis[0:1, B_T0:B_T0 + 1], op=ALU.add)
                        v.tensor_scalar(bis[0:1, B_T1:B_T1 + 1], bis[0:1, B_W:B_W + 1],
                                        inv, None, op0=ALU.mult)
                        vsp(v)
                        v.tensor_tensor(bis[0:1, B_HI:B_HI + 1], bis[0:1, B_LO:B_LO + 1],
                                        bis[0:1, B_T1:B_T1 + 1], op=ALU.add)
                        vsp(v)
                if j <= nrnd - 1:
                    v.tensor_tensor(bis[0:1, B_W:B_W + 1], bis[0:1, B_HI:B_HI + 1],
                                    bis[0:1, B_LO:B_LO + 1], op=ALU.subtract)
                    vsp(v)
                    v.tensor_scalar(bis[0:1, B_THR:B_THR + nthr], bis[0:1, frc:frc + nthr],
                                    bis[0:1, B_W:B_W + 1], None, op0=ALU.mult)
                    vsp(v)
                    v.tensor_scalar(bis[0:1, B_THR:B_THR + nthr], bis[0:1, B_THR:B_THR + nthr],
                                    bis[0:1, B_LO:B_LO + 1], None, op0=ALU.add)
                elif j == nrnd:
                    v.tensor_scalar(bis[0:1, B_THR:B_THR + nthr], bis[0:1, frc:frc + nthr],
                                    0.0, None, op0=ALU.mult)
                    vsp(v)
                    v.tensor_scalar(bis[0:1, B_THR:B_THR + nthr], bis[0:1, B_THR:B_THR + nthr],
                                    bis[0:1, B_HI:B_HI + 1], None, op0=ALU.add)
                vsp(v)
                v.tensor_scalar(bis[0:1, B_NTH:B_NTH + nthr], bis[0:1, B_THR:B_THR + nthr],
                                -1.0, None, op0=ALU.mult)
                vsp(v).then_inc(s_bv, 1)
            return f

        for j in range(nrnd + 1):
            dve_steps.append(mk_dve_round(j))

        def mk_act_round(j):
            def f(act):
                act.wait_ge(s_bp, base["bp"] + 16 * (3 + 4 * j + 2))
                mm = None
                for k in range(nthr):
                    for c in range(nch):
                        mm = act.activation(
                            pJ[:, :], cand[:, c * 512:(c + 1) * 512], AF.Sign,
                            bias=bisb[:, k:k + 1],
                            accum_out=cstat[:, j % 2 * 32 + k * nch + c:
                                            j % 2 * 32 + k * nch + c + 1])
                mm.then_inc(s_a, 1)
            return f

        for j in range(nrnd + 1):
            act_steps.append(mk_act_round(j))

        def dve_extract(v):
            v.wait_ge(s_bp, base["bp"] + 16 * (3 + 4 * (nrnd + 1)))
            v.tensor_reduce(
                bis[0:1, B_CHI:B_CHI + 1],
                srow[0:1, 0:128 * nthr].rearrange("q (p c) -> q p c", p=128)[:, :, 0:1],
                axis=AXY, op=ALU.add)
            vsp(v)
            v.tensor_scalar(bis[0:1, B_CHI:B_CHI + 1], bis[0:1, B_CHI:B_CHI + 1],
                            0.5, Ntot * 0.5, op0=ALU.mult, op1=ALU.add)
            v.tensor_scalar(bisb[:, 7:8], bisb[:, 0:1], -1.0, None, op0=ALU.mult)
            vsp(v)
            v.tensor_scalar(jk_f[:, 0:512], cand[:, 0:512], bisb[:, 7:8], -1e30,
                            op0=ALU.is_gt, op1=ALU.mult)
            v.tensor_tensor(jk_f[:, 0:512], jk_f[:, 0:512], cand[:, 0:512], op=ALU.add)
            v.reduce_max(cstat[:, 0:1], jk_f[:, 0:512], axis=AX)
            for c in range(1, nch):
                v.tensor_scalar(jk_f[:, 0:512], cand[:, c * 512:(c + 1) * 512],
                                bisb[:, 7:8], -1e30, op0=ALU.is_gt, op1=ALU.mult)
                v.tensor_tensor(jk_f[:, 0:512], jk_f[:, 0:512],
                                cand[:, c * 512:(c + 1) * 512], op=ALU.add)
                v.reduce_max(cstat[:, 1:2], jk_f[:, 0:512], axis=AX)
                vsp(v)
                v.tensor_tensor(cstat[:, 0:1], cstat[:, 0:1], cstat[:, 1:2], op=ALU.max)
            v.tensor_scalar(jk_f[:, 0:512], cand[:, 0:512], bisb[:, 7:8], 1e30,
                            op0=ALU.is_le, op1=ALU.mult)
            v.tensor_tensor(jk_f[:, 0:512], jk_f[:, 0:512], cand[:, 0:512], op=ALU.add)
            v.tensor_reduce(cstat[:, 2:3], jk_f[:, 0:512], axis=AX, op=ALU.min)
            for c in range(1, nch):
                v.tensor_scalar(jk_f[:, 0:512], cand[:, c * 512:(c + 1) * 512],
                                bisb[:, 7:8], 1e30, op0=ALU.is_le, op1=ALU.mult)
                v.tensor_tensor(jk_f[:, 0:512], jk_f[:, 0:512],
                                cand[:, c * 512:(c + 1) * 512], op=ALU.add)
                v.tensor_reduce(cstat[:, 3:4], jk_f[:, 0:512], axis=AX, op=ALU.min)
                vsp(v)
                v.tensor_tensor(cstat[:, 2:3], cstat[:, 2:3], cstat[:, 3:4],
                                op=ALU.min)
            vsp(v).then_inc(s_bv, 1)
        dve_steps.append(dve_extract)

        def dve_final(v):
            v.wait_ge(s_bp, base["bp"] + 16 * (3 + 4 * (nrnd + 1) + 4))
            v.tensor_reduce(bis[0:1, B_T0:B_T0 + 1],
                            srow[0:1, 0:256].rearrange("q (p c) -> q p c", p=128)[:, :, 0:1],
                            axis=AXY, op=ALU.max)
            v.tensor_reduce(bis[0:1, B_T1:B_T1 + 1],
                            srow[0:1, 256:512].rearrange("q (p c) -> q p c", p=128)[:, :, 0:1],
                            axis=AXY, op=ALU.min)
            v.tensor_scalar(bis[0:1, B_PRED:B_PRED + 1], bis[0:1, B_CHI:B_CHI + 1],
                            float(rank - 1), None, op0=ALU.is_equal)
            vsp(v)
            v.tensor_tensor(bis[0:1, B_T1:B_T1 + 1], bis[0:1, B_T1:B_T1 + 1],
                            bis[0:1, B_T0:B_T0 + 1], op=ALU.subtract)
            vsp(v)
            v.tensor_tensor(bis[0:1, B_T1:B_T1 + 1], bis[0:1, B_T1:B_T1 + 1],
                            bis[0:1, B_PRED:B_PRED + 1], op=ALU.mult)
            vsp(v)
            v.tensor_scalar(bis[0:1, B_T1:B_T1 + 1], bis[0:1, B_T1:B_T1 + 1],
                            float(frac), None, op0=ALU.mult)
            vsp(v)
            v.tensor_tensor(bis[0:1, B_AMAX:B_AMAX + 1], bis[0:1, B_T0:B_T0 + 1],
                            bis[0:1, B_T1:B_T1 + 1], op=ALU.add)
            vsp(v)
            v.tensor_scalar(bis[0:1, B_AMAX:B_AMAX + 1], bis[0:1, B_AMAX:B_AMAX + 1],
                            1e-5, None, op0=ALU.max)
            vsp(v)
            v.reciprocal(bis[0:1, B_S:B_S + 1], bis[0:1, B_AMAX:B_AMAX + 1])
            vsp(v)
            v.tensor_scalar(bis[0:1, B_S:B_S + 1], bis[0:1, B_S:B_S + 1],
                            127.0, None, op0=ALU.mult)
            v.tensor_tensor(bis[0:1, B_GS:B_GS + 1], bis[0:1, B_GAM:B_GAM + 1],
                            bis[0:1, B_AMAX:B_AMAX + 1], op=ALU.mult)
            vsp(v)
            v.tensor_scalar(bis[0:1, B_GS:B_GS + 1], bis[0:1, B_GS:B_GS + 1],
                            1.0 / 127.0, None, op0=ALU.mult)
            vsp(v).then_inc(s_bv, 1)
        dve_steps.append(dve_final)

        def emit_gpsimd(gp):
            bp = base["bp"]
            cc = base["cc"]
            bv = base["bv"]
            gp.wait_ge(s_bv, bv + 1)
            gp.dma_start(out=cc_in_t[0:128].rearrange("(p c) -> p c", p=128),
                         in_=cstat[:, 0:1]).then_inc(s_bp, 16)
            gp.wait_ge(s_bp, bp + 16)
            gp.collective_compute("AllReduce", ALU.max, replica_groups=RG,
                                  ins=[cc_in_t[0:128].opt()],
                                  outs=[cc_out_t[0:128].opt()]).then_inc(s_cc, 1)
            gp.wait_ge(s_cc, cc + 1)
            gp.dma_start(out=srow[0:1, 0:128],
                         in_=cc_out_t[0:128].rearrange("(q n) -> q n", q=1)).then_inc(s_bp, 16)
            gp.dma_start(out=row_t.ap()[0:1, 0:1], in_=bis[0:1, 0:1]).then_inc(s_bp, 16)
            for j in range(nrnd + 1):
                gp.wait_ge(s_bv, bv + 3 + j)
                if dbg is not None:
                    gp.dma_start(out=dbg[j:j + 1, :],
                                 in_=bis[0:1, 0:32]).then_inc(s_dbg, 16)
                gp.dma_start(out=row_t.ap()[0:1, 0:nthr],
                             in_=bis[0:1, B_NTH:B_NTH + nthr]).then_inc(s_bp, 16)
                gp.wait_ge(s_bp, bp + 16 * (3 + 4 * j + 1))
                gp.dma_start(out=bisb[:, 0:nthr],
                             in_=row_t.ap()[0:1, 0:nthr].partition_broadcast(128)
                             ).then_inc(s_bp, 16)
                gp.wait_ge(s_a, base["a_of_round"](j))
                b0 = j % 2 * 32
                roff = 1024 + 1024 * (j % 2)
                h = nch
                while h > 1:
                    h //= 2
                    for k in range(nthr):
                        gp.tensor_tensor(cstat[:, b0 + k * nch:b0 + k * nch + h],
                                         cstat[:, b0 + k * nch:b0 + k * nch + h],
                                         cstat[:, b0 + k * nch + h:b0 + k * nch + 2 * h],
                                         op=ALU.add)
                gp.dma_start(out=cc_in_t[roff:roff + 128 * nthr].rearrange(
                                 "(p k c) -> p k c", p=128, k=nthr),
                             in_=cstat[:, b0:b0 + nthr * nch].rearrange(
                                 "p (k c) -> p k c", k=nthr)[:, :, 0:1],
                             allow_slow_non_contiguous=True).then_inc(s_bp, 16)
                gp.wait_ge(s_bp, bp + 16 * (3 + 4 * j + 3))
                gp.collective_compute("AllReduce", ALU.add, replica_groups=RG,
                                      ins=[cc_in_t[roff:roff + 128 * nthr].opt()],
                                      outs=[cc_out_t[0:128 * nthr].opt()]).then_inc(s_cc, 1)
                gp.wait_ge(s_cc, cc + 2 + j)
                gp.dma_start(out=srow[0:1, 0:128 * nthr],
                             in_=cc_out_t[0:128 * nthr].rearrange("(q n) -> q n", q=1)
                             ).then_inc(s_bp, 16)
            gp.wait_ge(s_bv, bv + 2 + (nrnd + 1) + 1)
            if dbg is not None:
                gp.dma_start(out=dbg[nrnd + 1:nrnd + 2, :],
                             in_=bis[0:1, 0:32]).then_inc(s_dbg, 16)
                gp.dma_start(out=dbg2_t.ap()[:, 0:ncol],
                             in_=cand[:, :]).then_inc(s_dbg, 16)
            gp.dma_start(out=cc_in_t[3072:3328].rearrange("(p c) -> p c", p=128),
                         in_=cstat[:, 0:2]).then_inc(s_bp, 16)
            gp.dma_start(out=cc_in_t[3328:3584].rearrange("(p c) -> p c", p=128),
                         in_=cstat[:, 2:4]).then_inc(s_bp, 16)
            gp.wait_ge(s_bp, bp + 16 * (3 + 4 * (nrnd + 1) + 2))
            gp.collective_compute("AllReduce", ALU.max, replica_groups=RG,
                                  ins=[cc_in_t[3072:3328].opt()],
                                  outs=[cc_out_t[0:256].opt()]).then_inc(s_cc, 1)
            gp.wait_ge(s_cc, cc + 1 + (nrnd + 1) + 1)
            gp.collective_compute("AllReduce", ALU.min, replica_groups=RG,
                                  ins=[cc_in_t[3328:3584].opt()],
                                  outs=[cc_out_t[256:512].opt()]).then_inc(s_cc, 1)
            gp.wait_ge(s_cc, cc + 1 + (nrnd + 1) + 2)
            gp.dma_start(out=srow[0:1, 0:256],
                         in_=cc_out_t[0:256].rearrange("(q n) -> q n", q=1)).then_inc(s_bp, 16)
            gp.dma_start(out=srow[0:1, 256:512],
                         in_=cc_out_t[256:512].rearrange("(q n) -> q n", q=1)).then_inc(s_bp, 16)
            gp.wait_ge(s_bv, bv + 2 + (nrnd + 1) + 2)
            gp.dma_start(out=row_t.ap()[0:1, ro:ro + 4],
                         in_=bis[0:1, B_AMAX:B_AMAX + 4]).then_inc(s_bp, 16)
            gp.wait_ge(s_bp, bp + 16 * (3 + 4 * (nrnd + 1) + 5))
            gp.dma_start(out=bisb[:, 4:8],
                         in_=row_t.ap()[0:1, ro:ro + 4].partition_broadcast(128)
                         ).then_inc(s_bp, 16)
            gp.wait_ge(s_bp, bp + 16 * (3 + 4 * (nrnd + 1) + 6))
            if dbg is not None:
                gp.wait_ge(s_dbg, 16 * (nrnd + 3))
            gp.tensor_copy(sc1[:, s_slot:s_slot + 1], bisb[:, 5:6])
            gp.tensor_copy(sc1[:, gs_slot:gs_slot + 1], bisb[:, 7:8]).then_inc(s_g, 1)

        n_bp = 16 * (3 + 4 * (nrnd + 1) + 6)
        n_cc = 1 + (nrnd + 1) + 2
        n_bv = 2 + (nrnd + 1) + 2
        n_act = nrnd + 1
        n_g = 1
        return act_steps, dve_steps, emit_gpsimd, dict(bp=n_bp, cc=n_cc, bv=n_bv,
                                                       a=n_act, g=n_g)


    with nc.Block() as blk:
        @blk.sync
        def _(sync):
            for i, bd in enumerate(b_d):
                sync.dma_start(out=b_sb[:, i * DC:(i + 1) * DC],
                               in_=bd.rearrange("(c p) -> p c", p=128),
                               allow_slow_non_contiguous=True).then_inc(s_w, 16)
            sync.dma_start(out=g_bc[:, :], in_=lng_d.partition_broadcast(128)).then_inc(s_w, 16)
            sync.dma_start(out=bgb_bc[:, :], in_=lnb_d.partition_broadcast(128)).then_inc(s_w, 16)
            sync.dma_start(out=b4_bc[:, :], in_=b_d[3].partition_broadcast(128)).then_inc(s_w, 16)
            for i in range(NT):
                if i >= 2:
                    sync.wait_ge(s_a, 2 * (i - 1))
                sync.dma_start(out=xt[i % 2][:, :],
                               in_=x_d[i * 128:(i + 1) * 128, :]).then_inc(s_x, 16)

        @blk.gpsimd
        def _(gp):
            gp.iota(iot[:, :], [[1, 128]], channel_multiplier=0,
                    allow_small_or_imprecise_dtypes=True)
            gp.iota(pidx[:, :], [[0, 1]], channel_multiplier=1,
                    allow_small_or_imprecise_dtypes=True).then_inc(s_g, 1)

        @blk.scalar
        def _(act):
            for i in range(NT):
                act.wait_ge(s_x, 16 * (i + 1))
                act.activation(jk_f[:, :], xt[i % 2][:, :], AF.Identity,
                               accum_out=st_sum[:, i:i + 1])
                act.activation(jk_f[:, :], xt[i % 2][:, :], AF.Square,
                               accum_out=st_sq[:, i:i + 1]).then_inc(s_a, 2)

        @blk.vector
        def _(v):
            v.memset(ones_col[:, :], 1.0)
            v.memset(ones_row[:, :], 1.0)
            v.memset(ones_rowb[:, :], 1.0)
            v.memset(bis[:, :], 0.0)
            for k in range(3):
                v.memset(bis[0:1, B_FRC3 + k:B_FRC3 + k + 1], (k + 1) / 4.0)
            for k in range(7):
                v.memset(bis[0:1, B_FRC7 + k:B_FRC7 + k + 1], (k + 1) / 8.0)
            v.wait_ge(s_g, 1)
            v.tensor_scalar(identf[:, :], iot[:, :], pidx[:, :], None, op0=ALU.is_equal)
            v.tensor_copy(ident[:, :], identf[:, :]).then_inc(s_d, 1)
    C["x"] += NT * 16
    C["w"] += 7 * 16
    C["g"] += 1
    C["a"] += 2 * NT
    C["d"] += 1

    a0, d0 = C["a"], C["d"]
    with nc.Block() as blk:
        @blk.vector
        def _(v):
            v.wait_ge(s_a, a0)
            v.tensor_scalar(stt1[:, :], st_sum[:, :], INV1024, None, op0=ALU.mult)
            vsp(v)
            v.tensor_tensor(stt2[:, :], stt1[:, :], stt1[:, :], op=ALU.mult)
            v.tensor_scalar(rstd[:, :], st_sq[:, :], INV1024, 1e-5, op0=ALU.mult, op1=ALU.add)
            vsp(v)
            v.tensor_tensor(rstd[:, :], rstd[:, :], stt2[:, :], op=ALU.subtract)
            vsp(v).then_inc(s_d, 1)

        @blk.scalar
        def _(act):
            act.wait_ge(s_d, d0 + 1)
            act.activation(negmr[:, :], rstd[:, :], AF.Sqrt).then_inc(s_a, 1)

        @blk.vector
        def _(v):
            v.wait_ge(s_a, a0 + 1)
            v.reciprocal(rs2[:, :], negmr[:, :])
            vsp(v)
            v.tensor_tensor(stt2[:, :], rs2[:, :], rs2[:, :], op=ALU.mult)
            v.tensor_scalar(rstd[:, :], rstd[:, :], -1e-5, None, op0=ALU.add)
            vsp(v)
            v.tensor_tensor(stt2[:, :], stt2[:, :], rstd[:, :], op=ALU.mult)
            vsp(v)
            v.tensor_scalar(stt2[:, :], stt2[:, :], 1e-5, None, op0=ALU.add)
            v.tensor_scalar(stt1[:, :], stt1[:, :], -1.0, None, op0=ALU.mult)
            vsp(v)
            v.tensor_tensor(negmr[:, :], stt1[:, :], rs2[:, :], op=ALU.mult)
            v.tensor_copy(rstd[:, :], rs2[:, :])
            vsp(v).then_inc(s_d, 1)

        @blk.scalar
        def _(act):
            act.wait_ge(s_d, d0 + 2)
            act.activation(stt2[:, :], stt2[:, :], AF.Sqrt).then_inc(s_a, 1)

        @blk.vector
        def _(v):
            v.wait_ge(s_a, a0 + 2)
            v.reciprocal(rs2[:, :], stt2[:, :])
            vsp(v).then_inc(s_d, 1)
    C["a"] += 2
    C["d"] += 3

    x1, a1, d1, p1 = C["x"], C["a"], C["d"], C["p"]
    with nc.Block() as blk:
        @blk.sync
        def _(sync):
            for i in range(NT):
                if i >= 2:
                    sync.wait_ge(s_a, a1 + (i - 1))
                sync.dma_start(out=xt[i % 2][:, :],
                               in_=x_d[i * 128:(i + 1) * 128, :]).then_inc(s_x, 16)

        @blk.scalar
        def _(act):
            for i in range(NT):
                act.wait_ge(s_x, x1 + 16 * (i + 1))
                if i >= 2:
                    act.wait_ge(s_d, d1 + (i - 1))
                    act.wait_ge(s_p, p1 + (i - 1))
                act.activation(xnb[i % 2][:, :], xt[i % 2][:, :], AF.Identity,
                               bias=negmr[:, i:i + 1],
                               scale=rstd[:, i:i + 1]).then_inc(s_a, 1)

        @blk.tensor
        def _(pe):
            for i in range(NT):
                pe.wait_ge(s_d, d1 + i + 1)
                mm = None
                for dc in range(DC):
                    pe.transpose(pT1[:, dc * 128:(dc + 1) * 128],
                                 xnb[i % 2][:, dc * 128:(dc + 1) * 128], ident[:, :])
                for dc in range(DC):
                    mm = pe.transpose(pT2[:, dc * 128:(dc + 1) * 128],
                                      l2b[i % 2][:, dc * 128:(dc + 1) * 128], ident[:, :])
                mm.then_inc(s_p, 1)

        @blk.vector
        def _(v):
            for i in range(NT):
                v.wait_ge(s_a, a1 + (i + 1))
                v.scalar_tensor_tensor(l2b[i % 2][:, :], xnb[i % 2][:, :], rs2[:, i:i + 1],
                                       g_bc[:, :], op0=ALU.mult, op1=ALU.mult)
                v.tensor_tensor(l2b[i % 2][:, :], l2b[i % 2][:, :], bgb_bc[:, :],
                                op=ALU.add).then_inc(s_d, 1)
                v.scalar_tensor_tensor(jk_f[:, :], l2b[i % 2][:, :], -1.0, l2b[i % 2][:, :],
                                       op0=ALU.mult, op1=ALU.max)
                for ch in range(4):
                    v.max(q1cand[:, i * 32 + ch * 8:i * 32 + ch * 8 + 8],
                          jk_f[:, ch * 256:(ch + 1) * 256])
                v.wait_ge(s_p, p1 + i + 1)
                v.tensor_copy(
                    xnT.ap().rearrange("p (dc t) -> p dc t", dc=DC)[:, :, i * 128:(i + 1) * 128],
                    pT1[:, :].rearrange("p (dc t) -> p dc t", dc=DC))
                v.tensor_copy(
                    ln2T.ap().rearrange("p (dc t) -> p dc t", dc=DC)[:, :, i * 128:(i + 1) * 128],
                    pT2[:, :].rearrange("p (dc t) -> p dc t", dc=DC))
    C["x"] += NT * 16
    C["a"] += NT
    C["p"] += NT
    C["d"] += NT

    if stop == 1:
        return finish_early()

    for nm in ("stt2", "stt1", "rs2", "negmr", "rstd", "st_sq", "st_sum",
               "bgb_bc", "g_bc", "l2b1", "l2b0", "xnb1", "xnb0", "l2_f",
               "xn_f1", "xn_f0", "xt1", "xt0", "iot", "identf"):
        free(nm)
    qkvst = [enter(f"qkvst{i}", [128, 512], BF16) for i in range(2)]
    Wq_sb = enter("Wq_sb", [128, DC * D], BF16)
    Wk_sb = enter("Wk_sb", [128, DC * D], BF16)
    with nc.Block() as blk:
        @blk.gpsimd
        def _(gp):
            for wi, Wsb in enumerate((Wq_sb, Wk_sb)):
                for dc in range(DC):
                    gp.dma_start(out=Wsb[:, dc * D:(dc + 1) * D],
                                 in_=W_d[wi][dc * 128:(dc + 1) * 128, :]).then_inc(s_wg, 16)
    C["wg"] += 16 * 16

    d3, a3, p3, sp3, w3qk = C["d"], C["a"], C["p"], C["sp"], C["wg"]

    def qk_spills(u):
        if u < 0:
            return 0
        q = min(u + 1, 64)
        k = max(0, u + 1 - 64)
        return 2 * q + k

    with nc.Block() as blk:
        @blk.tensor
        def _(pe):
            pe.wait_ge(s_wg, w3qk)
            for proj in range(2):
                Wsb = (Wq_sb, Wk_sb)[proj]
                for ts in range(8):
                    for ec in range(DC):
                        u = (proj * 8 + ts) * DC + ec
                        psum = (pA, pB, pO1, pO2)[u % 4]
                        if u >= 4:
                            pe.wait_ge(s_a, a3 + (u - 3))
                        mm = None
                        for dc in range(DC):
                            mm = pe.matmul(psum[:, :],
                                           Wsb[:, dc * D + ec * 128:dc * D + (ec + 1) * 128],
                                           xnT[:, dc * T + ts * 512:dc * T + (ts + 1) * 512],
                                           start=(dc == 0), stop=(dc == DC - 1))
                        mm.then_inc(s_p, 1)

        @blk.scalar
        def _(act):
            for proj in range(2):
                for ts in range(8):
                    for ec in range(DC):
                        u = (proj * 8 + ts) * DC + ec
                        act.wait_ge(s_p, p3 + u + 1)
                        if u >= 2:
                            act.wait_ge(s_sp, sp3 + 16 * qk_spills(u - 2))
                        act.activation(qkvst[u % 2][:, :],
                                       (pA, pB, pO1, pO2)[u % 4][:, :], AF.Identity,
                                       bias=b_sb[:, proj * DC + ec:proj * DC + ec + 1]
                                       ).then_inc(s_a, 1)

        @blk.sync
        def _(sync):
            for proj in range(2):
                for ts in range(8):
                    for ec in range(DC):
                        u = (proj * 8 + ts) * DC + ec
                        sync.wait_ge(s_a, a3 + u + 1)
                        st = qkvst[u % 2]
                        if proj == 0:
                            for h in range(2):
                                sc = ts * 2 + h
                                sync.dma_start(
                                    out=QT_d[sc, ec * 128:(ec + 1) * 128, :],
                                    in_=st[:, h * 256:(h + 1) * 256]).then_inc(s_sp, 16)
                        else:
                            sync.dma_start(
                                out=KT_d[ec * 128:(ec + 1) * 128, ts * 512:(ts + 1) * 512],
                                in_=st[:, :]).then_inc(s_sp, 16)
    C["p"] += 128
    C["a"] += 128
    C["sp"] += 192 * 16

    if stop == 2:
        return finish_early()

    free("Wk_sb", "Wq_sb")
    Wv_sb = enter("Wv_sb", [128, DC * D], BF16)
    b3h = enter("b3h", [128, D], F32)
    d4, a4, p4, sp4, w4c = C["d"], C["a"], C["p"], C["sp"], C["wg"]
    with nc.Block() as blk:
        @blk.gpsimd
        def _(gp):
            gp.dma_start(out=b3h[:, :], in_=b_d[2].partition_broadcast(128)).then_inc(s_wg, 16)
            for dc in range(DC):
                gp.dma_start(out=Wv_sb[:, dc * D:(dc + 1) * D],
                             in_=W_d[2][dc * 128:(dc + 1) * 128, :]).then_inc(s_wg, 16)

        @blk.tensor
        def _(pe):
            pe.wait_ge(s_a, a4)
            pe.wait_ge(s_wg, w4c + 16 * (DC + 1))
            for tt in range(NT):
                for eh in range(2):
                    u = tt * 2 + eh
                    psum = (pA, pB, pO1, pO2)[u % 4]
                    if u >= 4:
                        pe.wait_ge(s_d, d4 + (u - 3))
                    mm = None
                    for dc in range(DC):
                        mm = pe.matmul(psum[:, :],
                                       xnT[:, dc * T + tt * 128:dc * T + (tt + 1) * 128],
                                       Wv_sb[:, dc * D + eh * 512:dc * D + (eh + 1) * 512],
                                       start=(dc == 0), stop=(dc == DC - 1))
                    mm.then_inc(s_p, 1)

        @blk.vector
        def _(v):
            v.wait_ge(s_wg, w4c + 16)
            for tt in range(NT):
                for eh in range(2):
                    u = tt * 2 + eh
                    v.wait_ge(s_p, p4 + u + 1)
                    if u < 2:
                        v.wait_ge(s_sp, sp4)
                    else:
                        v.wait_ge(s_sp, sp4 + 16 * (u - 1))
                    v.tensor_tensor(qkvst[u % 2][:, :], (pA, pB, pO1, pO2)[u % 4][:, :],
                                    b3h[:, eh * 512:(eh + 1) * 512], op=ALU.add).then_inc(s_d, 1)

        @blk.sync
        def _(sync):
            for tt in range(NT):
                for eh in range(2):
                    u = tt * 2 + eh
                    sync.wait_ge(s_d, d4 + u + 1)
                    sync.dma_start(out=V_d[tt * 128:(tt + 1) * 128, eh * 512:(eh + 1) * 512],
                                   in_=qkvst[u % 2][:, :]).then_inc(s_sp, 16)
    C["wg"] += 16 + DC * 16
    C["p"] += 64
    C["d"] += 64
    C["sp"] += 64 * 16

    if stop == 3:
        return finish_early()

    free("b3h", "Wv_sb", "qkvst1", "qkvst0")
    wstA = enter("wstA", [128, 1024], F32)
    wstA2 = enter("wstA2", [128, 1024], F32)
    qk_rl = enter("qk_rl", [128, T], BF16)
    nsq = enter("nsq", [128, T], BF16)
    normQ = enter("normQ", [1, T], BF16)
    normK = enter("normK", [1, T], BF16)
    rl5, a5, p5, d5, w5, bp5, g5 = (C["rl"], C["a"], C["p"], C["d"], C["w"],
                                    C["bp"], C["g"])
    with nc.Block() as blk:
        @blk.sync
        def _(sync):
            sync.wait_ge(s_sp, sp3 + 16 * 192)
            for proj in range(2):
                for ec in range(DC):
                    k = proj * DC + ec
                    if k >= 1:
                        sync.wait_ge(s_a, a5 + k)
                    if proj == 0:
                        for sc in range(SC):
                            sync.dma_start(out=qk_rl[:, sc * 256:(sc + 1) * 256],
                                           in_=QT_d[sc, ec * 128:(ec + 1) * 128, :]
                                           ).then_inc(s_rl, 16)
                    else:
                        sync.dma_start(out=qk_rl[:, :],
                                       in_=KT_d[ec * 128:(ec + 1) * 128, :]).then_inc(s_rl, 16)
            for pc in range(32):
                if pc >= 2:
                    sync.wait_ge(s_a, a5 + 16 + pc - 1)
                sync.dma_start(out=(wstA, wstA2)[pc % 2][:, :],
                               in_=w1T_d[(pc // 4) * 128:(pc // 4 + 1) * 128,
                                         (pc % 4) * 1024:(pc % 4 + 1) * 1024]
                               ).then_inc(s_w, 16)

        @blk.scalar
        def _(act):
            nds = 0
            for proj in range(2):
                for ec in range(DC):
                    k = proj * DC + ec
                    nds += SC if proj == 0 else 1
                    act.wait_ge(s_rl, rl5 + 16 * nds)
                    if k >= 1:
                        act.wait_ge(s_d, d5 + 8 * k)
                    act.activation(nsq[:, :], qk_rl[:, :], AF.Square).then_inc(s_a, 1)
            for pc in range(32):
                act.wait_ge(s_w, w5 + 16 * (pc + 1))
                act.activation(jk_f[:, 0:1024], (wstA, wstA2)[pc % 2][:, :], AF.Abs,
                               accum_out=cstat[:, 32 + pc:33 + pc]).then_inc(s_a, 1)

        @blk.tensor
        def _(pe):
            for k in range(16):
                pe.wait_ge(s_a, a5 + k + 1)
                for ts in range(8):
                    j = k * 8 + ts
                    if j >= 4:
                        pe.wait_ge(s_d, d5 + j - 3)
                    pe.matmul((pR, pO1, pA, pB)[j % 4][0:1, 0:512], ones_col[:, :],
                              nsq[:, ts * 512:(ts + 1) * 512],
                              start=True, stop=True).then_inc(s_p, 1)

        @blk.vector
        def _(v):
            v.memset(normQ[:, :], 0.0)
            v.memset(normK[:, :], 0.0)
            for k in range(16):
                row = normQ if k < 8 else normK
                for ts in range(8):
                    j = k * 8 + ts
                    v.wait_ge(s_p, p5 + j + 1)
                    v.tensor_tensor(row[0:1, ts * 512:(ts + 1) * 512],
                                    row[0:1, ts * 512:(ts + 1) * 512],
                                    (pR, pO1, pA, pB)[j % 4][0:1, 0:512],
                                    op=ALU.add).then_inc(s_d, 1)
            v.wait_ge(s_a, a5 + 16 + 32)
            v.reduce_sum(cstat[:, 0:1], cstat[:, 32:64], axis=AX)
            vsp(v).then_inc(s_d, 1)

        @blk.gpsimd
        def _(gp):
            gp.wait_ge(s_d, d5 + 129)
            gp.dma_start(out=cc_in_t[0:128].rearrange("(p c) -> p c", p=128),
                         in_=cstat[:, 0:1]).then_inc(s_bp, 16)
            gp.wait_ge(s_bp, bp5 + 16)
            gp.dma_start(out=srow[0:1, 0:128],
                         in_=cc_in_t[0:128].rearrange("(q n) -> q n", q=1)).then_inc(s_bp, 16)
            gp.wait_ge(s_bp, bp5 + 32)
            gp.engine_nop().then_inc(s_g, 1)

        @blk.vector
        def _(v):
            v.wait_ge(s_g, g5 + 1)
            v.reduce_sum(bis[0:1, B_GAM:B_GAM + 1], srow[0:1, 0:128], axis=AX)
            vsp(v)
            v.tensor_scalar(bis[0:1, B_GAM:B_GAM + 1], bis[0:1, B_GAM:B_GAM + 1],
                            NW, 1e-5, op0=ALU.mult, op1=ALU.add)
            vsp(v)
            v.reciprocal(bis[0:1, B_T0:B_T0 + 1], bis[0:1, B_GAM:B_GAM + 1])
            vsp(v).then_inc(s_d, 1)

        @blk.gpsimd
        def _(gp):
            gp.wait_ge(s_d, d5 + 130)
            gp.dma_start(out=row_t.ap()[0:1, 4:5],
                         in_=bis[0:1, B_T0:B_T0 + 1]).then_inc(s_bp, 16)
            gp.wait_ge(s_bp, bp5 + 48)
            gp.dma_start(out=sc1[:, 4:5],
                         in_=row_t.ap()[0:1, 4:5].partition_broadcast(128)).then_inc(s_bp, 16)
            gp.wait_ge(s_bp, bp5 + 64)
            gp.engine_nop().then_inc(s_g, 1)
    C["rl"] += (8 * SC + 8) * 16
    C["w"] += 32 * 16
    C["a"] += 16 + 32
    C["p"] += 128
    C["d"] += 128 + 1 + 1
    C["bp"] += 64
    C["g"] += 2

    a6i, d6i = C["a"], C["d"]
    with nc.Block() as blk:
        @blk.scalar
        def _(act):
            act.wait_ge(s_d, d6i - 2)
            act.activation(normQ[0:1, :], normQ[0:1, :], AF.Sqrt).then_inc(s_a, 1)

        @blk.vector
        def _(v):
            v.wait_ge(s_a, a6i + 1)
            v.reduce_max(bis[0:1, B_T1:B_T1 + 1], normK[0:1, :], axis=AX).then_inc(s_d, 1)

        @blk.scalar
        def _(act):
            act.wait_ge(s_d, d6i + 1)
            act.activation(bis[0:1, B_T1:B_T1 + 1], bis[0:1, B_T1:B_T1 + 1],
                           AF.Sqrt).then_inc(s_a, 1)

        @blk.vector
        def _(v):
            v.wait_ge(s_a, a6i + 2)
            v.tensor_scalar(m32[0:1, :], normQ[0:1, :], bis[0:1, B_T1:B_T1 + 1],
                            -1.0, op0=ALU.mult, op1=ALU.mult).then_inc(s_d, 1)
    C["a"] += 2
    C["d"] += 2
    free("normK", "normQ")

    if stop == 4:
        return finish_early()

    base1 = dict(a=C["a"], d=C["d"], bp=C["bp"], cc=C["cc"], bv=C["bv"])
    base1["a_of_round"] = lambda j, _a=C["a"]: _a + j + 1
    acts1, dves1, gp1, n1 = make_bisect("q1", q1cand, 1024, N1_CAND, R1, FRAC1,
                                        s_slot=0, gs_slot=1, base=base1, ro=8,
                                        dbg=dbg_t.ap() if DBG_Q1 else None,
                                        nthr=7, nrnd=NR1, frc=B_FRC7)
    with nc.Block() as blk:
        @blk.vector
        def _(v):
            for f in dves1:
                f(v)

        @blk.scalar
        def _(act):
            for f in acts1:
                f(act)

        @blk.gpsimd
        def _(gp):
            gp1(gp)
    C["a"] += n1["a"]
    C["bv"] += n1["bv"]
    C["bp"] += n1["bp"]
    C["cc"] += n1["cc"]
    C["g"] += n1["g"]

    if stop == 5:
        return finish_early()

    w7g, a7g, d7g, bp7g, g7g = C["w"], C["a"], C["d"], C["bp"], C["g"]
    with nc.Block() as blk:
        @blk.sync
        def _(sync):
            for ch in range(32):
                if ch >= 2:
                    sync.wait_ge(s_a, a7g + ch - 1)
                sync.dma_start(out=(wstA, wstA2)[ch % 2][:, :],
                               in_=w2T_d[ch * 128:(ch + 1) * 128, :]).then_inc(s_w, 16)

        @blk.scalar
        def _(act):
            for ch in range(32):
                act.wait_ge(s_w, w7g + 16 * (ch + 1))
                act.activation(jk_f[:, 0:1024], (wstA, wstA2)[ch % 2][:, :], AF.Abs,
                               accum_out=cstat[:, 32 + ch:33 + ch]).then_inc(s_a, 1)

        @blk.vector
        def _(v):
            v.wait_ge(s_a, a7g + 32)
            v.reduce_sum(cstat[:, 0:1], cstat[:, 32:64], axis=AX)
            vsp(v).then_inc(s_d, 1)

        @blk.gpsimd
        def _(gp):
            gp.wait_ge(s_d, d7g + 1)
            gp.dma_start(out=cc_in_t[0:128].rearrange("(p c) -> p c", p=128),
                         in_=cstat[:, 0:1]).then_inc(s_bp, 16)
            gp.wait_ge(s_bp, bp7g + 16)
            gp.dma_start(out=srow[0:1, 256:384],
                         in_=cc_in_t[0:128].rearrange("(q n) -> q n", q=1)).then_inc(s_bp, 16)
            gp.wait_ge(s_bp, bp7g + 32)
            gp.engine_nop().then_inc(s_g, 1)

        @blk.vector
        def _(v):
            v.wait_ge(s_g, g7g + 1)
            v.reduce_sum(bis[0:1, B_GAM:B_GAM + 1], srow[0:1, 256:384], axis=AX)
            vsp(v)
            v.tensor_scalar(bis[0:1, B_GAM:B_GAM + 1], bis[0:1, B_GAM:B_GAM + 1],
                            NW, 1e-5, op0=ALU.mult, op1=ALU.add)
            vsp(v)
            v.reciprocal(bis[0:1, B_T1:B_T1 + 1], bis[0:1, B_GAM:B_GAM + 1])
            vsp(v).then_inc(s_d, 1)

        @blk.gpsimd
        def _(gp):
            gp.wait_ge(s_d, d7g + 2)
            gp.dma_start(out=row_t.ap()[0:1, 5:6],
                         in_=bis[0:1, B_T1:B_T1 + 1]).then_inc(s_bp, 16)
            gp.wait_ge(s_bp, bp7g + 48)
            gp.dma_start(out=sc1[:, 5:6],
                         in_=row_t.ap()[0:1, 5:6].partition_broadcast(128)).then_inc(s_bp, 16)
            gp.wait_ge(s_bp, bp7g + 64)
            gp.engine_nop().then_inc(s_g, 1)
    C["w"] += 32 * 16
    C["a"] += 32
    C["d"] += 2
    C["bp"] += 64
    C["g"] += 2

    d7, g7b, w7 = C["d"], C["g"], C["w"]
    wst7 = [wstA, wstA2]
    with nc.Block() as blk:
        @blk.vector
        def _(v):
            v.wait_ge(s_g, g7b)
            for ts in range(8):
                for dc in range(DC):
                    k = dc * 8 + ts
                    sl_in = ln2T[:, k * 512:(k + 1) * 512]
                    sl_out = xqT[:, k * 512:(k + 1) * 512]
                    v.tensor_scalar(jk_f[:, 0:512], sl_in, sc1[:, 0:1], None, op0=ALU.mult)
                    v.tensor_scalar(jk_f[:, 0:512], jk_f[:, 0:512], MAGIC, MAGIC,
                                    op0=ALU.add, op1=ALU.subtract)
                    mm = v.tensor_scalar(sl_out, jk_f[:, 0:512], 127.0, -128.0,
                                         op0=ALU.min, op1=ALU.max)
                mm.then_inc(s_d, 1)

        @blk.sync
        def _(sync):
            sync.wait_ge(s_g, g7b)
            for pq in range(32):
                q, dc = pq // 8, pq % 8
                if pq >= 2:
                    sync.wait_ge(s_d, d7 + 8 + pq - 1)
                sync.dma_start(out=wst7[pq % 2][:, :],
                               in_=w1T_d[dc * 128:(dc + 1) * 128,
                                         q * 1024:(q + 1) * 1024]
                               ).then_inc(s_w, 16)

        @blk.vector
        def _(v):
            for pq in range(32):
                q, dc = pq // 8, pq % 8
                v.wait_ge(s_w, w7 + 16 * (pq + 1))
                v.tensor_scalar(jk_f[:, :], wst7[pq % 2][:, :], sc1[:, 4:5], None,
                                op0=ALU.mult)
                v.tensor_scalar(jk_f[:, :], jk_f[:, :], MAGIC, MAGIC,
                                op0=ALU.add, op1=ALU.subtract)
                v.tensor_scalar(w1q[:, dc * HID + q * 1024:dc * HID + (q + 1) * 1024],
                                jk_f[:, :], 1.0, -1.0, op0=ALU.min,
                                op1=ALU.max).then_inc(s_d, 1)
    C["d"] += 8 + 32
    C["w"] += 32 * 16
    free("nsq", "qk_rl", "wstA2", "wstA", "q1cand")
    q2cand = enter("q2cand", [128, 4096], F32)

    if stop == 6:
        return finish_early()

    gstage = enter("gstage", [128, HID], BF16)
    gst = [enter(f"gst{i}", [128, 512], F32) for i in range(2)]
    ab_f = [enter(f"ab_f{i}", [128, 512], F32) for i in range(2)]
    p8, a8, d8, gd8 = C["p"], C["a"], C["d"], C["sp"]
    with nc.Block() as blk:
        @blk.tensor
        def _(pe):
            for hc in range(HC):
                for ts in range(8):
                    u = hc * 8 + ts
                    if ts == 0:
                        pe.wait_ge(s_d, d8 - 32 + (hc // 8 + 1) * 8)
                    psum = (pA, pB, pO1, pO2)[u % 4]
                    if u >= 4:
                        pe.wait_ge(s_a, a8 + 2 * (u - 3))
                    mm = None
                    for dc in range(DC):
                        mm = pe.matmul(psum[:, :],
                                       w1q[:, dc * T + hc * 128:dc * T + (hc + 1) * 128],
                                       xqT[:, dc * T + ts * 512:dc * T + (ts + 1) * 512],
                                       start=(dc == 0), stop=(dc == DC - 1))
                    mm.then_inc(s_p, 1)

        @blk.scalar
        def _(act):
            for hc in range(HC):
                for ts in range(8):
                    u = hc * 8 + ts
                    act.wait_ge(s_p, p8 + u + 1)
                    if u >= 2:
                        act.wait_ge(s_d, d8 + (u - 1))
                    act.activation(gst[u % 2][:, :],
                                   (pA, pB, pO1, pO2)[u % 4][:, :], AF.Gelu,
                                   scale=sc1[:, 1:2])
                    act.activation(ab_f[u % 2][:, :], gst[u % 2][:, :],
                                   AF.Abs).then_inc(s_a, 2)

        @blk.vector
        def _(v):
            for hc in range(HC):
                if hc >= 1:
                    v.wait_ge(s_sp, gd8 + 16 * hc)
                for ts in range(8):
                    u = hc * 8 + ts
                    v.wait_ge(s_a, a8 + 2 * (u + 1))
                    cb = u * 16
                    v.max(q2cand[:, cb:cb + 8], ab_f[u % 2][:, 0:256])
                    v.max(q2cand[:, cb + 8:cb + 16], ab_f[u % 2][:, 256:512])
                    v.tensor_copy(gstage[:, ts * 512:(ts + 1) * 512],
                                  gst[u % 2][:, :]).then_inc(s_d, 1)

        @blk.sync
        def _(sync):
            for hc in range(HC):
                sync.wait_ge(s_d, d8 + (hc + 1) * 8)
                sync.dma_start(
                    out=gT_d[:, hc * 128:(hc + 1) * 128, :].rearrange("tt h t -> h tt t"),
                    in_=gstage[:, :].rearrange("p (tt t) -> p tt t", tt=NT)
                ).then_inc(s_sp, 16)
    C["p"] += 256
    C["a"] += 512
    C["d"] += 256
    C["sp"] += HC * 16
    free("ab_f1", "ab_f0", "gst1", "gst0", "gstage")

    if stop == 7:
        return finish_early()

    W4_sb = enter("W4_sb", [128, DC * D], BF16)
    w9, rl9, d9, sp9 = C["w"], C["rl"], C["d"], C["sp"]
    p9pe = C["p"]
    qt_sb = enter("qt_sb", [128, DC * 256], BF16)
    A_sb = enter("A_sb", [128, NT * 256], BF16)
    OT_sb = enter("OT_sb", [128, DC * 256], BF16)
    cb_sb = jk_f[:, 768:1024]
    crow = mk_f[0:1, 0:256]
    apst = enter("apst", [128, 512], BF16)
    with nc.Block() as blk:
        @blk.sync
        def _(sync):
            sync.wait_ge(s_sp, sp3 + 16 * 192 + 64 * 16)
            sync.wait_ge(s_p, p9pe)
            for ec in range(DC):
                sync.dma_start(out=KTs[:, ec * T:(ec + 1) * T],
                               in_=KT_d[ec * 128:(ec + 1) * 128, :]).then_inc(s_rl, 16)

        @blk.gpsimd
        def _(gp):
            gp.wait_ge(s_sp, sp9)
            for dc in range(DC):
                gp.dma_start(out=W4_sb[:, dc * D:(dc + 1) * D],
                             in_=W_d[3][dc * 128:(dc + 1) * 128, :]).then_inc(s_wg, 16)
    C["rl"] += DC * 16
    C["wg"] += DC * 16

    if stop == 8:
        return finish_early()

    PE_PC = 32 + 32 + 1 + DC + 4
    w10v = C["w"]
    ACT_PC = 32
    DVE_PC = 1 + 1 + DC + 4
    p10, a10, d10, rl10, sp10 = C["p"], C["a"], C["p"] * 0 + C["d"], C["rl"], C["sp"]

    def act_rounds_before(c):
        return max(0, min(c - 1, NR + 1))

    def _a_of_round(j):
        c = j + 1
        return a10 + (c + 1) * ACT_PC + act_rounds_before(c) + 1

    base2 = dict(a=a10, d=d10, bp=C["bp"], cc=C["cc"], bv=C["bv"],
                 a_of_round=_a_of_round)
    acts2, dves2, gp2, n2 = make_bisect("q2", q2cand, 4096, N2_CAND, R2, FRAC2,
                                        s_slot=2, gs_slot=3, base=base2, ro=12,
                                        nthr=3, nrnd=NR, frc=B_FRC3)
    with nc.Block() as blk:
        @blk.sync
        def _(sync):
            sync.wait_ge(s_rl, rl9 + DC * 16)
            for sc in range(SC):
                if sc >= 1:
                    sync.wait_ge(s_p, p10 + (sc - 1) * PE_PC + 32)
                sync.dma_start(out=qt_sb[:, :].rearrange("p (ec t) -> p ec t", ec=DC),
                               in_=QT_d[sc, :, :].rearrange("(ec p) t -> p ec t", p=128)
                               ).then_inc(s_rl, 16)
                if sc == 0:
                    for tt in range(NT):
                        sync.dma_start(out=Vs[:, tt * D:(tt + 1) * D],
                                       in_=V_d[tt * 128:(tt + 1) * 128, :]
                                       ).then_inc(s_w, 16)
                for g in range(4):
                    sync.wait_ge(s_d, d10 + sc * DVE_PC + 2 + DC + g + 1)
                    st, dh = g // 2, g % 2
                    sync.dma_start(
                        out=AP_d[sc * 256 + st * 128: sc * 256 + (st + 1) * 128,
                                 dh * 512:(dh + 1) * 512],
                        in_=apst[:, :]).then_inc(s_sp, 16)

        @blk.tensor
        def _(pe):
            pe.wait_ge(s_wg, C["wg"])
            for sc in range(SC):
                pe.wait_ge(s_rl, rl10 + 16 * (sc + 1))
                for tt in range(NT):
                    psum = (pA, pB)[tt % 2]
                    if tt < 2 and sc >= 1:
                        pe.wait_ge(s_d, d10 + sc * DVE_PC)
                    if tt >= 2:
                        pe.wait_ge(s_a, a10 + sc * ACT_PC + act_rounds_before(sc) + (tt - 1))
                    mm = None
                    for ec in range(DC):
                        mm = pe.matmul(psum[:, 0:256],
                                       KTs[:, ec * T + tt * 128:ec * T + (tt + 1) * 128],
                                       qt_sb[:, ec * 256:(ec + 1) * 256],
                                       start=(ec == 0), stop=False)
                    mm = pe.matmul(psum[:, 0:256], ones_rowb[:, :],
                                   m32[0:1, sc * 256:(sc + 1) * 256],
                                   start=False, stop=True)
                    mm.then_inc(s_p, 1)
                pe.wait_ge(s_a, a10 + (sc + 1) * ACT_PC + act_rounds_before(sc))
                for tt in range(NT):
                    pe.matmul(pR[0:1, 0:256], ones_col[:, :],
                              A_sb[:, tt * 256:(tt + 1) * 256],
                              start=(tt == 0), stop=(tt == NT - 1)).then_inc(s_p, 1)
                pe.wait_ge(s_d, d10 + sc * DVE_PC + 1)
                pe.matmul(pR[:, 256:512], ones_row[:, :], crow,
                          start=True, stop=True).then_inc(s_p, 1)
                for ec in range(DC):
                    psum = (pO1, pO2)[ec % 2]
                    if sc == 0 and ec == 0:
                        pe.wait_ge(s_w, w10v + NT * 16)
                    if ec >= 2:
                        pe.wait_ge(s_d, d10 + sc * DVE_PC + 2 + (ec - 1))
                    mm = None
                    for tt in range(NT):
                        mm = pe.matmul(psum[:, 0:256],
                                       Vs[:, tt * D + ec * 128:tt * D + (ec + 1) * 128],
                                       A_sb[:, tt * 256:(tt + 1) * 256],
                                       start=(tt == 0), stop=(tt == NT - 1))
                    mm.then_inc(s_p, 1)
                for g in range(4):
                    st, dh = g // 2, g % 2
                    psum = (pA, pB)[g % 2]
                    if g == 0:
                        pe.wait_ge(s_d, d10 + sc * DVE_PC + 2 + DC)
                    if g >= 2:
                        pe.wait_ge(s_d, d10 + sc * DVE_PC + 2 + DC + (g - 1))
                    mm = None
                    for ec in range(DC):
                        mm = pe.matmul(psum[:, :],
                                       OT_sb[:, ec * 256 + st * 128:ec * 256 + (st + 1) * 128],
                                       W4_sb[:, ec * D + dh * 512:ec * D + (dh + 1) * 512],
                                       start=(ec == 0), stop=(ec == DC - 1))
                    mm.then_inc(s_p, 1)

        @blk.scalar
        def _(act):
            for sc in range(SC):
                for tt in range(NT):
                    act.wait_ge(s_p, p10 + sc * PE_PC + tt + 1)
                    act.activation(A_sb[:, tt * 256:(tt + 1) * 256],
                                   (pA, pB)[tt % 2][:, 0:256], AF.Exp,
                                   scale=0.03125).then_inc(s_a, 1)
                if 1 <= sc <= NR + 1:
                    acts2[sc - 1](act)

        @blk.vector
        def _(v):
            dves2[0](v)
            dves2[1](v)
            di = 2
            for sc in range(SC):
                v.wait_ge(s_p, p10 + sc * PE_PC + 32 + 32)
                v.reciprocal(crow, pR[0:1, 0:256]).then_inc(s_d, 1)
                v.wait_ge(s_p, p10 + sc * PE_PC + 32 + 32 + 1)
                v.tensor_copy(cb_sb, pR[:, 256:512]).then_inc(s_d, 1)
                for ec in range(DC):
                    v.wait_ge(s_p, p10 + sc * PE_PC + 65 + ec + 1)
                    v.tensor_tensor(OT_sb[:, ec * 256:(ec + 1) * 256],
                                    (pO1, pO2)[ec % 2][:, 0:256], cb_sb,
                                    op=ALU.mult).then_inc(s_d, 1)
                for g in range(4):
                    v.wait_ge(s_p, p10 + sc * PE_PC + 65 + DC + g + 1)
                    if sc * 4 + g >= 1:
                        v.wait_ge(s_sp, sp10 + 16 * (sc * 4 + g))
                    st, dh = g // 2, g % 2
                    v.tensor_tensor(apst[:, :], (pA, pB)[g % 2][:, :],
                                    b4_bc[:, dh * 512:(dh + 1) * 512],
                                    op=ALU.add).then_inc(s_d, 1)
                if di < len(dves2) and sc <= NR:
                    dves2[di](v)
                    di += 1
            while di < len(dves2):
                dves2[di](v)
                di += 1

        @blk.gpsimd
        def _(gp):
            gp2(gp)
    C["p"] += SC * PE_PC
    C["w"] += NT * 16
    C["a"] += SC * ACT_PC + n2["a"]
    C["d"] += SC * DVE_PC
    C["bv"] += n2["bv"]
    C["rl"] += SC * 16
    C["sp"] += SC * 4 * 16
    C["bp"] += n2["bp"]
    C["cc"] += n2["cc"]
    C["g"] += n2["g"]
    free("apst", "OT_sb", "A_sb", "qt_sb", "W4_sb", "q2cand")

    if stop == 9:
        return finish_early()

    wstB = [enter(f"wstB{i}", [128, 1024], F32) for i in range(2)]
    w11, d11 = C["w"], C["d"]
    with nc.Block() as blk:
        @blk.sync
        def _(sync):
            for ch in range(HC):
                if ch >= 2:
                    sync.wait_ge(s_d, d11 + ch - 1)
                sync.dma_start(out=wstB[ch % 2][:, :],
                               in_=w2T_d[ch * 128:(ch + 1) * 128, :]).then_inc(s_w, 16)

        @blk.vector
        def _(v):
            for ch in range(HC):
                v.wait_ge(s_w, w11 + 16 * (ch + 1))
                v.tensor_scalar(jk_f[:, :], wstB[ch % 2][:, :], sc1[:, 5:6], None,
                                op0=ALU.mult)
                v.tensor_scalar(jk_f[:, :], jk_f[:, :], MAGIC, MAGIC,
                                op0=ALU.add, op1=ALU.subtract)
                v.tensor_scalar(w2q[:, ch * D:(ch + 1) * D], jk_f[:, :],
                                1.0, -1.0, op0=ALU.min, op1=ALU.max).then_inc(s_d, 1)
    C["w"] += HC * 16
    C["d"] += HC

    if stop == 10:
        return finish_early()

    free("wstB1", "wstB0")
    gload = enter("gload", [128, HID], BF16)
    gq = [enter(f"gq{i}", [128, HID], BF16) for i in range(2)]
    gs1_f = enter("gs1_f", [128, HID // 2], F32)
    xt2 = [enter(f"xt2{i}", [128, D], F32) for i in range(2)]
    apt = [enter(f"apt{i}", [128, D], BF16) for i in range(2)]
    outst = enter("outst", [128, D], F32)
    rl12, d12, p12, st12, sp12b = C["rl"], C["d"], C["p"], C["st"], C["sp"]
    wg12 = C["wg"]
    with nc.Block() as blk:
        @blk.sync
        def _(sync):
            sync.wait_ge(s_sp, sp12b)
            sync.wait_ge(s_sp, C["sp"])
            sync.wait_ge(s_d, d12)
            for tt in range(NT):
                if tt >= 1:
                    sync.wait_ge(s_d, d12 + (tt - 1) * 4 + 1)
                sync.dma_start(out=gload[:, :].rearrange("p (hc t) -> p hc t", hc=HC),
                               in_=gT_d[tt, :, :].rearrange("(hc p) t -> p hc t", p=128)
                               ).then_inc(s_rl, 16)
                if tt >= 2:
                    sync.wait_ge(s_d, d12 + (tt - 2) * 4 + 4)
                sync.dma_start(out=xt2[tt % 2][:, :],
                               in_=x_d[tt * 128:(tt + 1) * 128, :]).then_inc(s_rl, 16)
                sync.dma_start(out=apt[tt % 2][:, :],
                               in_=AP_d[tt * 128:(tt + 1) * 128, :]).then_inc(s_rl, 16)


        @blk.vector
        def _(v):
            for tt in range(NT):
                v.wait_ge(s_rl, rl12 + 16 * (3 * tt + 1))
                if tt >= 2:
                    v.wait_ge(s_p, p12 + 2 * (tt - 1))
                for hh in range(2):
                    sl = slice(hh * (HID // 2), (hh + 1) * (HID // 2))
                    v.tensor_scalar(gs1_f[:, :], gload[:, sl], sc1[:, 2:3], None,
                                    op0=ALU.mult)
                    v.tensor_scalar(gs1_f[:, :], gs1_f[:, :], MAGIC, MAGIC,
                                    op0=ALU.add, op1=ALU.subtract)
                    mm = v.tensor_scalar(gq[tt % 2][:, sl], gs1_f[:, :], 127.0, -128.0,
                                         op0=ALU.min, op1=ALU.max)
                mm.then_inc(s_d, 1)
                if tt >= 1:
                    v.wait_ge(s_st, st12 + 16 * tt)
                for dh in range(2):
                    v.wait_ge(s_p, p12 + tt * 2 + dh + 1)
                    psum = (pA, pB, pO1, pO2)[(tt * 2 + dh) % 4]
                    v.tensor_scalar(outst[:, dh * 512:(dh + 1) * 512],
                                    psum[:, :], sc1[:, 3:4], None, op0=ALU.mult).then_inc(s_d, 1)
                v.wait_ge(s_rl, rl12 + 16 * (3 * tt + 3))
                v.tensor_tensor(outst[:, :], outst[:, :],
                                xt2[tt % 2][:, :], op=ALU.add)
                v.tensor_tensor(outst[:, :], outst[:, :],
                                apt[tt % 2][:, :], op=ALU.add).then_inc(s_d, 1)

        @blk.tensor
        def _(pe):
            pe.wait_ge(s_d, d12)
            for tt in range(NT):
                pe.wait_ge(s_d, d12 + tt * 4 + 1)
                for dh in range(2):
                    psum = (pA, pB, pO1, pO2)[(tt * 2 + dh) % 4]
                    if tt >= 2:
                        pe.wait_ge(s_d, d12 + (tt - 2) * 4 + 2 + dh)
                    mm = None
                    for hc in range(HC):
                        mm = pe.matmul(psum[:, :],
                                       gq[tt % 2][:, hc * 128:(hc + 1) * 128],
                                       w2q[:, hc * D + dh * 512:hc * D + (dh + 1) * 512],
                                       start=(hc == 0), stop=(hc == HC - 1))
                    mm.then_inc(s_p, 1)

        @blk.scalar
        def _(act):
            for tt in range(NT):
                act.wait_ge(s_d, d12 + tt * 4 + 4)
                act.dma_start(out=out_d[tt * 128:(tt + 1) * 128, :],
                              in_=outst[:, :]).then_inc(s_st, 16)
            act.wait_ge(s_st, st12 + NT * 16)
    C["rl"] += NT * 3 * 16
    C["d"] += NT * 4
    C["p"] += NT * 2
    C["st"] += NT * 16

    es.close()
    return nc


_NC_CACHE = {}


def kernel(**inputs):
    x = np.ascontiguousarray(np.asarray(inputs["L0_attn_x"], dtype=np.float32))
    w1T = np.ascontiguousarray(np.asarray(inputs["L0_ffn_w1"], dtype=np.float32).T)
    w2T = np.ascontiguousarray(np.asarray(inputs["L0_ffn_w2"], dtype=np.float32).T)
    common = {
        "w1T": w1T,
        "w2T": w2T,
        "L0_ffn_ln_g": np.ascontiguousarray(np.asarray(inputs["L0_ffn_ln_g"], np.float32)),
        "L0_ffn_ln_b": np.ascontiguousarray(np.asarray(inputs["L0_ffn_ln_b"], np.float32)),
    }
    for i in (1, 2, 3, 4):
        common[f"L0_attn_W{i}"] = np.ascontiguousarray(
            np.asarray(inputs[f"L0_attn_W{i}"], np.float32))
        common[f"L0_attn_b{i}"] = np.ascontiguousarray(
            np.asarray(inputs[f"L0_attn_b{i}"], np.float32))

    if "nc" not in _NC_CACHE:
        _NC_CACHE["nc"] = build_program()
    nc = _NC_CACHE["nc"]

    in_maps = []
    for c in range(8):
        m = dict(common)
        m["L0_attn_x"] = np.ascontiguousarray(x[c])
        in_maps.append(m)
    res = run_bass_kernel_spmd(nc, in_maps, core_ids=list(range(8)))
    return np.stack([r["out"] for r in res.results], axis=0)



# version 16
# speedup vs baseline: 1.0193x; 1.0064x over previous
import math
import numpy as np

import concourse.bass as bass
import concourse.mybir as mybir
from concourse.bass_utils import run_bass_kernel_spmd

F32 = mybir.dt.float32
BF16 = mybir.dt.bfloat16
AF = mybir.ActivationFunctionType
ALU = mybir.AluOpType
AX = mybir.AxisListType.X
AXY = mybir.AxisListType.XY

T, D, HID = 4096, 1024, 4096
NT, DC, SC, HC = T // 128, D // 128, T // 256, HID // 128
MAGIC = 1.5 * 2.0 ** 23
INV1024 = 1.0 / 1024.0
NW = 1.0 / (HID * D)

N1_FULL, N2_FULL = 8 * T * D, 8 * T * HID
_p1 = 0.995 * (N1_FULL - 1)
I1 = math.floor(_p1)
FRAC1, R1 = _p1 - I1, N1_FULL - I1
_p2 = 0.995 * (N2_FULL - 1)
I2 = math.floor(_p2)
FRAC2, R2 = _p2 - I2, N2_FULL - I2
N1_CAND, N2_CAND = 8 * 128 * 1024, 8 * 128 * 4096
NR = 11
NR1 = 8
DBG_Q1 = False
RG = [list(range(8))]

B_LO, B_HI, B_W, B_NB, B_IDX = 0, 1, 2, 3, 4
B_T0, B_T1, B_CHI, B_AMAX, B_S, B_GAM, B_GS = 5, 6, 7, 8, 9, 10, 11
B_CNT, B_PRED, B_THR, B_NTH = 16, 24, 32, 40
B_FRC3, B_FRC7 = 48, 52


def build_program(stop=None):
    nc = bass.Bass("TRN2", target_bir_lowering=False)

    x_d = nc.dram_tensor("L0_attn_x", [T, D], F32, kind="ExternalInput").ap()
    W_d = [nc.dram_tensor(f"L0_attn_W{i}", [D, D], F32, kind="ExternalInput").ap()
           for i in (1, 2, 3, 4)]
    b_d = [nc.dram_tensor(f"L0_attn_b{i}", [D], F32, kind="ExternalInput").ap()
           for i in (1, 2, 3, 4)]
    lng_d = nc.dram_tensor("L0_ffn_ln_g", [D], F32, kind="ExternalInput").ap()
    lnb_d = nc.dram_tensor("L0_ffn_ln_b", [D], F32, kind="ExternalInput").ap()
    w1T_d = nc.dram_tensor("w1T", [D, HID], F32, kind="ExternalInput").ap()
    w2T_d = nc.dram_tensor("w2T", [HID, D], F32, kind="ExternalInput").ap()
    out_d = nc.dram_tensor("out", [T, D], F32, kind="ExternalOutput").ap()

    QT_d = nc.dram_tensor("QT_d", [SC, D, 256], BF16).ap()
    KT_d = nc.dram_tensor("KT_d", [D, T], BF16).ap()
    V_d = nc.dram_tensor("V_d", [T, D], BF16).ap()
    gT_d = nc.dram_tensor("gT_d", [NT, HID, 128], BF16).ap()
    AP_d = nc.dram_tensor("AP_d", [T, D], BF16).ap()
    cc_in_t = nc.dram_tensor("cc_in_d", [8192], F32)
    cc_out_t = nc.dram_tensor("cc_out_d", [8192], F32)
    row_t = nc.dram_tensor("row_d", [1, 64], F32, kind="ExternalOutput")
    dbg_t = nc.dram_tensor("dbg_d", [16, 32], F32, kind="ExternalOutput")
    dbg2_t = nc.dram_tensor("dbg2_d", [128, 1024], F32, kind="ExternalOutput")

    live = {}

    def enter(name, shape, dtype, kind="sbuf", side="left"):
        if kind == "sbuf":
            cm = nc.sbuf_tensor(name, shape, dtype, side=side)
        else:
            cm = nc.psum_tensor(name, shape, dtype)
        h = cm.__enter__()
        live[name] = cm
        return h

    def free(*names):
        for n in names:
            live.pop(n).__exit__(None, None, None)

    from contextlib import ExitStack
    es = ExitStack()

    def sem(name):
        return es.enter_context(nc.semaphore(name))

    ident = enter("ident", [128, 128], BF16, side="right")
    pidx = enter("pidx", [128, 1], F32, side="right")
    ones_col = enter("ones_col", [128, 1], BF16, side="right")
    ones_row = enter("ones_row", [1, 128], F32, side="right")
    ones_rowb = enter("ones_rowb", [1, 128], BF16, side="right")
    b_sb = enter("b_sb", [128, 4 * DC], F32, side="right")
    b4_bc = enter("b4_bc", [128, D], F32, side="right")

    bis = enter("bis", [1, 64], F32, side="right")
    bisb = enter("bisb", [128, 8], F32, side="right")
    srow = enter("srow", [1, 1024], F32, side="right")
    cstat = enter("cstat", [128, 64], F32, side="right")
    sc1 = enter("sc1", [128, 8], F32, side="right")
    jk_f = enter("jk_f", [128, D], F32, side="right")
    mk_f = enter("mk_f", [128, 256], F32, side="right")

    pT1 = enter("pT1", [128, 1024], BF16, "psum")
    pT2 = enter("pT2", [128, 1024], BF16, "psum")
    pA = enter("pA", [128, 512], F32, "psum")
    pB = enter("pB", [128, 512], F32, "psum")
    pR = enter("pR", [128, 512], F32, "psum")
    pO1 = enter("pO1", [128, 512], F32, "psum")
    pO2 = enter("pO2", [128, 512], F32, "psum")
    pJ = enter("pJ", [128, 512], F32, "psum")

    m32 = enter("m32", [1, T], BF16, side="right")
    big1 = enter("big1", [128, DC * T], BF16, side="right")
    big2 = enter("big2", [128, DC * T], BF16, side="right")
    xnT = xqT = KTs = w2q = big1
    ln2T = w1q = Vs = big2

    q1cand = enter("q1cand", [128, 1024], F32)
    identf = enter("identf", [128, 128], F32)
    iot = enter("iot", [128, 128], F32)
    xt = [enter(f"xt{i}", [128, D], F32) for i in range(2)]
    xn_f = [enter(f"xn_f{i}", [128, D], F32) for i in range(2)]
    l2_f = enter("l2_f", [128, D], F32)
    xnb = [enter(f"xnb{i}", [128, D], BF16) for i in range(2)]
    l2b = [enter(f"l2b{i}", [128, D], BF16) for i in range(2)]
    g_bc = enter("g_bc", [128, D], F32)
    bgb_bc = enter("bgb_bc", [128, D], F32)
    st_sum = enter("st_sum", [128, NT], F32)
    st_sq = enter("st_sq", [128, NT], F32)
    rstd = enter("rstd", [128, NT], F32)
    negmr = enter("negmr", [128, NT], F32)
    rs2 = enter("rs2", [128, NT], F32)
    stt1 = enter("stt1", [128, NT], F32)
    stt2 = enter("stt2", [128, NT], F32)

    s_x = sem("s_x")
    s_w = sem("s_w")
    s_a = sem("s_a")
    s_d = sem("s_d")
    s_p = sem("s_p")
    s_sp = sem("s_sp")
    s_rl = sem("s_rl")
    s_cc = sem("s_cc")
    s_bp = sem("s_bp")
    s_bv = sem("s_bv")
    s_g = sem("s_g")
    s_dbg = sem("s_dbg")
    s_st = sem("s_st")
    s_wg = sem("s_wg")

    C = dict(x=0, w=0, a=0, d=0, p=0, sp=0, rl=0, cc=0, bp=0, bv=0, g=0, st=0,
             wg=0)


    def finish_early():
        fin = enter("fin_sb", [128, 64], F32)
        with nc.Block() as blk:
            @blk.vector
            def _(v):
                v.memset(fin[:, :], 1.0).then_inc(s_d, 1)

            @blk.sync
            def _(sync):
                sync.wait_ge(s_d, C["d"] + 1)
                sync.dma_start(out=out_d[0:128, 0:64], in_=fin[:, :]).then_inc(s_st, 16)
                sync.wait_ge(s_st, C["st"] + 16)
        es.close()
        return nc

    def vsp(v):
        return v.tensor_scalar(jk_f[:, 512:576], jk_f[:, 512:576], 1.0, None,
                               op0=ALU.mult)

    def make_bisect(tag, cand, ncol, Ntot, rank, frac, s_slot, gs_slot, base,
                    ro=8, dbg=None, nthr=3, nrnd=None, frc=None):
        nch = ncol // 512
        if nrnd is None:
            nrnd = NR
        if frc is None:
            frc = B_FRC3
        inv = 1.0 / (nthr + 1)
        ncc = nthr * nch
        act_steps, dve_steps = [], []

        def dve_init(v):
            v.reduce_max(cstat[:, 0:1], cand[:, :], axis=AX)
            vsp(v).then_inc(s_bv, 1)
        dve_steps.append(dve_init)

        def dve_init2(v):
            v.wait_ge(s_bp, base["bp"] + 16 * 3)
            v.reduce_max(bis[0:1, B_HI:B_HI + 1], srow[0:1, 0:128], axis=AX)
            vsp(v)
            v.memset(bis[0:1, B_LO:B_LO + 1], 0.0)
            vsp(v).then_inc(s_bv, 1)
        dve_steps.append(dve_init2)

        def mk_dve_round(j):
            def f(v):
                if j > 0:
                    v.wait_ge(s_bp, base["bp"] + 16 * (3 + 4 * j))
                    for k in range(nthr):
                        v.tensor_reduce(
                            bis[0:1, B_CNT + k:B_CNT + k + 1],
                            srow[0:1, 0:128 * nthr].rearrange(
                                "q (p c) -> q p c", p=128)[:, :, k:k + 1],
                            axis=AXY, op=ALU.add)
                    if j <= nrnd:
                        vsp(v)
                        v.tensor_scalar(bis[0:1, B_CNT:B_CNT + nthr], bis[0:1, B_CNT:B_CNT + nthr],
                                        0.5, Ntot * 0.5, op0=ALU.mult, op1=ALU.add)
                        vsp(v)
                        v.tensor_scalar(bis[0:1, B_PRED:B_PRED + nthr], bis[0:1, B_CNT:B_CNT + nthr],
                                        float(rank) - 0.5, None, op0=ALU.is_lt)
                        vsp(v)
                        v.reduce_sum(bis[0:1, B_NB:B_NB + 1], bis[0:1, B_PRED:B_PRED + nthr], axis=AX)
                        vsp(v)
                        v.tensor_scalar(bis[0:1, B_IDX:B_IDX + 1], bis[0:1, B_NB:B_NB + 1],
                                        -1.0, float(nthr), op0=ALU.mult, op1=ALU.add)
                        v.tensor_tensor(bis[0:1, B_W:B_W + 1], bis[0:1, B_HI:B_HI + 1],
                                        bis[0:1, B_LO:B_LO + 1], op=ALU.subtract)
                        vsp(v)
                        v.tensor_tensor(bis[0:1, B_T0:B_T0 + 1], bis[0:1, B_W:B_W + 1],
                                        bis[0:1, B_IDX:B_IDX + 1], op=ALU.mult)
                        vsp(v)
                        v.tensor_scalar(bis[0:1, B_T0:B_T0 + 1], bis[0:1, B_T0:B_T0 + 1],
                                        inv, None, op0=ALU.mult)
                        vsp(v)
                        v.tensor_tensor(bis[0:1, B_LO:B_LO + 1], bis[0:1, B_LO:B_LO + 1],
                                        bis[0:1, B_T0:B_T0 + 1], op=ALU.add)
                        v.tensor_scalar(bis[0:1, B_T1:B_T1 + 1], bis[0:1, B_W:B_W + 1],
                                        inv, None, op0=ALU.mult)
                        vsp(v)
                        v.tensor_tensor(bis[0:1, B_HI:B_HI + 1], bis[0:1, B_LO:B_LO + 1],
                                        bis[0:1, B_T1:B_T1 + 1], op=ALU.add)
                        vsp(v)
                if j <= nrnd - 1:
                    v.tensor_tensor(bis[0:1, B_W:B_W + 1], bis[0:1, B_HI:B_HI + 1],
                                    bis[0:1, B_LO:B_LO + 1], op=ALU.subtract)
                    vsp(v)
                    v.tensor_scalar(bis[0:1, B_THR:B_THR + nthr], bis[0:1, frc:frc + nthr],
                                    bis[0:1, B_W:B_W + 1], None, op0=ALU.mult)
                    vsp(v)
                    v.tensor_scalar(bis[0:1, B_THR:B_THR + nthr], bis[0:1, B_THR:B_THR + nthr],
                                    bis[0:1, B_LO:B_LO + 1], None, op0=ALU.add)
                elif j == nrnd:
                    v.tensor_scalar(bis[0:1, B_THR:B_THR + nthr], bis[0:1, frc:frc + nthr],
                                    0.0, None, op0=ALU.mult)
                    vsp(v)
                    v.tensor_scalar(bis[0:1, B_THR:B_THR + nthr], bis[0:1, B_THR:B_THR + nthr],
                                    bis[0:1, B_HI:B_HI + 1], None, op0=ALU.add)
                vsp(v)
                v.tensor_scalar(bis[0:1, B_NTH:B_NTH + nthr], bis[0:1, B_THR:B_THR + nthr],
                                -1.0, None, op0=ALU.mult)
                vsp(v).then_inc(s_bv, 1)
            return f

        for j in range(nrnd + 1):
            dve_steps.append(mk_dve_round(j))

        def mk_act_round(j):
            def f(act):
                act.wait_ge(s_bp, base["bp"] + 16 * (3 + 4 * j + 2))
                mm = None
                for k in range(nthr):
                    for c in range(nch):
                        mm = act.activation(
                            pJ[:, :], cand[:, c * 512:(c + 1) * 512], AF.Sign,
                            bias=bisb[:, k:k + 1],
                            accum_out=cstat[:, j % 2 * 32 + k * nch + c:
                                            j % 2 * 32 + k * nch + c + 1])
                mm.then_inc(s_a, 1)
            return f

        for j in range(nrnd + 1):
            act_steps.append(mk_act_round(j))

        def dve_extract(v):
            v.wait_ge(s_bp, base["bp"] + 16 * (3 + 4 * (nrnd + 1)))
            v.tensor_reduce(
                bis[0:1, B_CHI:B_CHI + 1],
                srow[0:1, 0:128 * nthr].rearrange("q (p c) -> q p c", p=128)[:, :, 0:1],
                axis=AXY, op=ALU.add)
            vsp(v)
            v.tensor_scalar(bis[0:1, B_CHI:B_CHI + 1], bis[0:1, B_CHI:B_CHI + 1],
                            0.5, Ntot * 0.5, op0=ALU.mult, op1=ALU.add)
            v.tensor_scalar(bisb[:, 7:8], bisb[:, 0:1], -1.0, None, op0=ALU.mult)
            vsp(v)
            v.tensor_scalar(jk_f[:, 0:512], cand[:, 0:512], bisb[:, 7:8], -1e30,
                            op0=ALU.is_gt, op1=ALU.mult)
            v.tensor_tensor(jk_f[:, 0:512], jk_f[:, 0:512], cand[:, 0:512], op=ALU.add)
            v.reduce_max(cstat[:, 0:1], jk_f[:, 0:512], axis=AX)
            for c in range(1, nch):
                v.tensor_scalar(jk_f[:, 0:512], cand[:, c * 512:(c + 1) * 512],
                                bisb[:, 7:8], -1e30, op0=ALU.is_gt, op1=ALU.mult)
                v.tensor_tensor(jk_f[:, 0:512], jk_f[:, 0:512],
                                cand[:, c * 512:(c + 1) * 512], op=ALU.add)
                v.reduce_max(cstat[:, 1:2], jk_f[:, 0:512], axis=AX)
                vsp(v)
                v.tensor_tensor(cstat[:, 0:1], cstat[:, 0:1], cstat[:, 1:2], op=ALU.max)
            v.tensor_scalar(jk_f[:, 0:512], cand[:, 0:512], bisb[:, 7:8], 1e30,
                            op0=ALU.is_le, op1=ALU.mult)
            v.tensor_tensor(jk_f[:, 0:512], jk_f[:, 0:512], cand[:, 0:512], op=ALU.add)
            v.tensor_reduce(cstat[:, 2:3], jk_f[:, 0:512], axis=AX, op=ALU.min)
            for c in range(1, nch):
                v.tensor_scalar(jk_f[:, 0:512], cand[:, c * 512:(c + 1) * 512],
                                bisb[:, 7:8], 1e30, op0=ALU.is_le, op1=ALU.mult)
                v.tensor_tensor(jk_f[:, 0:512], jk_f[:, 0:512],
                                cand[:, c * 512:(c + 1) * 512], op=ALU.add)
                v.tensor_reduce(cstat[:, 3:4], jk_f[:, 0:512], axis=AX, op=ALU.min)
                vsp(v)
                v.tensor_tensor(cstat[:, 2:3], cstat[:, 2:3], cstat[:, 3:4],
                                op=ALU.min)
            vsp(v).then_inc(s_bv, 1)
        dve_steps.append(dve_extract)

        def dve_final(v):
            v.wait_ge(s_bp, base["bp"] + 16 * (3 + 4 * (nrnd + 1) + 4))
            v.tensor_reduce(bis[0:1, B_T0:B_T0 + 1],
                            srow[0:1, 0:256].rearrange("q (p c) -> q p c", p=128)[:, :, 0:1],
                            axis=AXY, op=ALU.max)
            v.tensor_reduce(bis[0:1, B_T1:B_T1 + 1],
                            srow[0:1, 256:512].rearrange("q (p c) -> q p c", p=128)[:, :, 0:1],
                            axis=AXY, op=ALU.min)
            v.tensor_scalar(bis[0:1, B_PRED:B_PRED + 1], bis[0:1, B_CHI:B_CHI + 1],
                            float(rank - 1), None, op0=ALU.is_equal)
            vsp(v)
            v.tensor_tensor(bis[0:1, B_T1:B_T1 + 1], bis[0:1, B_T1:B_T1 + 1],
                            bis[0:1, B_T0:B_T0 + 1], op=ALU.subtract)
            vsp(v)
            v.tensor_tensor(bis[0:1, B_T1:B_T1 + 1], bis[0:1, B_T1:B_T1 + 1],
                            bis[0:1, B_PRED:B_PRED + 1], op=ALU.mult)
            vsp(v)
            v.tensor_scalar(bis[0:1, B_T1:B_T1 + 1], bis[0:1, B_T1:B_T1 + 1],
                            float(frac), None, op0=ALU.mult)
            vsp(v)
            v.tensor_tensor(bis[0:1, B_AMAX:B_AMAX + 1], bis[0:1, B_T0:B_T0 + 1],
                            bis[0:1, B_T1:B_T1 + 1], op=ALU.add)
            vsp(v)
            v.tensor_scalar(bis[0:1, B_AMAX:B_AMAX + 1], bis[0:1, B_AMAX:B_AMAX + 1],
                            1e-5, None, op0=ALU.max)
            vsp(v)
            v.reciprocal(bis[0:1, B_S:B_S + 1], bis[0:1, B_AMAX:B_AMAX + 1])
            vsp(v)
            v.tensor_scalar(bis[0:1, B_S:B_S + 1], bis[0:1, B_S:B_S + 1],
                            127.0, None, op0=ALU.mult)
            v.tensor_tensor(bis[0:1, B_GS:B_GS + 1], bis[0:1, B_GAM:B_GAM + 1],
                            bis[0:1, B_AMAX:B_AMAX + 1], op=ALU.mult)
            vsp(v)
            v.tensor_scalar(bis[0:1, B_GS:B_GS + 1], bis[0:1, B_GS:B_GS + 1],
                            1.0 / 127.0, None, op0=ALU.mult)
            vsp(v).then_inc(s_bv, 1)
        dve_steps.append(dve_final)

        def emit_gpsimd(gp):
            bp = base["bp"]
            cc = base["cc"]
            bv = base["bv"]
            gp.wait_ge(s_bv, bv + 1)
            gp.dma_start(out=cc_in_t[0:128].rearrange("(p c) -> p c", p=128),
                         in_=cstat[:, 0:1]).then_inc(s_bp, 16)
            gp.wait_ge(s_bp, bp + 16)
            gp.collective_compute("AllReduce", ALU.max, replica_groups=RG,
                                  ins=[cc_in_t[0:128].opt()],
                                  outs=[cc_out_t[0:128].opt()]).then_inc(s_cc, 1)
            gp.wait_ge(s_cc, cc + 1)
            gp.dma_start(out=srow[0:1, 0:128],
                         in_=cc_out_t[0:128].rearrange("(q n) -> q n", q=1)).then_inc(s_bp, 16)
            gp.dma_start(out=row_t.ap()[0:1, 0:1], in_=bis[0:1, 0:1]).then_inc(s_bp, 16)
            for j in range(nrnd + 1):
                gp.wait_ge(s_bv, bv + 3 + j)
                if dbg is not None:
                    gp.dma_start(out=dbg[j:j + 1, :],
                                 in_=bis[0:1, 0:32]).then_inc(s_dbg, 16)
                gp.dma_start(out=row_t.ap()[0:1, 0:nthr],
                             in_=bis[0:1, B_NTH:B_NTH + nthr]).then_inc(s_bp, 16)
                gp.wait_ge(s_bp, bp + 16 * (3 + 4 * j + 1))
                gp.dma_start(out=bisb[:, 0:nthr],
                             in_=row_t.ap()[0:1, 0:nthr].partition_broadcast(128)
                             ).then_inc(s_bp, 16)
                gp.wait_ge(s_a, base["a_of_round"](j))
                b0 = j % 2 * 32
                roff = 1024 + 1024 * (j % 2)
                h = nch
                while h > 1:
                    h //= 2
                    for k in range(nthr):
                        gp.tensor_tensor(cstat[:, b0 + k * nch:b0 + k * nch + h],
                                         cstat[:, b0 + k * nch:b0 + k * nch + h],
                                         cstat[:, b0 + k * nch + h:b0 + k * nch + 2 * h],
                                         op=ALU.add)
                gp.dma_start(out=cc_in_t[roff:roff + 128 * nthr].rearrange(
                                 "(p k c) -> p k c", p=128, k=nthr),
                             in_=cstat[:, b0:b0 + nthr * nch].rearrange(
                                 "p (k c) -> p k c", k=nthr)[:, :, 0:1],
                             allow_slow_non_contiguous=True).then_inc(s_bp, 16)
                gp.wait_ge(s_bp, bp + 16 * (3 + 4 * j + 3))
                gp.collective_compute("AllReduce", ALU.add, replica_groups=RG,
                                      ins=[cc_in_t[roff:roff + 128 * nthr].opt()],
                                      outs=[cc_out_t[0:128 * nthr].opt()]).then_inc(s_cc, 1)
                gp.wait_ge(s_cc, cc + 2 + j)
                gp.dma_start(out=srow[0:1, 0:128 * nthr],
                             in_=cc_out_t[0:128 * nthr].rearrange("(q n) -> q n", q=1)
                             ).then_inc(s_bp, 16)
            gp.wait_ge(s_bv, bv + 2 + (nrnd + 1) + 1)
            if dbg is not None:
                gp.dma_start(out=dbg[nrnd + 1:nrnd + 2, :],
                             in_=bis[0:1, 0:32]).then_inc(s_dbg, 16)
                gp.dma_start(out=dbg2_t.ap()[:, 0:ncol],
                             in_=cand[:, :]).then_inc(s_dbg, 16)
            gp.dma_start(out=cc_in_t[3072:3328].rearrange("(p c) -> p c", p=128),
                         in_=cstat[:, 0:2]).then_inc(s_bp, 16)
            gp.dma_start(out=cc_in_t[3328:3584].rearrange("(p c) -> p c", p=128),
                         in_=cstat[:, 2:4]).then_inc(s_bp, 16)
            gp.wait_ge(s_bp, bp + 16 * (3 + 4 * (nrnd + 1) + 2))
            gp.collective_compute("AllReduce", ALU.max, replica_groups=RG,
                                  ins=[cc_in_t[3072:3328].opt()],
                                  outs=[cc_out_t[0:256].opt()]).then_inc(s_cc, 1)
            gp.wait_ge(s_cc, cc + 1 + (nrnd + 1) + 1)
            gp.collective_compute("AllReduce", ALU.min, replica_groups=RG,
                                  ins=[cc_in_t[3328:3584].opt()],
                                  outs=[cc_out_t[256:512].opt()]).then_inc(s_cc, 1)
            gp.wait_ge(s_cc, cc + 1 + (nrnd + 1) + 2)
            gp.dma_start(out=srow[0:1, 0:256],
                         in_=cc_out_t[0:256].rearrange("(q n) -> q n", q=1)).then_inc(s_bp, 16)
            gp.dma_start(out=srow[0:1, 256:512],
                         in_=cc_out_t[256:512].rearrange("(q n) -> q n", q=1)).then_inc(s_bp, 16)
            gp.wait_ge(s_bv, bv + 2 + (nrnd + 1) + 2)
            gp.dma_start(out=row_t.ap()[0:1, ro:ro + 4],
                         in_=bis[0:1, B_AMAX:B_AMAX + 4]).then_inc(s_bp, 16)
            gp.wait_ge(s_bp, bp + 16 * (3 + 4 * (nrnd + 1) + 5))
            gp.dma_start(out=bisb[:, 4:8],
                         in_=row_t.ap()[0:1, ro:ro + 4].partition_broadcast(128)
                         ).then_inc(s_bp, 16)
            gp.wait_ge(s_bp, bp + 16 * (3 + 4 * (nrnd + 1) + 6))
            if dbg is not None:
                gp.wait_ge(s_dbg, 16 * (nrnd + 3))
            gp.tensor_copy(sc1[:, s_slot:s_slot + 1], bisb[:, 5:6])
            gp.tensor_copy(sc1[:, gs_slot:gs_slot + 1], bisb[:, 7:8]).then_inc(s_g, 1)

        n_bp = 16 * (3 + 4 * (nrnd + 1) + 6)
        n_cc = 1 + (nrnd + 1) + 2
        n_bv = 2 + (nrnd + 1) + 2
        n_act = nrnd + 1
        n_g = 1
        return act_steps, dve_steps, emit_gpsimd, dict(bp=n_bp, cc=n_cc, bv=n_bv,
                                                       a=n_act, g=n_g)


    with nc.Block() as blk:
        @blk.sync
        def _(sync):
            for i, bd in enumerate(b_d):
                sync.dma_start(out=b_sb[:, i * DC:(i + 1) * DC],
                               in_=bd.rearrange("(c p) -> p c", p=128),
                               allow_slow_non_contiguous=True).then_inc(s_w, 16)
            sync.dma_start(out=g_bc[:, :], in_=lng_d.partition_broadcast(128)).then_inc(s_w, 16)
            sync.dma_start(out=bgb_bc[:, :], in_=lnb_d.partition_broadcast(128)).then_inc(s_w, 16)
            sync.dma_start(out=b4_bc[:, :], in_=b_d[3].partition_broadcast(128)).then_inc(s_w, 16)
            for i in range(NT):
                if i >= 2:
                    sync.wait_ge(s_a, 2 * (i - 1))
                sync.dma_start(out=xt[i % 2][:, :],
                               in_=x_d[i * 128:(i + 1) * 128, :]).then_inc(s_x, 16)

        @blk.gpsimd
        def _(gp):
            gp.iota(iot[:, :], [[1, 128]], channel_multiplier=0,
                    allow_small_or_imprecise_dtypes=True)
            gp.iota(pidx[:, :], [[0, 1]], channel_multiplier=1,
                    allow_small_or_imprecise_dtypes=True).then_inc(s_g, 1)

        @blk.scalar
        def _(act):
            for i in range(NT):
                act.wait_ge(s_x, 16 * (i + 1))
                act.activation(jk_f[:, :], xt[i % 2][:, :], AF.Identity,
                               accum_out=st_sum[:, i:i + 1])
                act.activation(jk_f[:, :], xt[i % 2][:, :], AF.Square,
                               accum_out=st_sq[:, i:i + 1]).then_inc(s_a, 2)

        @blk.vector
        def _(v):
            v.memset(ones_col[:, :], 1.0)
            v.memset(ones_row[:, :], 1.0)
            v.memset(ones_rowb[:, :], 1.0)
            v.memset(bis[:, :], 0.0)
            for k in range(3):
                v.memset(bis[0:1, B_FRC3 + k:B_FRC3 + k + 1], (k + 1) / 4.0)
            for k in range(7):
                v.memset(bis[0:1, B_FRC7 + k:B_FRC7 + k + 1], (k + 1) / 8.0)
            v.wait_ge(s_g, 1)
            v.tensor_scalar(identf[:, :], iot[:, :], pidx[:, :], None, op0=ALU.is_equal)
            v.tensor_copy(ident[:, :], identf[:, :]).then_inc(s_d, 1)
    C["x"] += NT * 16
    C["w"] += 7 * 16
    C["g"] += 1
    C["a"] += 2 * NT
    C["d"] += 1

    a0, d0 = C["a"], C["d"]
    with nc.Block() as blk:
        @blk.vector
        def _(v):
            v.wait_ge(s_a, a0)
            v.tensor_scalar(stt1[:, :], st_sum[:, :], INV1024, None, op0=ALU.mult)
            vsp(v)
            v.tensor_tensor(stt2[:, :], stt1[:, :], stt1[:, :], op=ALU.mult)
            v.tensor_scalar(rstd[:, :], st_sq[:, :], INV1024, 1e-5, op0=ALU.mult, op1=ALU.add)
            vsp(v)
            v.tensor_tensor(rstd[:, :], rstd[:, :], stt2[:, :], op=ALU.subtract)
            vsp(v).then_inc(s_d, 1)

        @blk.scalar
        def _(act):
            act.wait_ge(s_d, d0 + 1)
            act.activation(negmr[:, :], rstd[:, :], AF.Sqrt).then_inc(s_a, 1)

        @blk.vector
        def _(v):
            v.wait_ge(s_a, a0 + 1)
            v.reciprocal(rs2[:, :], negmr[:, :])
            vsp(v)
            v.tensor_tensor(stt2[:, :], rs2[:, :], rs2[:, :], op=ALU.mult)
            v.tensor_scalar(rstd[:, :], rstd[:, :], -1e-5, None, op0=ALU.add)
            vsp(v)
            v.tensor_tensor(stt2[:, :], stt2[:, :], rstd[:, :], op=ALU.mult)
            vsp(v)
            v.tensor_scalar(stt2[:, :], stt2[:, :], 1e-5, None, op0=ALU.add)
            v.tensor_scalar(stt1[:, :], stt1[:, :], -1.0, None, op0=ALU.mult)
            vsp(v)
            v.tensor_tensor(negmr[:, :], stt1[:, :], rs2[:, :], op=ALU.mult)
            v.tensor_copy(rstd[:, :], rs2[:, :])
            vsp(v).then_inc(s_d, 1)

        @blk.scalar
        def _(act):
            act.wait_ge(s_d, d0 + 2)
            act.activation(stt2[:, :], stt2[:, :], AF.Sqrt).then_inc(s_a, 1)

        @blk.vector
        def _(v):
            v.wait_ge(s_a, a0 + 2)
            v.reciprocal(rs2[:, :], stt2[:, :])
            vsp(v).then_inc(s_d, 1)
    C["a"] += 2
    C["d"] += 3

    x1, a1, d1, p1 = C["x"], C["a"], C["d"], C["p"]
    with nc.Block() as blk:
        @blk.sync
        def _(sync):
            for i in range(NT):
                if i >= 2:
                    sync.wait_ge(s_a, a1 + (i - 1))
                sync.dma_start(out=xt[i % 2][:, :],
                               in_=x_d[i * 128:(i + 1) * 128, :]).then_inc(s_x, 16)

        @blk.scalar
        def _(act):
            for i in range(NT):
                act.wait_ge(s_x, x1 + 16 * (i + 1))
                if i >= 2:
                    act.wait_ge(s_d, d1 + (i - 1))
                    act.wait_ge(s_p, p1 + (i - 1))
                act.activation(xnb[i % 2][:, :], xt[i % 2][:, :], AF.Identity,
                               bias=negmr[:, i:i + 1],
                               scale=rstd[:, i:i + 1]).then_inc(s_a, 1)

        @blk.tensor
        def _(pe):
            for i in range(NT):
                pe.wait_ge(s_d, d1 + i + 1)
                mm = None
                for dc in range(DC):
                    pe.transpose(pT1[:, dc * 128:(dc + 1) * 128],
                                 xnb[i % 2][:, dc * 128:(dc + 1) * 128], ident[:, :])
                for dc in range(DC):
                    mm = pe.transpose(pT2[:, dc * 128:(dc + 1) * 128],
                                      l2b[i % 2][:, dc * 128:(dc + 1) * 128], ident[:, :])
                mm.then_inc(s_p, 1)

        @blk.vector
        def _(v):
            for i in range(NT):
                v.wait_ge(s_a, a1 + (i + 1))
                v.scalar_tensor_tensor(l2b[i % 2][:, :], xnb[i % 2][:, :], rs2[:, i:i + 1],
                                       g_bc[:, :], op0=ALU.mult, op1=ALU.mult)
                v.tensor_tensor(l2b[i % 2][:, :], l2b[i % 2][:, :], bgb_bc[:, :],
                                op=ALU.add).then_inc(s_d, 1)
                v.scalar_tensor_tensor(jk_f[:, :], l2b[i % 2][:, :], -1.0, l2b[i % 2][:, :],
                                       op0=ALU.mult, op1=ALU.max)
                for ch in range(4):
                    v.max(q1cand[:, i * 32 + ch * 8:i * 32 + ch * 8 + 8],
                          jk_f[:, ch * 256:(ch + 1) * 256])
                v.wait_ge(s_p, p1 + i + 1)
                v.tensor_copy(
                    xnT.ap().rearrange("p (dc t) -> p dc t", dc=DC)[:, :, i * 128:(i + 1) * 128],
                    pT1[:, :].rearrange("p (dc t) -> p dc t", dc=DC))
                v.tensor_copy(
                    ln2T.ap().rearrange("p (dc t) -> p dc t", dc=DC)[:, :, i * 128:(i + 1) * 128],
                    pT2[:, :].rearrange("p (dc t) -> p dc t", dc=DC))
    C["x"] += NT * 16
    C["a"] += NT
    C["p"] += NT
    C["d"] += NT

    if stop == 1:
        return finish_early()

    for nm in ("stt2", "stt1", "rs2", "negmr", "rstd", "st_sq", "st_sum",
               "bgb_bc", "g_bc", "l2b1", "l2b0", "xnb1", "xnb0", "l2_f",
               "xn_f1", "xn_f0", "xt1", "xt0", "iot", "identf"):
        free(nm)
    qkvst = [enter(f"qkvst{i}", [128, 512], BF16) for i in range(2)]
    Wq_sb = enter("Wq_sb", [128, DC * D], BF16)
    Wk_sb = enter("Wk_sb", [128, DC * D], BF16)
    with nc.Block() as blk:
        @blk.gpsimd
        def _(gp):
            for wi, Wsb in enumerate((Wq_sb, Wk_sb)):
                for dc in range(DC):
                    gp.dma_start(out=Wsb[:, dc * D:(dc + 1) * D],
                                 in_=W_d[wi][dc * 128:(dc + 1) * 128, :]).then_inc(s_wg, 16)
    C["wg"] += 16 * 16

    d3, a3, p3, sp3, w3qk = C["d"], C["a"], C["p"], C["sp"], C["wg"]

    def qk_spills(u):
        if u < 0:
            return 0
        q = min(u + 1, 64)
        k = max(0, u + 1 - 64)
        return 2 * q + k

    with nc.Block() as blk:
        @blk.tensor
        def _(pe):
            pe.wait_ge(s_wg, w3qk)
            for proj in range(2):
                Wsb = (Wq_sb, Wk_sb)[proj]
                for ts in range(8):
                    for ec in range(DC):
                        u = (proj * 8 + ts) * DC + ec
                        psum = (pA, pB, pO1, pO2)[u % 4]
                        if u >= 4:
                            pe.wait_ge(s_a, a3 + (u - 3))
                        mm = None
                        for dc in range(DC):
                            mm = pe.matmul(psum[:, :],
                                           Wsb[:, dc * D + ec * 128:dc * D + (ec + 1) * 128],
                                           xnT[:, dc * T + ts * 512:dc * T + (ts + 1) * 512],
                                           start=(dc == 0), stop=(dc == DC - 1))
                        mm.then_inc(s_p, 1)

        @blk.scalar
        def _(act):
            for proj in range(2):
                for ts in range(8):
                    for ec in range(DC):
                        u = (proj * 8 + ts) * DC + ec
                        act.wait_ge(s_p, p3 + u + 1)
                        if u >= 2:
                            act.wait_ge(s_sp, sp3 + 16 * qk_spills(u - 2))
                        act.activation(qkvst[u % 2][:, :],
                                       (pA, pB, pO1, pO2)[u % 4][:, :], AF.Identity,
                                       bias=b_sb[:, proj * DC + ec:proj * DC + ec + 1]
                                       ).then_inc(s_a, 1)

        @blk.sync
        def _(sync):
            for proj in range(2):
                for ts in range(8):
                    for ec in range(DC):
                        u = (proj * 8 + ts) * DC + ec
                        sync.wait_ge(s_a, a3 + u + 1)
                        st = qkvst[u % 2]
                        if proj == 0:
                            for h in range(2):
                                sc = ts * 2 + h
                                sync.dma_start(
                                    out=QT_d[sc, ec * 128:(ec + 1) * 128, :],
                                    in_=st[:, h * 256:(h + 1) * 256]).then_inc(s_sp, 16)
                        else:
                            sync.dma_start(
                                out=KT_d[ec * 128:(ec + 1) * 128, ts * 512:(ts + 1) * 512],
                                in_=st[:, :]).then_inc(s_sp, 16)
    C["p"] += 128
    C["a"] += 128
    C["sp"] += 192 * 16

    if stop == 2:
        return finish_early()

    free("Wk_sb", "Wq_sb")
    Wv_sb = enter("Wv_sb", [128, DC * D], BF16)
    b3h = enter("b3h", [128, D], F32)
    d4, a4, p4, sp4, w4c = C["d"], C["a"], C["p"], C["sp"], C["wg"]
    with nc.Block() as blk:
        @blk.gpsimd
        def _(gp):
            gp.dma_start(out=b3h[:, :], in_=b_d[2].partition_broadcast(128)).then_inc(s_wg, 16)
            for dc in range(DC):
                gp.dma_start(out=Wv_sb[:, dc * D:(dc + 1) * D],
                             in_=W_d[2][dc * 128:(dc + 1) * 128, :]).then_inc(s_wg, 16)

        @blk.tensor
        def _(pe):
            pe.wait_ge(s_a, a4)
            pe.wait_ge(s_wg, w4c + 16 * (DC + 1))
            for tt in range(NT):
                for eh in range(2):
                    u = tt * 2 + eh
                    psum = (pA, pB, pO1, pO2)[u % 4]
                    if u >= 4:
                        pe.wait_ge(s_d, d4 + (u - 3))
                    mm = None
                    for dc in range(DC):
                        mm = pe.matmul(psum[:, :],
                                       xnT[:, dc * T + tt * 128:dc * T + (tt + 1) * 128],
                                       Wv_sb[:, dc * D + eh * 512:dc * D + (eh + 1) * 512],
                                       start=(dc == 0), stop=(dc == DC - 1))
                    mm.then_inc(s_p, 1)

        @blk.vector
        def _(v):
            v.wait_ge(s_wg, w4c + 16)
            for tt in range(NT):
                for eh in range(2):
                    u = tt * 2 + eh
                    v.wait_ge(s_p, p4 + u + 1)
                    if u < 2:
                        v.wait_ge(s_sp, sp4)
                    else:
                        v.wait_ge(s_sp, sp4 + 16 * (u - 1))
                    v.tensor_tensor(qkvst[u % 2][:, :], (pA, pB, pO1, pO2)[u % 4][:, :],
                                    b3h[:, eh * 512:(eh + 1) * 512], op=ALU.add).then_inc(s_d, 1)

        @blk.sync
        def _(sync):
            for tt in range(NT):
                for eh in range(2):
                    u = tt * 2 + eh
                    sync.wait_ge(s_d, d4 + u + 1)
                    sync.dma_start(out=V_d[tt * 128:(tt + 1) * 128, eh * 512:(eh + 1) * 512],
                                   in_=qkvst[u % 2][:, :]).then_inc(s_sp, 16)
    C["wg"] += 16 + DC * 16
    C["p"] += 64
    C["d"] += 64
    C["sp"] += 64 * 16

    if stop == 3:
        return finish_early()

    free("b3h", "Wv_sb", "qkvst1", "qkvst0")
    wstA = enter("wstA", [128, 1024], F32)
    wstA2 = enter("wstA2", [128, 1024], F32)
    qk_rl = enter("qk_rl", [128, T], BF16)
    nsq = enter("nsq", [128, T], BF16)
    normQ = enter("normQ", [1, T], BF16)
    normK = enter("normK", [1, T], BF16)
    rl5, a5, p5, d5, w5, bp5, g5 = (C["rl"], C["a"], C["p"], C["d"], C["w"],
                                    C["bp"], C["g"])
    with nc.Block() as blk:
        @blk.sync
        def _(sync):
            sync.wait_ge(s_sp, sp3 + 16 * 192)
            for proj in range(2):
                for ec in range(DC):
                    k = proj * DC + ec
                    if k >= 1:
                        sync.wait_ge(s_a, a5 + k)
                    if proj == 0:
                        for sc in range(SC):
                            sync.dma_start(out=qk_rl[:, sc * 256:(sc + 1) * 256],
                                           in_=QT_d[sc, ec * 128:(ec + 1) * 128, :]
                                           ).then_inc(s_rl, 16)
                    else:
                        sync.dma_start(out=qk_rl[:, :],
                                       in_=KT_d[ec * 128:(ec + 1) * 128, :]).then_inc(s_rl, 16)
            for pc in range(32):
                if pc >= 2:
                    sync.wait_ge(s_a, a5 + 16 + pc - 1)
                sync.dma_start(out=(wstA, wstA2)[pc % 2][:, :],
                               in_=w1T_d[(pc // 4) * 128:(pc // 4 + 1) * 128,
                                         (pc % 4) * 1024:(pc % 4 + 1) * 1024]
                               ).then_inc(s_w, 16)

        @blk.scalar
        def _(act):
            nds = 0
            for proj in range(2):
                for ec in range(DC):
                    k = proj * DC + ec
                    nds += SC if proj == 0 else 1
                    act.wait_ge(s_rl, rl5 + 16 * nds)
                    if k >= 1:
                        act.wait_ge(s_d, d5 + 8 * k)
                    act.activation(nsq[:, :], qk_rl[:, :], AF.Square).then_inc(s_a, 1)
            for pc in range(32):
                act.wait_ge(s_w, w5 + 16 * (pc + 1))
                act.activation(jk_f[:, 0:1024], (wstA, wstA2)[pc % 2][:, :], AF.Abs,
                               accum_out=cstat[:, 32 + pc:33 + pc]).then_inc(s_a, 1)

        @blk.tensor
        def _(pe):
            for k in range(16):
                pe.wait_ge(s_a, a5 + k + 1)
                for ts in range(8):
                    j = k * 8 + ts
                    if j >= 4:
                        pe.wait_ge(s_d, d5 + j - 3)
                    pe.matmul((pR, pO1, pA, pB)[j % 4][0:1, 0:512], ones_col[:, :],
                              nsq[:, ts * 512:(ts + 1) * 512],
                              start=True, stop=True).then_inc(s_p, 1)

        @blk.vector
        def _(v):
            v.memset(normQ[:, :], 0.0)
            v.memset(normK[:, :], 0.0)
            for k in range(16):
                row = normQ if k < 8 else normK
                for ts in range(8):
                    j = k * 8 + ts
                    v.wait_ge(s_p, p5 + j + 1)
                    v.tensor_tensor(row[0:1, ts * 512:(ts + 1) * 512],
                                    row[0:1, ts * 512:(ts + 1) * 512],
                                    (pR, pO1, pA, pB)[j % 4][0:1, 0:512],
                                    op=ALU.add).then_inc(s_d, 1)
            v.wait_ge(s_a, a5 + 16 + 32)
            v.reduce_sum(cstat[:, 0:1], cstat[:, 32:64], axis=AX)
            vsp(v).then_inc(s_d, 1)

        @blk.gpsimd
        def _(gp):
            gp.wait_ge(s_d, d5 + 129)
            gp.dma_start(out=cc_in_t[0:128].rearrange("(p c) -> p c", p=128),
                         in_=cstat[:, 0:1]).then_inc(s_bp, 16)
            gp.wait_ge(s_bp, bp5 + 16)
            gp.dma_start(out=srow[0:1, 0:128],
                         in_=cc_in_t[0:128].rearrange("(q n) -> q n", q=1)).then_inc(s_bp, 16)
            gp.wait_ge(s_bp, bp5 + 32)
            gp.engine_nop().then_inc(s_g, 1)

        @blk.vector
        def _(v):
            v.wait_ge(s_g, g5 + 1)
            v.reduce_sum(bis[0:1, B_GAM:B_GAM + 1], srow[0:1, 0:128], axis=AX)
            vsp(v)
            v.tensor_scalar(bis[0:1, B_GAM:B_GAM + 1], bis[0:1, B_GAM:B_GAM + 1],
                            NW, 1e-5, op0=ALU.mult, op1=ALU.add)
            vsp(v)
            v.reciprocal(bis[0:1, B_T0:B_T0 + 1], bis[0:1, B_GAM:B_GAM + 1])
            vsp(v).then_inc(s_d, 1)

        @blk.gpsimd
        def _(gp):
            gp.wait_ge(s_d, d5 + 130)
            gp.dma_start(out=row_t.ap()[0:1, 4:5],
                         in_=bis[0:1, B_T0:B_T0 + 1]).then_inc(s_bp, 16)
            gp.wait_ge(s_bp, bp5 + 48)
            gp.dma_start(out=sc1[:, 4:5],
                         in_=row_t.ap()[0:1, 4:5].partition_broadcast(128)).then_inc(s_bp, 16)
            gp.wait_ge(s_bp, bp5 + 64)
            gp.engine_nop().then_inc(s_g, 1)
    C["rl"] += (8 * SC + 8) * 16
    C["w"] += 32 * 16
    C["a"] += 16 + 32
    C["p"] += 128
    C["d"] += 128 + 1 + 1
    C["bp"] += 64
    C["g"] += 2

    a6i, d6i = C["a"], C["d"]
    with nc.Block() as blk:
        @blk.scalar
        def _(act):
            act.wait_ge(s_d, d6i - 2)
            act.activation(normQ[0:1, :], normQ[0:1, :], AF.Sqrt).then_inc(s_a, 1)

        @blk.vector
        def _(v):
            v.wait_ge(s_a, a6i + 1)
            v.reduce_max(bis[0:1, B_T1:B_T1 + 1], normK[0:1, :], axis=AX).then_inc(s_d, 1)

        @blk.scalar
        def _(act):
            act.wait_ge(s_d, d6i + 1)
            act.activation(bis[0:1, B_T1:B_T1 + 1], bis[0:1, B_T1:B_T1 + 1],
                           AF.Sqrt).then_inc(s_a, 1)

        @blk.vector
        def _(v):
            v.wait_ge(s_a, a6i + 2)
            v.tensor_scalar(m32[0:1, :], normQ[0:1, :], bis[0:1, B_T1:B_T1 + 1],
                            -1.0, op0=ALU.mult, op1=ALU.mult).then_inc(s_d, 1)
    C["a"] += 2
    C["d"] += 2
    free("normK", "normQ")

    if stop == 4:
        return finish_early()

    base1 = dict(a=C["a"], d=C["d"], bp=C["bp"], cc=C["cc"], bv=C["bv"])
    base1["a_of_round"] = lambda j, _a=C["a"]: _a + j + 1
    acts1, dves1, gp1, n1 = make_bisect("q1", q1cand, 1024, N1_CAND, R1, FRAC1,
                                        s_slot=0, gs_slot=1, base=base1, ro=8,
                                        dbg=dbg_t.ap() if DBG_Q1 else None,
                                        nthr=7, nrnd=NR1, frc=B_FRC7)
    with nc.Block() as blk:
        @blk.vector
        def _(v):
            for f in dves1:
                f(v)

        @blk.scalar
        def _(act):
            for f in acts1:
                f(act)

        @blk.gpsimd
        def _(gp):
            gp1(gp)
    C["a"] += n1["a"]
    C["bv"] += n1["bv"]
    C["bp"] += n1["bp"]
    C["cc"] += n1["cc"]
    C["g"] += n1["g"]

    if stop == 5:
        return finish_early()

    w7g, a7g, d7g, bp7g, g7g = C["w"], C["a"], C["d"], C["bp"], C["g"]
    with nc.Block() as blk:
        @blk.sync
        def _(sync):
            for ch in range(32):
                if ch >= 2:
                    sync.wait_ge(s_a, a7g + ch - 1)
                sync.dma_start(out=(wstA, wstA2)[ch % 2][:, :],
                               in_=w2T_d[ch * 128:(ch + 1) * 128, :]).then_inc(s_w, 16)

        @blk.scalar
        def _(act):
            for ch in range(32):
                act.wait_ge(s_w, w7g + 16 * (ch + 1))
                act.activation(jk_f[:, 0:1024], (wstA, wstA2)[ch % 2][:, :], AF.Abs,
                               accum_out=cstat[:, 32 + ch:33 + ch]).then_inc(s_a, 1)

        @blk.vector
        def _(v):
            v.wait_ge(s_a, a7g + 32)
            v.reduce_sum(cstat[:, 0:1], cstat[:, 32:64], axis=AX)
            vsp(v).then_inc(s_d, 1)

        @blk.gpsimd
        def _(gp):
            gp.wait_ge(s_d, d7g + 1)
            gp.dma_start(out=cc_in_t[0:128].rearrange("(p c) -> p c", p=128),
                         in_=cstat[:, 0:1]).then_inc(s_bp, 16)
            gp.wait_ge(s_bp, bp7g + 16)
            gp.dma_start(out=srow[0:1, 256:384],
                         in_=cc_in_t[0:128].rearrange("(q n) -> q n", q=1)).then_inc(s_bp, 16)
            gp.wait_ge(s_bp, bp7g + 32)
            gp.engine_nop().then_inc(s_g, 1)

        @blk.vector
        def _(v):
            v.wait_ge(s_g, g7g + 1)
            v.reduce_sum(bis[0:1, B_GAM:B_GAM + 1], srow[0:1, 256:384], axis=AX)
            vsp(v)
            v.tensor_scalar(bis[0:1, B_GAM:B_GAM + 1], bis[0:1, B_GAM:B_GAM + 1],
                            NW, 1e-5, op0=ALU.mult, op1=ALU.add)
            vsp(v)
            v.reciprocal(bis[0:1, B_T1:B_T1 + 1], bis[0:1, B_GAM:B_GAM + 1])
            vsp(v).then_inc(s_d, 1)

        @blk.gpsimd
        def _(gp):
            gp.wait_ge(s_d, d7g + 2)
            gp.dma_start(out=row_t.ap()[0:1, 5:6],
                         in_=bis[0:1, B_T1:B_T1 + 1]).then_inc(s_bp, 16)
            gp.wait_ge(s_bp, bp7g + 48)
            gp.dma_start(out=sc1[:, 5:6],
                         in_=row_t.ap()[0:1, 5:6].partition_broadcast(128)).then_inc(s_bp, 16)
            gp.wait_ge(s_bp, bp7g + 64)
            gp.engine_nop().then_inc(s_g, 1)
    C["w"] += 32 * 16
    C["a"] += 32
    C["d"] += 2
    C["bp"] += 64
    C["g"] += 2

    d7, g7b, w7 = C["d"], C["g"], C["w"]
    wst7 = [wstA, wstA2]
    with nc.Block() as blk:
        @blk.vector
        def _(v):
            v.wait_ge(s_g, g7b)
            for ts in range(8):
                for dc in range(DC):
                    k = dc * 8 + ts
                    sl_in = ln2T[:, k * 512:(k + 1) * 512]
                    sl_out = xqT[:, k * 512:(k + 1) * 512]
                    v.tensor_scalar(jk_f[:, 0:512], sl_in, sc1[:, 0:1], None, op0=ALU.mult)
                    v.tensor_scalar(jk_f[:, 0:512], jk_f[:, 0:512], MAGIC, MAGIC,
                                    op0=ALU.add, op1=ALU.subtract)
                    mm = v.tensor_scalar(sl_out, jk_f[:, 0:512], 127.0, -128.0,
                                         op0=ALU.min, op1=ALU.max)
                mm.then_inc(s_d, 1)

        @blk.sync
        def _(sync):
            sync.wait_ge(s_g, g7b)
            for pq in range(32):
                q, dc = pq // 8, pq % 8
                if pq >= 2:
                    sync.wait_ge(s_d, d7 + 8 + pq - 1)
                sync.dma_start(out=wst7[pq % 2][:, :],
                               in_=w1T_d[dc * 128:(dc + 1) * 128,
                                         q * 1024:(q + 1) * 1024]
                               ).then_inc(s_w, 16)

        @blk.vector
        def _(v):
            for pq in range(32):
                q, dc = pq // 8, pq % 8
                v.wait_ge(s_w, w7 + 16 * (pq + 1))
                v.tensor_scalar(jk_f[:, :], wst7[pq % 2][:, :], sc1[:, 4:5], None,
                                op0=ALU.mult)
                v.tensor_scalar(jk_f[:, :], jk_f[:, :], MAGIC, MAGIC,
                                op0=ALU.add, op1=ALU.subtract)
                v.tensor_scalar(w1q[:, dc * HID + q * 1024:dc * HID + (q + 1) * 1024],
                                jk_f[:, :], 1.0, -1.0, op0=ALU.min,
                                op1=ALU.max).then_inc(s_d, 1)
    C["d"] += 8 + 32
    C["w"] += 32 * 16
    free("nsq", "qk_rl", "wstA2", "wstA", "q1cand")
    q2cand = enter("q2cand", [128, 4096], F32)

    if stop == 6:
        return finish_early()

    gstage = enter("gstage", [128, HID], BF16)
    gst = [enter(f"gst{i}", [128, 512], F32) for i in range(2)]
    ab_f = [enter(f"ab_f{i}", [128, 512], F32) for i in range(2)]
    p8, a8, d8, gd8 = C["p"], C["a"], C["d"], C["sp"]
    with nc.Block() as blk:
        @blk.tensor
        def _(pe):
            for hc in range(HC):
                for ts in range(8):
                    u = hc * 8 + ts
                    if ts == 0:
                        pe.wait_ge(s_d, d8 - 32 + (hc // 8 + 1) * 8)
                    psum = (pA, pB, pO1, pO2)[u % 4]
                    if u >= 4:
                        pe.wait_ge(s_a, a8 + 2 * (u - 3))
                    mm = None
                    for dc in range(DC):
                        mm = pe.matmul(psum[:, :],
                                       w1q[:, dc * T + hc * 128:dc * T + (hc + 1) * 128],
                                       xqT[:, dc * T + ts * 512:dc * T + (ts + 1) * 512],
                                       start=(dc == 0), stop=(dc == DC - 1))
                    mm.then_inc(s_p, 1)

        @blk.scalar
        def _(act):
            for hc in range(HC):
                for ts in range(8):
                    u = hc * 8 + ts
                    act.wait_ge(s_p, p8 + u + 1)
                    if u >= 2:
                        act.wait_ge(s_d, d8 + (u - 1))
                    act.activation(gst[u % 2][:, :],
                                   (pA, pB, pO1, pO2)[u % 4][:, :], AF.Gelu,
                                   scale=sc1[:, 1:2])
                    act.activation(ab_f[u % 2][:, :], gst[u % 2][:, :],
                                   AF.Abs).then_inc(s_a, 2)

        @blk.vector
        def _(v):
            for hc in range(HC):
                if hc >= 1:
                    v.wait_ge(s_sp, gd8 + 16 * hc)
                for ts in range(8):
                    u = hc * 8 + ts
                    v.wait_ge(s_a, a8 + 2 * (u + 1))
                    cb = u * 16
                    v.max(q2cand[:, cb:cb + 8], ab_f[u % 2][:, 0:256])
                    v.max(q2cand[:, cb + 8:cb + 16], ab_f[u % 2][:, 256:512])
                    v.tensor_copy(gstage[:, ts * 512:(ts + 1) * 512],
                                  gst[u % 2][:, :]).then_inc(s_d, 1)

        @blk.sync
        def _(sync):
            for hc in range(HC):
                sync.wait_ge(s_d, d8 + (hc + 1) * 8)
                sync.dma_start(
                    out=gT_d[:, hc * 128:(hc + 1) * 128, :].rearrange("tt h t -> h tt t"),
                    in_=gstage[:, :].rearrange("p (tt t) -> p tt t", tt=NT)
                ).then_inc(s_sp, 16)
    C["p"] += 256
    C["a"] += 512
    C["d"] += 256
    C["sp"] += HC * 16
    free("ab_f1", "ab_f0", "gst1", "gst0", "gstage")

    if stop == 7:
        return finish_early()

    W4_sb = enter("W4_sb", [128, DC * D], BF16)
    w9, rl9, d9, sp9 = C["w"], C["rl"], C["d"], C["sp"]
    p9pe = C["p"]
    qt_sb = enter("qt_sb", [128, DC * 256], BF16)
    A_sb = enter("A_sb", [128, NT * 256], BF16)
    OT_sb = enter("OT_sb", [128, DC * 256], BF16)
    cb_sb = jk_f[:, 768:1024]
    crow = mk_f[0:1, 0:256]
    apst = enter("apst", [128, 512], BF16)
    with nc.Block() as blk:
        @blk.sync
        def _(sync):
            sync.wait_ge(s_sp, sp3 + 16 * 192 + 64 * 16)
            sync.wait_ge(s_p, p9pe)
            for ec in range(DC):
                sync.dma_start(out=KTs[:, ec * T:(ec + 1) * T],
                               in_=KT_d[ec * 128:(ec + 1) * 128, :]).then_inc(s_rl, 16)

        @blk.gpsimd
        def _(gp):
            gp.wait_ge(s_sp, sp9)
            for dc in range(DC):
                gp.dma_start(out=W4_sb[:, dc * D:(dc + 1) * D],
                             in_=W_d[3][dc * 128:(dc + 1) * 128, :]).then_inc(s_wg, 16)
    C["rl"] += DC * 16
    C["wg"] += DC * 16

    if stop == 8:
        return finish_early()

    PE_PC = 32 + 32 + 1 + DC + 4
    w10v = C["w"]
    ACT_PC = 32
    DVE_PC = 1 + 1 + DC + 4
    p10, a10, d10, rl10, sp10 = C["p"], C["a"], C["p"] * 0 + C["d"], C["rl"], C["sp"]

    def act_rounds_before(c):
        return max(0, min(c - 1, NR + 1))

    def _a_of_round(j):
        c = j + 1
        return a10 + (c + 1) * ACT_PC + act_rounds_before(c) + 1

    base2 = dict(a=a10, d=d10, bp=C["bp"], cc=C["cc"], bv=C["bv"],
                 a_of_round=_a_of_round)
    acts2, dves2, gp2, n2 = make_bisect("q2", q2cand, 4096, N2_CAND, R2, FRAC2,
                                        s_slot=2, gs_slot=3, base=base2, ro=12,
                                        nthr=3, nrnd=NR, frc=B_FRC3)
    with nc.Block() as blk:
        @blk.sync
        def _(sync):
            sync.wait_ge(s_rl, rl9 + DC * 16)
            for sc in range(SC):
                if sc >= 1:
                    sync.wait_ge(s_p, p10 + (sc - 1) * PE_PC + 32)
                sync.dma_start(out=qt_sb[:, :].rearrange("p (ec t) -> p ec t", ec=DC),
                               in_=QT_d[sc, :, :].rearrange("(ec p) t -> p ec t", p=128)
                               ).then_inc(s_rl, 16)
                if sc == 0:
                    for tt in range(NT):
                        sync.dma_start(out=Vs[:, tt * D:(tt + 1) * D],
                                       in_=V_d[tt * 128:(tt + 1) * 128, :]
                                       ).then_inc(s_w, 16)
                for g in range(4):
                    sync.wait_ge(s_d, d10 + sc * DVE_PC + 2 + DC + g + 1)
                    st, dh = g // 2, g % 2
                    sync.dma_start(
                        out=AP_d[sc * 256 + st * 128: sc * 256 + (st + 1) * 128,
                                 dh * 512:(dh + 1) * 512],
                        in_=apst[:, :]).then_inc(s_sp, 16)

        @blk.tensor
        def _(pe):
            pe.wait_ge(s_wg, C["wg"])
            for sc in range(SC):
                pe.wait_ge(s_rl, rl10 + 16 * (sc + 1))
                for tt in range(NT):
                    psum = (pA, pB, pO1, pO2)[tt % 4]
                    if tt < 4 and sc >= 1:
                        pe.wait_ge(s_d, d10 + sc * DVE_PC)
                    if tt >= 4:
                        pe.wait_ge(s_a, a10 + sc * ACT_PC + act_rounds_before(sc) + (tt - 3))
                    mm = None
                    for ec in range(DC):
                        mm = pe.matmul(psum[:, 0:256],
                                       KTs[:, ec * T + tt * 128:ec * T + (tt + 1) * 128],
                                       qt_sb[:, ec * 256:(ec + 1) * 256],
                                       start=(ec == 0), stop=False)
                    mm = pe.matmul(psum[:, 0:256], ones_rowb[:, :],
                                   m32[0:1, sc * 256:(sc + 1) * 256],
                                   start=False, stop=True)
                    mm.then_inc(s_p, 1)
                pe.wait_ge(s_a, a10 + (sc + 1) * ACT_PC + act_rounds_before(sc))
                for tt in range(NT):
                    pe.matmul(pR[0:1, 0:256], ones_col[:, :],
                              A_sb[:, tt * 256:(tt + 1) * 256],
                              start=(tt == 0), stop=(tt == NT - 1)).then_inc(s_p, 1)
                pe.wait_ge(s_d, d10 + sc * DVE_PC + 1)
                pe.matmul(pR[:, 256:512], ones_row[:, :], crow,
                          start=True, stop=True).then_inc(s_p, 1)
                for ec in range(DC):
                    psum = (pO1, pO2)[ec % 2]
                    if sc == 0 and ec == 0:
                        pe.wait_ge(s_w, w10v + NT * 16)
                    if ec >= 2:
                        pe.wait_ge(s_d, d10 + sc * DVE_PC + 2 + (ec - 1))
                    mm = None
                    for tt in range(NT):
                        mm = pe.matmul(psum[:, 0:256],
                                       Vs[:, tt * D + ec * 128:tt * D + (ec + 1) * 128],
                                       A_sb[:, tt * 256:(tt + 1) * 256],
                                       start=(tt == 0), stop=(tt == NT - 1))
                    mm.then_inc(s_p, 1)
                for g in range(4):
                    st, dh = g // 2, g % 2
                    psum = (pA, pB)[g % 2]
                    if g == 0:
                        pe.wait_ge(s_d, d10 + sc * DVE_PC + 2 + DC)
                    if g >= 2:
                        pe.wait_ge(s_d, d10 + sc * DVE_PC + 2 + DC + (g - 1))
                    mm = None
                    for ec in range(DC):
                        mm = pe.matmul(psum[:, :],
                                       OT_sb[:, ec * 256 + st * 128:ec * 256 + (st + 1) * 128],
                                       W4_sb[:, ec * D + dh * 512:ec * D + (dh + 1) * 512],
                                       start=(ec == 0), stop=(ec == DC - 1))
                    mm.then_inc(s_p, 1)

        @blk.scalar
        def _(act):
            for sc in range(SC):
                for tt in range(NT):
                    act.wait_ge(s_p, p10 + sc * PE_PC + tt + 1)
                    act.activation(A_sb[:, tt * 256:(tt + 1) * 256],
                                   (pA, pB, pO1, pO2)[tt % 4][:, 0:256], AF.Exp,
                                   scale=0.03125).then_inc(s_a, 1)
                if 1 <= sc <= NR + 1:
                    acts2[sc - 1](act)

        @blk.vector
        def _(v):
            dves2[0](v)
            dves2[1](v)
            di = 2
            for sc in range(SC):
                v.wait_ge(s_p, p10 + sc * PE_PC + 32 + 32)
                v.reciprocal(crow, pR[0:1, 0:256]).then_inc(s_d, 1)
                v.wait_ge(s_p, p10 + sc * PE_PC + 32 + 32 + 1)
                v.tensor_copy(cb_sb, pR[:, 256:512]).then_inc(s_d, 1)
                for ec in range(DC):
                    v.wait_ge(s_p, p10 + sc * PE_PC + 65 + ec + 1)
                    v.tensor_tensor(OT_sb[:, ec * 256:(ec + 1) * 256],
                                    (pO1, pO2)[ec % 2][:, 0:256], cb_sb,
                                    op=ALU.mult).then_inc(s_d, 1)
                for g in range(4):
                    v.wait_ge(s_p, p10 + sc * PE_PC + 65 + DC + g + 1)
                    if sc * 4 + g >= 1:
                        v.wait_ge(s_sp, sp10 + 16 * (sc * 4 + g))
                    st, dh = g // 2, g % 2
                    v.tensor_tensor(apst[:, :], (pA, pB)[g % 2][:, :],
                                    b4_bc[:, dh * 512:(dh + 1) * 512],
                                    op=ALU.add).then_inc(s_d, 1)
                if di < len(dves2) and sc <= NR:
                    dves2[di](v)
                    di += 1
            while di < len(dves2):
                dves2[di](v)
                di += 1

        @blk.gpsimd
        def _(gp):
            gp2(gp)
    C["p"] += SC * PE_PC
    C["w"] += NT * 16
    C["a"] += SC * ACT_PC + n2["a"]
    C["d"] += SC * DVE_PC
    C["bv"] += n2["bv"]
    C["rl"] += SC * 16
    C["sp"] += SC * 4 * 16
    C["bp"] += n2["bp"]
    C["cc"] += n2["cc"]
    C["g"] += n2["g"]
    free("apst", "OT_sb", "A_sb", "qt_sb", "W4_sb", "q2cand")

    if stop == 9:
        return finish_early()

    wstB = [enter(f"wstB{i}", [128, 1024], F32) for i in range(2)]
    w11, d11 = C["w"], C["d"]
    with nc.Block() as blk:
        @blk.sync
        def _(sync):
            for ch in range(HC):
                if ch >= 2:
                    sync.wait_ge(s_d, d11 + ch - 1)
                sync.dma_start(out=wstB[ch % 2][:, :],
                               in_=w2T_d[ch * 128:(ch + 1) * 128, :]).then_inc(s_w, 16)

        @blk.vector
        def _(v):
            for ch in range(HC):
                v.wait_ge(s_w, w11 + 16 * (ch + 1))
                v.tensor_scalar(jk_f[:, :], wstB[ch % 2][:, :], sc1[:, 5:6], None,
                                op0=ALU.mult)
                v.tensor_scalar(jk_f[:, :], jk_f[:, :], MAGIC, MAGIC,
                                op0=ALU.add, op1=ALU.subtract)
                v.tensor_scalar(w2q[:, ch * D:(ch + 1) * D], jk_f[:, :],
                                1.0, -1.0, op0=ALU.min, op1=ALU.max).then_inc(s_d, 1)
    C["w"] += HC * 16
    C["d"] += HC

    if stop == 10:
        return finish_early()

    free("wstB1", "wstB0")
    gload = enter("gload", [128, HID], BF16)
    gq = [enter(f"gq{i}", [128, HID], BF16) for i in range(2)]
    gs1_f = enter("gs1_f", [128, HID // 2], F32)
    xt2 = [enter(f"xt2{i}", [128, D], F32) for i in range(2)]
    apt = [enter(f"apt{i}", [128, D], BF16) for i in range(2)]
    outst = enter("outst", [128, D], F32)
    rl12, d12, p12, st12, sp12b = C["rl"], C["d"], C["p"], C["st"], C["sp"]
    wg12 = C["wg"]
    with nc.Block() as blk:
        @blk.sync
        def _(sync):
            sync.wait_ge(s_sp, sp12b)
            sync.wait_ge(s_sp, C["sp"])
            sync.wait_ge(s_d, d12)
            for tt in range(NT):
                if tt >= 1:
                    sync.wait_ge(s_d, d12 + (tt - 1) * 4 + 1)
                sync.dma_start(out=gload[:, :].rearrange("p (hc t) -> p hc t", hc=HC),
                               in_=gT_d[tt, :, :].rearrange("(hc p) t -> p hc t", p=128)
                               ).then_inc(s_rl, 16)
                if tt >= 2:
                    sync.wait_ge(s_d, d12 + (tt - 2) * 4 + 4)
                sync.dma_start(out=xt2[tt % 2][:, :],
                               in_=x_d[tt * 128:(tt + 1) * 128, :]).then_inc(s_rl, 16)
                sync.dma_start(out=apt[tt % 2][:, :],
                               in_=AP_d[tt * 128:(tt + 1) * 128, :]).then_inc(s_rl, 16)


        @blk.vector
        def _(v):
            for tt in range(NT):
                v.wait_ge(s_rl, rl12 + 16 * (3 * tt + 1))
                if tt >= 2:
                    v.wait_ge(s_p, p12 + 2 * (tt - 1))
                for hh in range(2):
                    sl = slice(hh * (HID // 2), (hh + 1) * (HID // 2))
                    v.tensor_scalar(gs1_f[:, :], gload[:, sl], sc1[:, 2:3], None,
                                    op0=ALU.mult)
                    v.tensor_scalar(gs1_f[:, :], gs1_f[:, :], MAGIC, MAGIC,
                                    op0=ALU.add, op1=ALU.subtract)
                    mm = v.tensor_scalar(gq[tt % 2][:, sl], gs1_f[:, :], 127.0, -128.0,
                                         op0=ALU.min, op1=ALU.max)
                mm.then_inc(s_d, 1)
                if tt >= 1:
                    v.wait_ge(s_st, st12 + 16 * tt)
                for dh in range(2):
                    v.wait_ge(s_p, p12 + tt * 2 + dh + 1)
                    psum = (pA, pB, pO1, pO2)[(tt * 2 + dh) % 4]
                    v.tensor_scalar(outst[:, dh * 512:(dh + 1) * 512],
                                    psum[:, :], sc1[:, 3:4], None, op0=ALU.mult).then_inc(s_d, 1)
                v.wait_ge(s_rl, rl12 + 16 * (3 * tt + 3))
                v.tensor_tensor(outst[:, :], outst[:, :],
                                xt2[tt % 2][:, :], op=ALU.add)
                v.tensor_tensor(outst[:, :], outst[:, :],
                                apt[tt % 2][:, :], op=ALU.add).then_inc(s_d, 1)

        @blk.tensor
        def _(pe):
            pe.wait_ge(s_d, d12)
            for tt in range(NT):
                pe.wait_ge(s_d, d12 + tt * 4 + 1)
                for dh in range(2):
                    psum = (pA, pB, pO1, pO2)[(tt * 2 + dh) % 4]
                    if tt >= 2:
                        pe.wait_ge(s_d, d12 + (tt - 2) * 4 + 2 + dh)
                    mm = None
                    for hc in range(HC):
                        mm = pe.matmul(psum[:, :],
                                       gq[tt % 2][:, hc * 128:(hc + 1) * 128],
                                       w2q[:, hc * D + dh * 512:hc * D + (dh + 1) * 512],
                                       start=(hc == 0), stop=(hc == HC - 1))
                    mm.then_inc(s_p, 1)

        @blk.scalar
        def _(act):
            for tt in range(NT):
                act.wait_ge(s_d, d12 + tt * 4 + 4)
                act.dma_start(out=out_d[tt * 128:(tt + 1) * 128, :],
                              in_=outst[:, :]).then_inc(s_st, 16)
            act.wait_ge(s_st, st12 + NT * 16)
    C["rl"] += NT * 3 * 16
    C["d"] += NT * 4
    C["p"] += NT * 2
    C["st"] += NT * 16

    es.close()
    return nc


_NC_CACHE = {}


def kernel(**inputs):
    x = np.ascontiguousarray(np.asarray(inputs["L0_attn_x"], dtype=np.float32))
    w1T = np.ascontiguousarray(np.asarray(inputs["L0_ffn_w1"], dtype=np.float32).T)
    w2T = np.ascontiguousarray(np.asarray(inputs["L0_ffn_w2"], dtype=np.float32).T)
    common = {
        "w1T": w1T,
        "w2T": w2T,
        "L0_ffn_ln_g": np.ascontiguousarray(np.asarray(inputs["L0_ffn_ln_g"], np.float32)),
        "L0_ffn_ln_b": np.ascontiguousarray(np.asarray(inputs["L0_ffn_ln_b"], np.float32)),
    }
    for i in (1, 2, 3, 4):
        common[f"L0_attn_W{i}"] = np.ascontiguousarray(
            np.asarray(inputs[f"L0_attn_W{i}"], np.float32))
        common[f"L0_attn_b{i}"] = np.ascontiguousarray(
            np.asarray(inputs[f"L0_attn_b{i}"], np.float32))

    if "nc" not in _NC_CACHE:
        _NC_CACHE["nc"] = build_program()
    nc = _NC_CACHE["nc"]

    in_maps = []
    for c in range(8):
        m = dict(common)
        m["L0_attn_x"] = np.ascontiguousarray(x[c])
        in_maps.append(m)
    res = run_bass_kernel_spmd(nc, in_maps, core_ids=list(range(8)))
    return np.stack([r["out"] for r in res.results], axis=0)



# version 17
# speedup vs baseline: 1.0197x; 1.0004x over previous
import math
import numpy as np

import concourse.bass as bass
import concourse.mybir as mybir
from concourse.bass_utils import run_bass_kernel_spmd

F32 = mybir.dt.float32
BF16 = mybir.dt.bfloat16
AF = mybir.ActivationFunctionType
ALU = mybir.AluOpType
AX = mybir.AxisListType.X
AXY = mybir.AxisListType.XY

T, D, HID = 4096, 1024, 4096
NT, DC, SC, HC = T // 128, D // 128, T // 256, HID // 128
MAGIC = 1.5 * 2.0 ** 23
INV1024 = 1.0 / 1024.0
NW = 1.0 / (HID * D)

N1_FULL, N2_FULL = 8 * T * D, 8 * T * HID
_p1 = 0.995 * (N1_FULL - 1)
I1 = math.floor(_p1)
FRAC1, R1 = _p1 - I1, N1_FULL - I1
_p2 = 0.995 * (N2_FULL - 1)
I2 = math.floor(_p2)
FRAC2, R2 = _p2 - I2, N2_FULL - I2
N1_CAND, N2_CAND = 8 * 128 * 1024, 8 * 128 * 4096
NR = 11
NR1 = 8
DBG_Q1 = False
RG = [list(range(8))]

B_LO, B_HI, B_W, B_NB, B_IDX = 0, 1, 2, 3, 4
B_T0, B_T1, B_CHI, B_AMAX, B_S, B_GAM, B_GS = 5, 6, 7, 8, 9, 10, 11
B_CNT, B_PRED, B_THR, B_NTH = 16, 24, 32, 40
B_FRC3, B_FRC7 = 48, 52


def build_program(stop=None):
    nc = bass.Bass("TRN2", target_bir_lowering=False)

    x_d = nc.dram_tensor("L0_attn_x", [T, D], F32, kind="ExternalInput").ap()
    W_d = [nc.dram_tensor(f"L0_attn_W{i}", [D, D], F32, kind="ExternalInput").ap()
           for i in (1, 2, 3, 4)]
    b_d = [nc.dram_tensor(f"L0_attn_b{i}", [D], F32, kind="ExternalInput").ap()
           for i in (1, 2, 3, 4)]
    lng_d = nc.dram_tensor("L0_ffn_ln_g", [D], F32, kind="ExternalInput").ap()
    lnb_d = nc.dram_tensor("L0_ffn_ln_b", [D], F32, kind="ExternalInput").ap()
    w1T_d = nc.dram_tensor("w1T", [D, HID], F32, kind="ExternalInput").ap()
    w2T_d = nc.dram_tensor("w2T", [HID, D], F32, kind="ExternalInput").ap()
    out_d = nc.dram_tensor("out", [T, D], F32, kind="ExternalOutput").ap()

    QT_d = nc.dram_tensor("QT_d", [SC, D, 256], BF16).ap()
    KT_d = nc.dram_tensor("KT_d", [D, T], BF16).ap()
    V_d = nc.dram_tensor("V_d", [T, D], BF16).ap()
    gT_d = nc.dram_tensor("gT_d", [NT, HID, 128], BF16).ap()
    AP_d = nc.dram_tensor("AP_d", [T, D], BF16).ap()
    cc_in_t = nc.dram_tensor("cc_in_d", [8192], F32)
    cc_out_t = nc.dram_tensor("cc_out_d", [8192], F32)
    row_t = nc.dram_tensor("row_d", [1, 64], F32, kind="ExternalOutput")
    dbg_t = nc.dram_tensor("dbg_d", [16, 32], F32, kind="ExternalOutput")
    dbg2_t = nc.dram_tensor("dbg2_d", [128, 1024], F32, kind="ExternalOutput")

    live = {}

    def enter(name, shape, dtype, kind="sbuf", side="left"):
        if kind == "sbuf":
            cm = nc.sbuf_tensor(name, shape, dtype, side=side)
        else:
            cm = nc.psum_tensor(name, shape, dtype)
        h = cm.__enter__()
        live[name] = cm
        return h

    def free(*names):
        for n in names:
            live.pop(n).__exit__(None, None, None)

    from contextlib import ExitStack
    es = ExitStack()

    def sem(name):
        return es.enter_context(nc.semaphore(name))

    ident = enter("ident", [128, 128], BF16, side="right")
    pidx = enter("pidx", [128, 1], F32, side="right")
    ones_col = enter("ones_col", [128, 1], BF16, side="right")
    ones_row = enter("ones_row", [1, 128], F32, side="right")
    ones_rowb = enter("ones_rowb", [1, 128], BF16, side="right")
    b_sb = enter("b_sb", [128, 4 * DC], F32, side="right")
    b4_bc = enter("b4_bc", [128, D], F32, side="right")

    bis = enter("bis", [1, 64], F32, side="right")
    bisb = enter("bisb", [128, 8], F32, side="right")
    srow = enter("srow", [1, 1024], F32, side="right")
    cstat = enter("cstat", [128, 64], F32, side="right")
    sc1 = enter("sc1", [128, 8], F32, side="right")
    jk_f = enter("jk_f", [128, D], F32, side="right")
    mk_f = enter("mk_f", [128, 256], F32, side="right")

    pT1 = enter("pT1", [128, 1024], BF16, "psum")
    pT2 = enter("pT2", [128, 1024], BF16, "psum")
    pA = enter("pA", [128, 512], F32, "psum")
    pB = enter("pB", [128, 512], F32, "psum")
    pR = enter("pR", [128, 512], F32, "psum")
    pO1 = enter("pO1", [128, 512], F32, "psum")
    pO2 = enter("pO2", [128, 512], F32, "psum")
    pJ = enter("pJ", [128, 512], F32, "psum")

    m32 = enter("m32", [1, T], BF16, side="right")
    big1 = enter("big1", [128, DC * T], BF16, side="right")
    big2 = enter("big2", [128, DC * T], BF16, side="right")
    xnT = xqT = KTs = w2q = big1
    ln2T = w1q = Vs = big2

    q1cand = enter("q1cand", [128, 1024], F32)
    identf = enter("identf", [128, 128], F32)
    iot = enter("iot", [128, 128], F32)
    xt = [enter(f"xt{i}", [128, D], F32) for i in range(2)]
    xn_f = [enter(f"xn_f{i}", [128, D], F32) for i in range(2)]
    l2_f = enter("l2_f", [128, D], F32)
    xnb = [enter(f"xnb{i}", [128, D], BF16) for i in range(2)]
    l2b = [enter(f"l2b{i}", [128, D], BF16) for i in range(2)]
    g_bc = enter("g_bc", [128, D], F32)
    bgb_bc = enter("bgb_bc", [128, D], F32)
    st_sum = enter("st_sum", [128, NT], F32)
    st_sq = enter("st_sq", [128, NT], F32)
    rstd = enter("rstd", [128, NT], F32)
    negmr = enter("negmr", [128, NT], F32)
    rs2 = enter("rs2", [128, NT], F32)
    stt1 = enter("stt1", [128, NT], F32)
    stt2 = enter("stt2", [128, NT], F32)

    s_x = sem("s_x")
    s_w = sem("s_w")
    s_a = sem("s_a")
    s_d = sem("s_d")
    s_p = sem("s_p")
    s_sp = sem("s_sp")
    s_rl = sem("s_rl")
    s_cc = sem("s_cc")
    s_bp = sem("s_bp")
    s_bv = sem("s_bv")
    s_g = sem("s_g")
    s_dbg = sem("s_dbg")
    s_st = sem("s_st")
    s_wg = sem("s_wg")

    C = dict(x=0, w=0, a=0, d=0, p=0, sp=0, rl=0, cc=0, bp=0, bv=0, g=0, st=0,
             wg=0)


    def finish_early():
        fin = enter("fin_sb", [128, 64], F32)
        with nc.Block() as blk:
            @blk.vector
            def _(v):
                v.memset(fin[:, :], 1.0).then_inc(s_d, 1)

            @blk.sync
            def _(sync):
                sync.wait_ge(s_d, C["d"] + 1)
                sync.dma_start(out=out_d[0:128, 0:64], in_=fin[:, :]).then_inc(s_st, 16)
                sync.wait_ge(s_st, C["st"] + 16)
        es.close()
        return nc

    def vsp(v):
        return v.tensor_scalar(jk_f[:, 512:576], jk_f[:, 512:576], 1.0, None,
                               op0=ALU.mult)

    def make_bisect(tag, cand, ncol, Ntot, rank, frac, s_slot, gs_slot, base,
                    ro=8, dbg=None, nthr=3, nrnd=None, frc=None):
        nch = ncol // 512
        if nrnd is None:
            nrnd = NR
        if frc is None:
            frc = B_FRC3
        inv = 1.0 / (nthr + 1)
        ncc = nthr * nch
        act_steps, dve_steps = [], []

        def dve_init(v):
            v.reduce_max(cstat[:, 0:1], cand[:, :], axis=AX)
            vsp(v).then_inc(s_bv, 1)
        dve_steps.append(dve_init)

        def dve_init2(v):
            v.wait_ge(s_bp, base["bp"] + 16 * 3)
            v.reduce_max(bis[0:1, B_HI:B_HI + 1], srow[0:1, 0:128], axis=AX)
            vsp(v)
            v.memset(bis[0:1, B_LO:B_LO + 1], 0.0)
            vsp(v).then_inc(s_bv, 1)
        dve_steps.append(dve_init2)

        def mk_dve_round(j):
            def f(v):
                if j > 0:
                    v.wait_ge(s_bp, base["bp"] + 16 * (3 + 4 * j))
                    for k in range(nthr):
                        v.tensor_reduce(
                            bis[0:1, B_CNT + k:B_CNT + k + 1],
                            srow[0:1, 0:128 * nthr].rearrange(
                                "q (p c) -> q p c", p=128)[:, :, k:k + 1],
                            axis=AXY, op=ALU.add)
                    if j <= nrnd:
                        vsp(v)
                        v.tensor_scalar(bis[0:1, B_CNT:B_CNT + nthr], bis[0:1, B_CNT:B_CNT + nthr],
                                        0.5, Ntot * 0.5, op0=ALU.mult, op1=ALU.add)
                        vsp(v)
                        v.tensor_scalar(bis[0:1, B_PRED:B_PRED + nthr], bis[0:1, B_CNT:B_CNT + nthr],
                                        float(rank) - 0.5, None, op0=ALU.is_lt)
                        vsp(v)
                        v.reduce_sum(bis[0:1, B_NB:B_NB + 1], bis[0:1, B_PRED:B_PRED + nthr], axis=AX)
                        vsp(v)
                        v.tensor_scalar(bis[0:1, B_IDX:B_IDX + 1], bis[0:1, B_NB:B_NB + 1],
                                        -1.0, float(nthr), op0=ALU.mult, op1=ALU.add)
                        v.tensor_tensor(bis[0:1, B_W:B_W + 1], bis[0:1, B_HI:B_HI + 1],
                                        bis[0:1, B_LO:B_LO + 1], op=ALU.subtract)
                        vsp(v)
                        v.tensor_tensor(bis[0:1, B_T0:B_T0 + 1], bis[0:1, B_W:B_W + 1],
                                        bis[0:1, B_IDX:B_IDX + 1], op=ALU.mult)
                        vsp(v)
                        v.tensor_scalar(bis[0:1, B_T0:B_T0 + 1], bis[0:1, B_T0:B_T0 + 1],
                                        inv, None, op0=ALU.mult)
                        vsp(v)
                        v.tensor_tensor(bis[0:1, B_LO:B_LO + 1], bis[0:1, B_LO:B_LO + 1],
                                        bis[0:1, B_T0:B_T0 + 1], op=ALU.add)
                        v.tensor_scalar(bis[0:1, B_T1:B_T1 + 1], bis[0:1, B_W:B_W + 1],
                                        inv, None, op0=ALU.mult)
                        vsp(v)
                        v.tensor_tensor(bis[0:1, B_HI:B_HI + 1], bis[0:1, B_LO:B_LO + 1],
                                        bis[0:1, B_T1:B_T1 + 1], op=ALU.add)
                        vsp(v)
                if j <= nrnd - 1:
                    v.tensor_tensor(bis[0:1, B_W:B_W + 1], bis[0:1, B_HI:B_HI + 1],
                                    bis[0:1, B_LO:B_LO + 1], op=ALU.subtract)
                    vsp(v)
                    v.tensor_scalar(bis[0:1, B_THR:B_THR + nthr], bis[0:1, frc:frc + nthr],
                                    bis[0:1, B_W:B_W + 1], None, op0=ALU.mult)
                    vsp(v)
                    v.tensor_scalar(bis[0:1, B_THR:B_THR + nthr], bis[0:1, B_THR:B_THR + nthr],
                                    bis[0:1, B_LO:B_LO + 1], None, op0=ALU.add)
                elif j == nrnd:
                    v.tensor_scalar(bis[0:1, B_THR:B_THR + nthr], bis[0:1, frc:frc + nthr],
                                    0.0, None, op0=ALU.mult)
                    vsp(v)
                    v.tensor_scalar(bis[0:1, B_THR:B_THR + nthr], bis[0:1, B_THR:B_THR + nthr],
                                    bis[0:1, B_HI:B_HI + 1], None, op0=ALU.add)
                vsp(v)
                v.tensor_scalar(bis[0:1, B_NTH:B_NTH + nthr], bis[0:1, B_THR:B_THR + nthr],
                                -1.0, None, op0=ALU.mult)
                vsp(v).then_inc(s_bv, 1)
            return f

        for j in range(nrnd + 1):
            dve_steps.append(mk_dve_round(j))

        def mk_act_round(j):
            def f(act):
                act.wait_ge(s_bp, base["bp"] + 16 * (3 + 4 * j + 2))
                mm = None
                for k in range(nthr):
                    for c in range(nch):
                        mm = act.activation(
                            pJ[:, :], cand[:, c * 512:(c + 1) * 512], AF.Sign,
                            bias=bisb[:, k:k + 1],
                            accum_out=cstat[:, j % 2 * 32 + k * nch + c:
                                            j % 2 * 32 + k * nch + c + 1])
                mm.then_inc(s_a, 1)
            return f

        for j in range(nrnd + 1):
            act_steps.append(mk_act_round(j))

        def dve_extract(v):
            v.wait_ge(s_bp, base["bp"] + 16 * (3 + 4 * (nrnd + 1)))
            v.tensor_reduce(
                bis[0:1, B_CHI:B_CHI + 1],
                srow[0:1, 0:128 * nthr].rearrange("q (p c) -> q p c", p=128)[:, :, 0:1],
                axis=AXY, op=ALU.add)
            vsp(v)
            v.tensor_scalar(bis[0:1, B_CHI:B_CHI + 1], bis[0:1, B_CHI:B_CHI + 1],
                            0.5, Ntot * 0.5, op0=ALU.mult, op1=ALU.add)
            v.tensor_scalar(bisb[:, 7:8], bisb[:, 0:1], -1.0, None, op0=ALU.mult)
            vsp(v)
            v.tensor_scalar(jk_f[:, 0:512], cand[:, 0:512], bisb[:, 7:8], -1e30,
                            op0=ALU.is_gt, op1=ALU.mult)
            v.tensor_tensor(jk_f[:, 0:512], jk_f[:, 0:512], cand[:, 0:512], op=ALU.add)
            v.reduce_max(cstat[:, 0:1], jk_f[:, 0:512], axis=AX)
            for c in range(1, nch):
                v.tensor_scalar(jk_f[:, 0:512], cand[:, c * 512:(c + 1) * 512],
                                bisb[:, 7:8], -1e30, op0=ALU.is_gt, op1=ALU.mult)
                v.tensor_tensor(jk_f[:, 0:512], jk_f[:, 0:512],
                                cand[:, c * 512:(c + 1) * 512], op=ALU.add)
                v.reduce_max(cstat[:, 1:2], jk_f[:, 0:512], axis=AX)
                vsp(v)
                v.tensor_tensor(cstat[:, 0:1], cstat[:, 0:1], cstat[:, 1:2], op=ALU.max)
            v.tensor_scalar(jk_f[:, 0:512], cand[:, 0:512], bisb[:, 7:8], 1e30,
                            op0=ALU.is_le, op1=ALU.mult)
            v.tensor_tensor(jk_f[:, 0:512], jk_f[:, 0:512], cand[:, 0:512], op=ALU.add)
            v.tensor_reduce(cstat[:, 2:3], jk_f[:, 0:512], axis=AX, op=ALU.min)
            for c in range(1, nch):
                v.tensor_scalar(jk_f[:, 0:512], cand[:, c * 512:(c + 1) * 512],
                                bisb[:, 7:8], 1e30, op0=ALU.is_le, op1=ALU.mult)
                v.tensor_tensor(jk_f[:, 0:512], jk_f[:, 0:512],
                                cand[:, c * 512:(c + 1) * 512], op=ALU.add)
                v.tensor_reduce(cstat[:, 3:4], jk_f[:, 0:512], axis=AX, op=ALU.min)
                vsp(v)
                v.tensor_tensor(cstat[:, 2:3], cstat[:, 2:3], cstat[:, 3:4],
                                op=ALU.min)
            vsp(v).then_inc(s_bv, 1)
        dve_steps.append(dve_extract)

        def dve_final(v):
            v.wait_ge(s_bp, base["bp"] + 16 * (3 + 4 * (nrnd + 1) + 4))
            v.tensor_reduce(bis[0:1, B_T0:B_T0 + 1],
                            srow[0:1, 0:256].rearrange("q (p c) -> q p c", p=128)[:, :, 0:1],
                            axis=AXY, op=ALU.max)
            v.tensor_reduce(bis[0:1, B_T1:B_T1 + 1],
                            srow[0:1, 256:512].rearrange("q (p c) -> q p c", p=128)[:, :, 0:1],
                            axis=AXY, op=ALU.min)
            v.tensor_scalar(bis[0:1, B_PRED:B_PRED + 1], bis[0:1, B_CHI:B_CHI + 1],
                            float(rank - 1), None, op0=ALU.is_equal)
            vsp(v)
            v.tensor_tensor(bis[0:1, B_T1:B_T1 + 1], bis[0:1, B_T1:B_T1 + 1],
                            bis[0:1, B_T0:B_T0 + 1], op=ALU.subtract)
            vsp(v)
            v.tensor_tensor(bis[0:1, B_T1:B_T1 + 1], bis[0:1, B_T1:B_T1 + 1],
                            bis[0:1, B_PRED:B_PRED + 1], op=ALU.mult)
            vsp(v)
            v.tensor_scalar(bis[0:1, B_T1:B_T1 + 1], bis[0:1, B_T1:B_T1 + 1],
                            float(frac), None, op0=ALU.mult)
            vsp(v)
            v.tensor_tensor(bis[0:1, B_AMAX:B_AMAX + 1], bis[0:1, B_T0:B_T0 + 1],
                            bis[0:1, B_T1:B_T1 + 1], op=ALU.add)
            vsp(v)
            v.tensor_scalar(bis[0:1, B_AMAX:B_AMAX + 1], bis[0:1, B_AMAX:B_AMAX + 1],
                            1e-5, None, op0=ALU.max)
            vsp(v)
            v.reciprocal(bis[0:1, B_S:B_S + 1], bis[0:1, B_AMAX:B_AMAX + 1])
            vsp(v)
            v.tensor_scalar(bis[0:1, B_S:B_S + 1], bis[0:1, B_S:B_S + 1],
                            127.0, None, op0=ALU.mult)
            v.tensor_tensor(bis[0:1, B_GS:B_GS + 1], bis[0:1, B_GAM:B_GAM + 1],
                            bis[0:1, B_AMAX:B_AMAX + 1], op=ALU.mult)
            vsp(v)
            v.tensor_scalar(bis[0:1, B_GS:B_GS + 1], bis[0:1, B_GS:B_GS + 1],
                            1.0 / 127.0, None, op0=ALU.mult)
            vsp(v).then_inc(s_bv, 1)
        dve_steps.append(dve_final)

        def emit_gpsimd(gp):
            bp = base["bp"]
            cc = base["cc"]
            bv = base["bv"]
            gp.wait_ge(s_bv, bv + 1)
            gp.dma_start(out=cc_in_t[0:128].rearrange("(p c) -> p c", p=128),
                         in_=cstat[:, 0:1]).then_inc(s_bp, 16)
            gp.wait_ge(s_bp, bp + 16)
            gp.collective_compute("AllReduce", ALU.max, replica_groups=RG,
                                  ins=[cc_in_t[0:128].opt()],
                                  outs=[cc_out_t[0:128].opt()]).then_inc(s_cc, 1)
            gp.wait_ge(s_cc, cc + 1)
            gp.dma_start(out=srow[0:1, 0:128],
                         in_=cc_out_t[0:128].rearrange("(q n) -> q n", q=1)).then_inc(s_bp, 16)
            gp.dma_start(out=row_t.ap()[0:1, 0:1], in_=bis[0:1, 0:1]).then_inc(s_bp, 16)
            for j in range(nrnd + 1):
                gp.wait_ge(s_bv, bv + 3 + j)
                if dbg is not None:
                    gp.dma_start(out=dbg[j:j + 1, :],
                                 in_=bis[0:1, 0:32]).then_inc(s_dbg, 16)
                gp.dma_start(out=row_t.ap()[0:1, 0:nthr],
                             in_=bis[0:1, B_NTH:B_NTH + nthr]).then_inc(s_bp, 16)
                gp.wait_ge(s_bp, bp + 16 * (3 + 4 * j + 1))
                gp.dma_start(out=bisb[:, 0:nthr],
                             in_=row_t.ap()[0:1, 0:nthr].partition_broadcast(128)
                             ).then_inc(s_bp, 16)
                gp.wait_ge(s_a, base["a_of_round"](j))
                b0 = j % 2 * 32
                roff = 1024 + 1024 * (j % 2)
                h = nch
                while h > 1:
                    h //= 2
                    for k in range(nthr):
                        gp.tensor_tensor(cstat[:, b0 + k * nch:b0 + k * nch + h],
                                         cstat[:, b0 + k * nch:b0 + k * nch + h],
                                         cstat[:, b0 + k * nch + h:b0 + k * nch + 2 * h],
                                         op=ALU.add)
                gp.dma_start(out=cc_in_t[roff:roff + 128 * nthr].rearrange(
                                 "(p k c) -> p k c", p=128, k=nthr),
                             in_=cstat[:, b0:b0 + nthr * nch].rearrange(
                                 "p (k c) -> p k c", k=nthr)[:, :, 0:1],
                             allow_slow_non_contiguous=True).then_inc(s_bp, 16)
                gp.wait_ge(s_bp, bp + 16 * (3 + 4 * j + 3))
                gp.collective_compute("AllReduce", ALU.add, replica_groups=RG,
                                      ins=[cc_in_t[roff:roff + 128 * nthr].opt()],
                                      outs=[cc_out_t[0:128 * nthr].opt()]).then_inc(s_cc, 1)
                gp.wait_ge(s_cc, cc + 2 + j)
                gp.dma_start(out=srow[0:1, 0:128 * nthr],
                             in_=cc_out_t[0:128 * nthr].rearrange("(q n) -> q n", q=1)
                             ).then_inc(s_bp, 16)
            gp.wait_ge(s_bv, bv + 2 + (nrnd + 1) + 1)
            if dbg is not None:
                gp.dma_start(out=dbg[nrnd + 1:nrnd + 2, :],
                             in_=bis[0:1, 0:32]).then_inc(s_dbg, 16)
                gp.dma_start(out=dbg2_t.ap()[:, 0:ncol],
                             in_=cand[:, :]).then_inc(s_dbg, 16)
            gp.dma_start(out=cc_in_t[3072:3328].rearrange("(p c) -> p c", p=128),
                         in_=cstat[:, 0:2]).then_inc(s_bp, 16)
            gp.dma_start(out=cc_in_t[3328:3584].rearrange("(p c) -> p c", p=128),
                         in_=cstat[:, 2:4]).then_inc(s_bp, 16)
            gp.wait_ge(s_bp, bp + 16 * (3 + 4 * (nrnd + 1) + 2))
            gp.collective_compute("AllReduce", ALU.max, replica_groups=RG,
                                  ins=[cc_in_t[3072:3328].opt()],
                                  outs=[cc_out_t[0:256].opt()]).then_inc(s_cc, 1)
            gp.wait_ge(s_cc, cc + 1 + (nrnd + 1) + 1)
            gp.collective_compute("AllReduce", ALU.min, replica_groups=RG,
                                  ins=[cc_in_t[3328:3584].opt()],
                                  outs=[cc_out_t[256:512].opt()]).then_inc(s_cc, 1)
            gp.wait_ge(s_cc, cc + 1 + (nrnd + 1) + 2)
            gp.dma_start(out=srow[0:1, 0:256],
                         in_=cc_out_t[0:256].rearrange("(q n) -> q n", q=1)).then_inc(s_bp, 16)
            gp.dma_start(out=srow[0:1, 256:512],
                         in_=cc_out_t[256:512].rearrange("(q n) -> q n", q=1)).then_inc(s_bp, 16)
            gp.wait_ge(s_bv, bv + 2 + (nrnd + 1) + 2)
            gp.dma_start(out=row_t.ap()[0:1, ro:ro + 4],
                         in_=bis[0:1, B_AMAX:B_AMAX + 4]).then_inc(s_bp, 16)
            gp.wait_ge(s_bp, bp + 16 * (3 + 4 * (nrnd + 1) + 5))
            gp.dma_start(out=bisb[:, 4:8],
                         in_=row_t.ap()[0:1, ro:ro + 4].partition_broadcast(128)
                         ).then_inc(s_bp, 16)
            gp.wait_ge(s_bp, bp + 16 * (3 + 4 * (nrnd + 1) + 6))
            if dbg is not None:
                gp.wait_ge(s_dbg, 16 * (nrnd + 3))
            gp.tensor_copy(sc1[:, s_slot:s_slot + 1], bisb[:, 5:6])
            gp.tensor_copy(sc1[:, gs_slot:gs_slot + 1], bisb[:, 7:8]).then_inc(s_g, 1)

        n_bp = 16 * (3 + 4 * (nrnd + 1) + 6)
        n_cc = 1 + (nrnd + 1) + 2
        n_bv = 2 + (nrnd + 1) + 2
        n_act = nrnd + 1
        n_g = 1
        return act_steps, dve_steps, emit_gpsimd, dict(bp=n_bp, cc=n_cc, bv=n_bv,
                                                       a=n_act, g=n_g)


    with nc.Block() as blk:
        @blk.sync
        def _(sync):
            for i, bd in enumerate(b_d):
                sync.dma_start(out=b_sb[:, i * DC:(i + 1) * DC],
                               in_=bd.rearrange("(c p) -> p c", p=128),
                               allow_slow_non_contiguous=True).then_inc(s_w, 16)
            sync.dma_start(out=g_bc[:, :], in_=lng_d.partition_broadcast(128)).then_inc(s_w, 16)
            sync.dma_start(out=bgb_bc[:, :], in_=lnb_d.partition_broadcast(128)).then_inc(s_w, 16)
            sync.dma_start(out=b4_bc[:, :], in_=b_d[3].partition_broadcast(128)).then_inc(s_w, 16)
            for i in range(NT):
                if i >= 2:
                    sync.wait_ge(s_a, 2 * (i - 1))
                sync.dma_start(out=xt[i % 2][:, :],
                               in_=x_d[i * 128:(i + 1) * 128, :]).then_inc(s_x, 16)

        @blk.gpsimd
        def _(gp):
            gp.iota(iot[:, :], [[1, 128]], channel_multiplier=0,
                    allow_small_or_imprecise_dtypes=True)
            gp.iota(pidx[:, :], [[0, 1]], channel_multiplier=1,
                    allow_small_or_imprecise_dtypes=True).then_inc(s_g, 1)

        @blk.scalar
        def _(act):
            for i in range(NT):
                act.wait_ge(s_x, 16 * (i + 1))
                act.activation(jk_f[:, :], xt[i % 2][:, :], AF.Identity,
                               accum_out=st_sum[:, i:i + 1])
                act.activation(jk_f[:, :], xt[i % 2][:, :], AF.Square,
                               accum_out=st_sq[:, i:i + 1]).then_inc(s_a, 2)

        @blk.vector
        def _(v):
            v.memset(ones_col[:, :], 1.0)
            v.memset(ones_row[:, :], 1.0)
            v.memset(ones_rowb[:, :], 1.0)
            v.memset(bis[:, :], 0.0)
            for k in range(3):
                v.memset(bis[0:1, B_FRC3 + k:B_FRC3 + k + 1], (k + 1) / 4.0)
            for k in range(7):
                v.memset(bis[0:1, B_FRC7 + k:B_FRC7 + k + 1], (k + 1) / 8.0)
            v.wait_ge(s_g, 1)
            v.tensor_scalar(identf[:, :], iot[:, :], pidx[:, :], None, op0=ALU.is_equal)
            v.tensor_copy(ident[:, :], identf[:, :]).then_inc(s_d, 1)
    C["x"] += NT * 16
    C["w"] += 7 * 16
    C["g"] += 1
    C["a"] += 2 * NT
    C["d"] += 1

    a0, d0 = C["a"], C["d"]
    with nc.Block() as blk:
        @blk.vector
        def _(v):
            v.wait_ge(s_a, a0)
            v.tensor_scalar(stt1[:, :], st_sum[:, :], INV1024, None, op0=ALU.mult)
            vsp(v)
            v.tensor_tensor(stt2[:, :], stt1[:, :], stt1[:, :], op=ALU.mult)
            v.tensor_scalar(rstd[:, :], st_sq[:, :], INV1024, 1e-5, op0=ALU.mult, op1=ALU.add)
            vsp(v)
            v.tensor_tensor(rstd[:, :], rstd[:, :], stt2[:, :], op=ALU.subtract)
            vsp(v).then_inc(s_d, 1)

        @blk.scalar
        def _(act):
            act.wait_ge(s_d, d0 + 1)
            act.activation(negmr[:, :], rstd[:, :], AF.Sqrt).then_inc(s_a, 1)

        @blk.vector
        def _(v):
            v.wait_ge(s_a, a0 + 1)
            v.reciprocal(rs2[:, :], negmr[:, :])
            vsp(v)
            v.tensor_tensor(stt2[:, :], rs2[:, :], rs2[:, :], op=ALU.mult)
            v.tensor_scalar(rstd[:, :], rstd[:, :], -1e-5, None, op0=ALU.add)
            vsp(v)
            v.tensor_tensor(stt2[:, :], stt2[:, :], rstd[:, :], op=ALU.mult)
            vsp(v)
            v.tensor_scalar(stt2[:, :], stt2[:, :], 1e-5, None, op0=ALU.add)
            v.tensor_scalar(stt1[:, :], stt1[:, :], -1.0, None, op0=ALU.mult)
            vsp(v)
            v.tensor_tensor(negmr[:, :], stt1[:, :], rs2[:, :], op=ALU.mult)
            v.tensor_copy(rstd[:, :], rs2[:, :])
            vsp(v).then_inc(s_d, 1)

        @blk.scalar
        def _(act):
            act.wait_ge(s_d, d0 + 2)
            act.activation(stt2[:, :], stt2[:, :], AF.Sqrt).then_inc(s_a, 1)

        @blk.vector
        def _(v):
            v.wait_ge(s_a, a0 + 2)
            v.reciprocal(rs2[:, :], stt2[:, :])
            vsp(v).then_inc(s_d, 1)
    C["a"] += 2
    C["d"] += 3

    x1, a1, d1, p1 = C["x"], C["a"], C["d"], C["p"]
    with nc.Block() as blk:
        @blk.sync
        def _(sync):
            for i in range(NT):
                if i >= 2:
                    sync.wait_ge(s_a, a1 + (i - 1))
                sync.dma_start(out=xt[i % 2][:, :],
                               in_=x_d[i * 128:(i + 1) * 128, :]).then_inc(s_x, 16)

        @blk.scalar
        def _(act):
            for i in range(NT):
                act.wait_ge(s_x, x1 + 16 * (i + 1))
                if i >= 2:
                    act.wait_ge(s_d, d1 + (i - 1))
                    act.wait_ge(s_p, p1 + (i - 1))
                act.activation(xnb[i % 2][:, :], xt[i % 2][:, :], AF.Identity,
                               bias=negmr[:, i:i + 1],
                               scale=rstd[:, i:i + 1]).then_inc(s_a, 1)

        @blk.tensor
        def _(pe):
            for i in range(NT):
                pe.wait_ge(s_d, d1 + i + 1)
                mm = None
                for dc in range(DC):
                    pe.transpose(pT1[:, dc * 128:(dc + 1) * 128],
                                 xnb[i % 2][:, dc * 128:(dc + 1) * 128], ident[:, :])
                for dc in range(DC):
                    mm = pe.transpose(pT2[:, dc * 128:(dc + 1) * 128],
                                      l2b[i % 2][:, dc * 128:(dc + 1) * 128], ident[:, :])
                mm.then_inc(s_p, 1)

        @blk.vector
        def _(v):
            for i in range(NT):
                v.wait_ge(s_a, a1 + (i + 1))
                v.scalar_tensor_tensor(l2b[i % 2][:, :], xnb[i % 2][:, :], rs2[:, i:i + 1],
                                       g_bc[:, :], op0=ALU.mult, op1=ALU.mult)
                v.tensor_tensor(l2b[i % 2][:, :], l2b[i % 2][:, :], bgb_bc[:, :],
                                op=ALU.add).then_inc(s_d, 1)
                v.scalar_tensor_tensor(jk_f[:, :], l2b[i % 2][:, :], -1.0, l2b[i % 2][:, :],
                                       op0=ALU.mult, op1=ALU.max)
                for ch in range(4):
                    v.max(q1cand[:, i * 32 + ch * 8:i * 32 + ch * 8 + 8],
                          jk_f[:, ch * 256:(ch + 1) * 256])
                v.wait_ge(s_p, p1 + i + 1)
                v.tensor_copy(
                    xnT.ap().rearrange("p (dc t) -> p dc t", dc=DC)[:, :, i * 128:(i + 1) * 128],
                    pT1[:, :].rearrange("p (dc t) -> p dc t", dc=DC))
                v.tensor_copy(
                    ln2T.ap().rearrange("p (dc t) -> p dc t", dc=DC)[:, :, i * 128:(i + 1) * 128],
                    pT2[:, :].rearrange("p (dc t) -> p dc t", dc=DC))
    C["x"] += NT * 16
    C["a"] += NT
    C["p"] += NT
    C["d"] += NT

    if stop == 1:
        return finish_early()

    for nm in ("stt2", "stt1", "rs2", "negmr", "rstd", "st_sq", "st_sum",
               "bgb_bc", "g_bc", "l2b1", "l2b0", "xnb1", "xnb0", "l2_f",
               "xn_f1", "xn_f0", "xt1", "xt0", "iot", "identf"):
        free(nm)
    qkvst = [enter(f"qkvst{i}", [128, 512], BF16) for i in range(2)]
    Wq_sb = enter("Wq_sb", [128, DC * D], BF16)
    Wk_sb = enter("Wk_sb", [128, DC * D], BF16)
    with nc.Block() as blk:
        @blk.gpsimd
        def _(gp):
            for wi, Wsb in enumerate((Wq_sb, Wk_sb)):
                for dc in range(DC):
                    gp.dma_start(out=Wsb[:, dc * D:(dc + 1) * D],
                                 in_=W_d[wi][dc * 128:(dc + 1) * 128, :]).then_inc(s_wg, 16)
    C["wg"] += 16 * 16

    d3, a3, p3, sp3, w3qk = C["d"], C["a"], C["p"], C["sp"], C["wg"]

    def qk_spills(u):
        if u < 0:
            return 0
        q = min(u + 1, 64)
        k = max(0, u + 1 - 64)
        return 2 * q + k

    with nc.Block() as blk:
        @blk.tensor
        def _(pe):
            pe.wait_ge(s_wg, w3qk)
            for proj in range(2):
                Wsb = (Wq_sb, Wk_sb)[proj]
                for ts in range(8):
                    for ec in range(DC):
                        u = (proj * 8 + ts) * DC + ec
                        psum = (pA, pB, pO1, pO2)[u % 4]
                        if u >= 4:
                            pe.wait_ge(s_a, a3 + (u - 3))
                        mm = None
                        for dc in range(DC):
                            mm = pe.matmul(psum[:, :],
                                           Wsb[:, dc * D + ec * 128:dc * D + (ec + 1) * 128],
                                           xnT[:, dc * T + ts * 512:dc * T + (ts + 1) * 512],
                                           start=(dc == 0), stop=(dc == DC - 1))
                        mm.then_inc(s_p, 1)

        @blk.scalar
        def _(act):
            for proj in range(2):
                for ts in range(8):
                    for ec in range(DC):
                        u = (proj * 8 + ts) * DC + ec
                        act.wait_ge(s_p, p3 + u + 1)
                        if u >= 2:
                            act.wait_ge(s_sp, sp3 + 16 * qk_spills(u - 2))
                        act.activation(qkvst[u % 2][:, :],
                                       (pA, pB, pO1, pO2)[u % 4][:, :], AF.Identity,
                                       bias=b_sb[:, proj * DC + ec:proj * DC + ec + 1]
                                       ).then_inc(s_a, 1)

        @blk.sync
        def _(sync):
            for proj in range(2):
                for ts in range(8):
                    for ec in range(DC):
                        u = (proj * 8 + ts) * DC + ec
                        sync.wait_ge(s_a, a3 + u + 1)
                        st = qkvst[u % 2]
                        if proj == 0:
                            for h in range(2):
                                sc = ts * 2 + h
                                sync.dma_start(
                                    out=QT_d[sc, ec * 128:(ec + 1) * 128, :],
                                    in_=st[:, h * 256:(h + 1) * 256]).then_inc(s_sp, 16)
                        else:
                            sync.dma_start(
                                out=KT_d[ec * 128:(ec + 1) * 128, ts * 512:(ts + 1) * 512],
                                in_=st[:, :]).then_inc(s_sp, 16)
    C["p"] += 128
    C["a"] += 128
    C["sp"] += 192 * 16

    if stop == 2:
        return finish_early()

    free("Wk_sb", "Wq_sb")
    Wv_sb = enter("Wv_sb", [128, DC * D], BF16)
    b3h = enter("b3h", [128, D], F32)
    d4, a4, p4, sp4, w4c = C["d"], C["a"], C["p"], C["sp"], C["wg"]
    with nc.Block() as blk:
        @blk.gpsimd
        def _(gp):
            gp.dma_start(out=b3h[:, :], in_=b_d[2].partition_broadcast(128)).then_inc(s_wg, 16)
            for dc in range(DC):
                gp.dma_start(out=Wv_sb[:, dc * D:(dc + 1) * D],
                             in_=W_d[2][dc * 128:(dc + 1) * 128, :]).then_inc(s_wg, 16)

        @blk.tensor
        def _(pe):
            pe.wait_ge(s_a, a4)
            pe.wait_ge(s_wg, w4c + 16 * (DC + 1))
            for tt in range(NT):
                for eh in range(2):
                    u = tt * 2 + eh
                    psum = (pA, pB, pO1, pO2)[u % 4]
                    if u >= 4:
                        pe.wait_ge(s_d, d4 + (u - 3))
                    mm = None
                    for dc in range(DC):
                        mm = pe.matmul(psum[:, :],
                                       xnT[:, dc * T + tt * 128:dc * T + (tt + 1) * 128],
                                       Wv_sb[:, dc * D + eh * 512:dc * D + (eh + 1) * 512],
                                       start=(dc == 0), stop=(dc == DC - 1))
                    mm.then_inc(s_p, 1)

        @blk.vector
        def _(v):
            v.wait_ge(s_wg, w4c + 16)
            for tt in range(NT):
                for eh in range(2):
                    u = tt * 2 + eh
                    v.wait_ge(s_p, p4 + u + 1)
                    if u < 2:
                        v.wait_ge(s_sp, sp4)
                    else:
                        v.wait_ge(s_sp, sp4 + 16 * (u - 1))
                    v.tensor_tensor(qkvst[u % 2][:, :], (pA, pB, pO1, pO2)[u % 4][:, :],
                                    b3h[:, eh * 512:(eh + 1) * 512], op=ALU.add).then_inc(s_d, 1)

        @blk.sync
        def _(sync):
            for tt in range(NT):
                for eh in range(2):
                    u = tt * 2 + eh
                    sync.wait_ge(s_d, d4 + u + 1)
                    sync.dma_start(out=V_d[tt * 128:(tt + 1) * 128, eh * 512:(eh + 1) * 512],
                                   in_=qkvst[u % 2][:, :]).then_inc(s_sp, 16)
    C["wg"] += 16 + DC * 16
    C["p"] += 64
    C["d"] += 64
    C["sp"] += 64 * 16

    if stop == 3:
        return finish_early()

    free("b3h", "Wv_sb", "qkvst1", "qkvst0")
    wstA = enter("wstA", [128, 1024], F32)
    wstA2 = enter("wstA2", [128, 1024], F32)
    qk_rl = enter("qk_rl", [128, T], BF16)
    nsq = enter("nsq", [128, T], BF16)
    normQ = enter("normQ", [1, T], BF16)
    normK = enter("normK", [1, T], BF16)
    rl5, a5, p5, d5, w5, bp5, g5 = (C["rl"], C["a"], C["p"], C["d"], C["w"],
                                    C["bp"], C["g"])
    with nc.Block() as blk:
        @blk.sync
        def _(sync):
            sync.wait_ge(s_sp, sp3 + 16 * 192)
            for proj in range(2):
                for ec in range(DC):
                    k = proj * DC + ec
                    if k >= 1:
                        sync.wait_ge(s_a, a5 + k)
                    if proj == 0:
                        for sc in range(SC):
                            sync.dma_start(out=qk_rl[:, sc * 256:(sc + 1) * 256],
                                           in_=QT_d[sc, ec * 128:(ec + 1) * 128, :]
                                           ).then_inc(s_rl, 16)
                    else:
                        sync.dma_start(out=qk_rl[:, :],
                                       in_=KT_d[ec * 128:(ec + 1) * 128, :]).then_inc(s_rl, 16)
            for pc in range(32):
                if pc >= 2:
                    sync.wait_ge(s_a, a5 + 16 + pc - 1)
                sync.dma_start(out=(wstA, wstA2)[pc % 2][:, :],
                               in_=w1T_d[(pc // 4) * 128:(pc // 4 + 1) * 128,
                                         (pc % 4) * 1024:(pc % 4 + 1) * 1024]
                               ).then_inc(s_w, 16)

        @blk.scalar
        def _(act):
            nds = 0
            for proj in range(2):
                for ec in range(DC):
                    k = proj * DC + ec
                    nds += SC if proj == 0 else 1
                    act.wait_ge(s_rl, rl5 + 16 * nds)
                    if k >= 1:
                        act.wait_ge(s_d, d5 + 8 * k)
                    act.activation(nsq[:, :], qk_rl[:, :], AF.Square).then_inc(s_a, 1)
            for pc in range(32):
                act.wait_ge(s_w, w5 + 16 * (pc + 1))
                act.activation(jk_f[:, 0:1024], (wstA, wstA2)[pc % 2][:, :], AF.Abs,
                               accum_out=cstat[:, 32 + pc:33 + pc]).then_inc(s_a, 1)

        @blk.tensor
        def _(pe):
            for k in range(16):
                pe.wait_ge(s_a, a5 + k + 1)
                for ts in range(8):
                    j = k * 8 + ts
                    if j >= 4:
                        pe.wait_ge(s_d, d5 + j - 3)
                    pe.matmul((pR, pO1, pA, pB)[j % 4][0:1, 0:512], ones_col[:, :],
                              nsq[:, ts * 512:(ts + 1) * 512],
                              start=True, stop=True).then_inc(s_p, 1)

        @blk.vector
        def _(v):
            v.memset(normQ[:, :], 0.0)
            v.memset(normK[:, :], 0.0)
            for k in range(16):
                row = normQ if k < 8 else normK
                for ts in range(8):
                    j = k * 8 + ts
                    v.wait_ge(s_p, p5 + j + 1)
                    v.tensor_tensor(row[0:1, ts * 512:(ts + 1) * 512],
                                    row[0:1, ts * 512:(ts + 1) * 512],
                                    (pR, pO1, pA, pB)[j % 4][0:1, 0:512],
                                    op=ALU.add).then_inc(s_d, 1)
            v.wait_ge(s_a, a5 + 16 + 32)
            v.reduce_sum(cstat[:, 0:1], cstat[:, 32:64], axis=AX)
            vsp(v).then_inc(s_d, 1)

        @blk.gpsimd
        def _(gp):
            gp.wait_ge(s_d, d5 + 129)
            gp.dma_start(out=cc_in_t[0:128].rearrange("(p c) -> p c", p=128),
                         in_=cstat[:, 0:1]).then_inc(s_bp, 16)
            gp.wait_ge(s_bp, bp5 + 16)
            gp.dma_start(out=srow[0:1, 0:128],
                         in_=cc_in_t[0:128].rearrange("(q n) -> q n", q=1)).then_inc(s_bp, 16)
            gp.wait_ge(s_bp, bp5 + 32)
            gp.engine_nop().then_inc(s_g, 1)

        @blk.vector
        def _(v):
            v.wait_ge(s_g, g5 + 1)
            v.reduce_sum(bis[0:1, B_GAM:B_GAM + 1], srow[0:1, 0:128], axis=AX)
            vsp(v)
            v.tensor_scalar(bis[0:1, B_GAM:B_GAM + 1], bis[0:1, B_GAM:B_GAM + 1],
                            NW, 1e-5, op0=ALU.mult, op1=ALU.add)
            vsp(v)
            v.reciprocal(bis[0:1, B_T0:B_T0 + 1], bis[0:1, B_GAM:B_GAM + 1])
            vsp(v).then_inc(s_d, 1)

        @blk.gpsimd
        def _(gp):
            gp.wait_ge(s_d, d5 + 130)
            gp.dma_start(out=row_t.ap()[0:1, 4:5],
                         in_=bis[0:1, B_T0:B_T0 + 1]).then_inc(s_bp, 16)
            gp.wait_ge(s_bp, bp5 + 48)
            gp.dma_start(out=sc1[:, 4:5],
                         in_=row_t.ap()[0:1, 4:5].partition_broadcast(128)).then_inc(s_bp, 16)
            gp.wait_ge(s_bp, bp5 + 64)
            gp.engine_nop().then_inc(s_g, 1)
    C["rl"] += (8 * SC + 8) * 16
    C["w"] += 32 * 16
    C["a"] += 16 + 32
    C["p"] += 128
    C["d"] += 128 + 1 + 1
    C["bp"] += 64
    C["g"] += 2

    a6i, d6i = C["a"], C["d"]
    with nc.Block() as blk:
        @blk.scalar
        def _(act):
            act.wait_ge(s_d, d6i - 2)
            act.activation(normQ[0:1, :], normQ[0:1, :], AF.Sqrt).then_inc(s_a, 1)

        @blk.vector
        def _(v):
            v.wait_ge(s_a, a6i + 1)
            v.reduce_max(bis[0:1, B_T1:B_T1 + 1], normK[0:1, :], axis=AX).then_inc(s_d, 1)

        @blk.scalar
        def _(act):
            act.wait_ge(s_d, d6i + 1)
            act.activation(bis[0:1, B_T1:B_T1 + 1], bis[0:1, B_T1:B_T1 + 1],
                           AF.Sqrt).then_inc(s_a, 1)

        @blk.vector
        def _(v):
            v.wait_ge(s_a, a6i + 2)
            v.tensor_scalar(m32[0:1, :], normQ[0:1, :], bis[0:1, B_T1:B_T1 + 1],
                            -1.0, op0=ALU.mult, op1=ALU.mult).then_inc(s_d, 1)
    C["a"] += 2
    C["d"] += 2
    free("normK", "normQ")

    if stop == 4:
        return finish_early()

    base1 = dict(a=C["a"], d=C["d"], bp=C["bp"], cc=C["cc"], bv=C["bv"])
    base1["a_of_round"] = lambda j, _a=C["a"]: _a + j + 1
    acts1, dves1, gp1, n1 = make_bisect("q1", q1cand, 1024, N1_CAND, R1, FRAC1,
                                        s_slot=0, gs_slot=1, base=base1, ro=8,
                                        dbg=dbg_t.ap() if DBG_Q1 else None,
                                        nthr=7, nrnd=NR1, frc=B_FRC7)
    with nc.Block() as blk:
        @blk.vector
        def _(v):
            for f in dves1:
                f(v)

        @blk.scalar
        def _(act):
            for f in acts1:
                f(act)

        @blk.gpsimd
        def _(gp):
            gp1(gp)
    C["a"] += n1["a"]
    C["bv"] += n1["bv"]
    C["bp"] += n1["bp"]
    C["cc"] += n1["cc"]
    C["g"] += n1["g"]

    if stop == 5:
        return finish_early()

    w7g, a7g, d7g, bp7g, g7g = C["w"], C["a"], C["d"], C["bp"], C["g"]
    with nc.Block() as blk:
        @blk.sync
        def _(sync):
            for ch in range(32):
                if ch >= 2:
                    sync.wait_ge(s_a, a7g + ch - 1)
                sync.dma_start(out=(wstA, wstA2)[ch % 2][:, :],
                               in_=w2T_d[ch * 128:(ch + 1) * 128, :]).then_inc(s_w, 16)

        @blk.scalar
        def _(act):
            for ch in range(32):
                act.wait_ge(s_w, w7g + 16 * (ch + 1))
                act.activation(jk_f[:, 0:1024], (wstA, wstA2)[ch % 2][:, :], AF.Abs,
                               accum_out=cstat[:, 32 + ch:33 + ch]).then_inc(s_a, 1)

        @blk.vector
        def _(v):
            v.wait_ge(s_a, a7g + 32)
            v.reduce_sum(cstat[:, 0:1], cstat[:, 32:64], axis=AX)
            vsp(v).then_inc(s_d, 1)

        @blk.gpsimd
        def _(gp):
            gp.wait_ge(s_d, d7g + 1)
            gp.dma_start(out=cc_in_t[0:128].rearrange("(p c) -> p c", p=128),
                         in_=cstat[:, 0:1]).then_inc(s_bp, 16)
            gp.wait_ge(s_bp, bp7g + 16)
            gp.dma_start(out=srow[0:1, 256:384],
                         in_=cc_in_t[0:128].rearrange("(q n) -> q n", q=1)).then_inc(s_bp, 16)
            gp.wait_ge(s_bp, bp7g + 32)
            gp.engine_nop().then_inc(s_g, 1)

        @blk.vector
        def _(v):
            v.wait_ge(s_g, g7g + 1)
            v.reduce_sum(bis[0:1, B_GAM:B_GAM + 1], srow[0:1, 256:384], axis=AX)
            vsp(v)
            v.tensor_scalar(bis[0:1, B_GAM:B_GAM + 1], bis[0:1, B_GAM:B_GAM + 1],
                            NW, 1e-5, op0=ALU.mult, op1=ALU.add)
            vsp(v)
            v.reciprocal(bis[0:1, B_T1:B_T1 + 1], bis[0:1, B_GAM:B_GAM + 1])
            vsp(v).then_inc(s_d, 1)

        @blk.gpsimd
        def _(gp):
            gp.wait_ge(s_d, d7g + 2)
            gp.dma_start(out=row_t.ap()[0:1, 5:6],
                         in_=bis[0:1, B_T1:B_T1 + 1]).then_inc(s_bp, 16)
            gp.wait_ge(s_bp, bp7g + 48)
            gp.dma_start(out=sc1[:, 5:6],
                         in_=row_t.ap()[0:1, 5:6].partition_broadcast(128)).then_inc(s_bp, 16)
            gp.wait_ge(s_bp, bp7g + 64)
            gp.engine_nop().then_inc(s_g, 1)
    C["w"] += 32 * 16
    C["a"] += 32
    C["d"] += 2
    C["bp"] += 64
    C["g"] += 2

    d7, g7b, w7 = C["d"], C["g"], C["w"]
    wst7 = [wstA, wstA2]
    with nc.Block() as blk:
        @blk.vector
        def _(v):
            v.wait_ge(s_g, g7b)
            for ts in range(8):
                for dc in range(DC):
                    k = dc * 8 + ts
                    sl_in = ln2T[:, k * 512:(k + 1) * 512]
                    sl_out = xqT[:, k * 512:(k + 1) * 512]
                    v.tensor_scalar(jk_f[:, 0:512], sl_in, sc1[:, 0:1], None, op0=ALU.mult)
                    v.tensor_scalar(jk_f[:, 0:512], jk_f[:, 0:512], MAGIC, MAGIC,
                                    op0=ALU.add, op1=ALU.subtract)
                    mm = v.tensor_scalar(sl_out, jk_f[:, 0:512], 127.0, -128.0,
                                         op0=ALU.min, op1=ALU.max)
                mm.then_inc(s_d, 1)

        @blk.sync
        def _(sync):
            sync.wait_ge(s_g, g7b)
            for pq in range(32):
                q, dc = pq // 8, pq % 8
                if pq >= 2:
                    sync.wait_ge(s_d, d7 + 8 + pq - 1)
                sync.dma_start(out=wst7[pq % 2][:, :],
                               in_=w1T_d[dc * 128:(dc + 1) * 128,
                                         q * 1024:(q + 1) * 1024]
                               ).then_inc(s_w, 16)

        @blk.vector
        def _(v):
            for pq in range(32):
                q, dc = pq // 8, pq % 8
                v.wait_ge(s_w, w7 + 16 * (pq + 1))
                v.tensor_scalar(jk_f[:, :], wst7[pq % 2][:, :], sc1[:, 4:5], None,
                                op0=ALU.mult)
                v.tensor_scalar(jk_f[:, :], jk_f[:, :], MAGIC, MAGIC,
                                op0=ALU.add, op1=ALU.subtract)
                v.tensor_scalar(w1q[:, dc * HID + q * 1024:dc * HID + (q + 1) * 1024],
                                jk_f[:, :], 1.0, -1.0, op0=ALU.min,
                                op1=ALU.max).then_inc(s_d, 1)
    C["d"] += 8 + 32
    C["w"] += 32 * 16
    free("nsq", "qk_rl", "wstA2", "wstA", "q1cand")
    q2cand = enter("q2cand", [128, 4096], F32)

    if stop == 6:
        return finish_early()

    gstage = enter("gstage", [128, HID], BF16)
    gst = [enter(f"gst{i}", [128, 512], F32) for i in range(2)]
    ab_f = [enter(f"ab_f{i}", [128, 512], F32) for i in range(2)]
    p8, a8, d8, gd8 = C["p"], C["a"], C["d"], C["sp"]
    with nc.Block() as blk:
        @blk.tensor
        def _(pe):
            for hc in range(HC):
                for ts in range(8):
                    u = hc * 8 + ts
                    if ts == 0:
                        pe.wait_ge(s_d, d8 - 32 + (hc // 8 + 1) * 8)
                    psum = (pA, pB, pO1, pO2)[u % 4]
                    if u >= 4:
                        pe.wait_ge(s_a, a8 + 2 * (u - 3))
                    mm = None
                    for dc in range(DC):
                        mm = pe.matmul(psum[:, :],
                                       w1q[:, dc * T + hc * 128:dc * T + (hc + 1) * 128],
                                       xqT[:, dc * T + ts * 512:dc * T + (ts + 1) * 512],
                                       start=(dc == 0), stop=(dc == DC - 1))
                    mm.then_inc(s_p, 1)

        @blk.scalar
        def _(act):
            for hc in range(HC):
                for ts in range(8):
                    u = hc * 8 + ts
                    act.wait_ge(s_p, p8 + u + 1)
                    if u >= 2:
                        act.wait_ge(s_d, d8 + (u - 1))
                    act.activation(gst[u % 2][:, :],
                                   (pA, pB, pO1, pO2)[u % 4][:, :], AF.Gelu,
                                   scale=sc1[:, 1:2])
                    act.activation(ab_f[u % 2][:, :], gst[u % 2][:, :],
                                   AF.Abs).then_inc(s_a, 2)

        @blk.vector
        def _(v):
            for hc in range(HC):
                if hc >= 1:
                    v.wait_ge(s_sp, gd8 + 16 * hc)
                for ts in range(8):
                    u = hc * 8 + ts
                    v.wait_ge(s_a, a8 + 2 * (u + 1))
                    cb = u * 16
                    v.max(q2cand[:, cb:cb + 8], ab_f[u % 2][:, 0:256])
                    v.max(q2cand[:, cb + 8:cb + 16], ab_f[u % 2][:, 256:512])
                    v.tensor_copy(gstage[:, ts * 512:(ts + 1) * 512],
                                  gst[u % 2][:, :]).then_inc(s_d, 1)

        @blk.sync
        def _(sync):
            for hc in range(HC):
                sync.wait_ge(s_d, d8 + (hc + 1) * 8)
                sync.dma_start(
                    out=gT_d[:, hc * 128:(hc + 1) * 128, :].rearrange("tt h t -> h tt t"),
                    in_=gstage[:, :].rearrange("p (tt t) -> p tt t", tt=NT)
                ).then_inc(s_sp, 16)
    C["p"] += 256
    C["a"] += 512
    C["d"] += 256
    C["sp"] += HC * 16
    free("ab_f1", "ab_f0", "gst1", "gst0", "gstage")

    if stop == 7:
        return finish_early()

    W4_sb = enter("W4_sb", [128, DC * D], BF16)
    w9, rl9, d9, sp9 = C["w"], C["rl"], C["d"], C["sp"]
    p9pe = C["p"]
    qt_sb = enter("qt_sb", [128, DC * 256], BF16)
    A_sb = enter("A_sb", [128, NT * 256], BF16)
    OT_sb = enter("OT_sb", [128, DC * 256], BF16)
    cb_sb = jk_f[:, 768:1024]
    crow = mk_f[0:1, 0:256]
    apst = enter("apst", [128, 512], BF16)
    with nc.Block() as blk:
        @blk.sync
        def _(sync):
            sync.wait_ge(s_sp, sp3 + 16 * 192 + 64 * 16)
            sync.wait_ge(s_p, p9pe)
            for ec in range(DC):
                sync.dma_start(out=KTs[:, ec * T:(ec + 1) * T],
                               in_=KT_d[ec * 128:(ec + 1) * 128, :]).then_inc(s_rl, 16)

        @blk.gpsimd
        def _(gp):
            gp.wait_ge(s_sp, sp9)
            for dc in range(DC):
                gp.dma_start(out=W4_sb[:, dc * D:(dc + 1) * D],
                             in_=W_d[3][dc * 128:(dc + 1) * 128, :]).then_inc(s_wg, 16)
    C["rl"] += DC * 16
    C["wg"] += DC * 16

    if stop == 8:
        return finish_early()

    PE_PC = 32 + 32 + 1 + DC + 4
    w10v = C["w"]
    ACT_PC = 32
    DVE_PC = 1 + 1 + DC + 4
    p10, a10, d10, rl10, sp10 = C["p"], C["a"], C["p"] * 0 + C["d"], C["rl"], C["sp"]

    def act_rounds_before(c):
        return max(0, min(c - 1, NR + 1))

    def _a_of_round(j):
        c = j + 1
        return a10 + (c + 1) * ACT_PC + act_rounds_before(c) + 1

    base2 = dict(a=a10, d=d10, bp=C["bp"], cc=C["cc"], bv=C["bv"],
                 a_of_round=_a_of_round)
    acts2, dves2, gp2, n2 = make_bisect("q2", q2cand, 4096, N2_CAND, R2, FRAC2,
                                        s_slot=2, gs_slot=3, base=base2, ro=12,
                                        nthr=3, nrnd=NR, frc=B_FRC3)
    with nc.Block() as blk:
        @blk.sync
        def _(sync):
            sync.wait_ge(s_rl, rl9 + DC * 16)
            for sc in range(SC):
                if sc >= 1:
                    sync.wait_ge(s_p, p10 + (sc - 1) * PE_PC + 32)
                sync.dma_start(out=qt_sb[:, :].rearrange("p (ec t) -> p ec t", ec=DC),
                               in_=QT_d[sc, :, :].rearrange("(ec p) t -> p ec t", p=128)
                               ).then_inc(s_rl, 16)
                if sc == 0:
                    for tt in range(NT):
                        sync.dma_start(out=Vs[:, tt * D:(tt + 1) * D],
                                       in_=V_d[tt * 128:(tt + 1) * 128, :]
                                       ).then_inc(s_w, 16)
                for g in range(4):
                    sync.wait_ge(s_d, d10 + sc * DVE_PC + 2 + DC + g + 1)
                    st, dh = g // 2, g % 2
                    sync.dma_start(
                        out=AP_d[sc * 256 + st * 128: sc * 256 + (st + 1) * 128,
                                 dh * 512:(dh + 1) * 512],
                        in_=apst[:, :]).then_inc(s_sp, 16)

        @blk.tensor
        def _(pe):
            pe.wait_ge(s_wg, C["wg"])
            for sc in range(SC):
                pe.wait_ge(s_rl, rl10 + 16 * (sc + 1))
                for tt in range(NT):
                    psum = (pA, pB)[tt % 2]
                    if tt < 2 and sc >= 1:
                        pe.wait_ge(s_d, d10 + sc * DVE_PC)
                    if tt >= 2:
                        pe.wait_ge(s_a, a10 + sc * ACT_PC + act_rounds_before(sc) + (tt - 1))
                    mm = None
                    for ec in range(DC):
                        mm = pe.matmul(psum[:, 0:256],
                                       KTs[:, ec * T + tt * 128:ec * T + (tt + 1) * 128],
                                       qt_sb[:, ec * 256:(ec + 1) * 256],
                                       start=(ec == 0), stop=False)
                    mm = pe.matmul(psum[:, 0:256], ones_rowb[:, :],
                                   m32[0:1, sc * 256:(sc + 1) * 256],
                                   start=False, stop=True)
                    mm.then_inc(s_p, 1)
                pe.wait_ge(s_a, a10 + (sc + 1) * ACT_PC + act_rounds_before(sc))
                for tt in range(NT):
                    pe.matmul(pR[0:1, 0:256], ones_col[:, :],
                              A_sb[:, tt * 256:(tt + 1) * 256],
                              start=(tt == 0), stop=(tt == NT - 1)).then_inc(s_p, 1)
                pe.wait_ge(s_d, d10 + sc * DVE_PC + 1)
                pe.matmul(pR[:, 256:512], ones_row[:, :], crow,
                          start=True, stop=True).then_inc(s_p, 1)
                for ec in range(DC):
                    psum = (pO1, pO2)[ec % 2]
                    if sc == 0 and ec == 0:
                        pe.wait_ge(s_w, w10v + NT * 16)
                    if ec >= 2:
                        pe.wait_ge(s_d, d10 + sc * DVE_PC + 2 + (ec - 1))
                    mm = None
                    for tt in range(NT):
                        mm = pe.matmul(psum[:, 0:256],
                                       Vs[:, tt * D + ec * 128:tt * D + (ec + 1) * 128],
                                       A_sb[:, tt * 256:(tt + 1) * 256],
                                       start=(tt == 0), stop=(tt == NT - 1))
                    mm.then_inc(s_p, 1)
                for g in range(4):
                    st, dh = g // 2, g % 2
                    psum = (pA, pB)[g % 2]
                    if g == 0:
                        pe.wait_ge(s_d, d10 + sc * DVE_PC + 2 + DC)
                    if g >= 2:
                        pe.wait_ge(s_d, d10 + sc * DVE_PC + 2 + DC + (g - 1))
                    mm = None
                    for ec in range(DC):
                        mm = pe.matmul(psum[:, :],
                                       OT_sb[:, ec * 256 + st * 128:ec * 256 + (st + 1) * 128],
                                       W4_sb[:, ec * D + dh * 512:ec * D + (dh + 1) * 512],
                                       start=(ec == 0), stop=(ec == DC - 1))
                    mm.then_inc(s_p, 1)

        @blk.scalar
        def _(act):
            for sc in range(SC):
                for tt in range(NT):
                    act.wait_ge(s_p, p10 + sc * PE_PC + tt + 1)
                    act.activation(A_sb[:, tt * 256:(tt + 1) * 256],
                                   (pA, pB)[tt % 2][:, 0:256], AF.Exp,
                                   scale=0.03125).then_inc(s_a, 1)
                if 1 <= sc <= NR + 1:
                    acts2[sc - 1](act)

        @blk.vector
        def _(v):
            dves2[0](v)
            dves2[1](v)
            di = 2
            for sc in range(SC):
                v.wait_ge(s_p, p10 + sc * PE_PC + 32 + 32)
                v.reciprocal(crow, pR[0:1, 0:256]).then_inc(s_d, 1)
                v.wait_ge(s_p, p10 + sc * PE_PC + 32 + 32 + 1)
                v.tensor_copy(cb_sb, pR[:, 256:512]).then_inc(s_d, 1)
                for ec in range(DC):
                    v.wait_ge(s_p, p10 + sc * PE_PC + 65 + ec + 1)
                    v.tensor_tensor(OT_sb[:, ec * 256:(ec + 1) * 256],
                                    (pO1, pO2)[ec % 2][:, 0:256], cb_sb,
                                    op=ALU.mult).then_inc(s_d, 1)
                for g in range(4):
                    v.wait_ge(s_p, p10 + sc * PE_PC + 65 + DC + g + 1)
                    if sc * 4 + g >= 1:
                        v.wait_ge(s_sp, sp10 + 16 * (sc * 4 + g))
                    st, dh = g // 2, g % 2
                    v.tensor_tensor(apst[:, :], (pA, pB)[g % 2][:, :],
                                    b4_bc[:, dh * 512:(dh + 1) * 512],
                                    op=ALU.add).then_inc(s_d, 1)
                if di < len(dves2) and sc <= NR:
                    dves2[di](v)
                    di += 1
            while di < len(dves2):
                dves2[di](v)
                di += 1

        @blk.gpsimd
        def _(gp):
            gp2(gp)
    C["p"] += SC * PE_PC
    C["w"] += NT * 16
    C["a"] += SC * ACT_PC + n2["a"]
    C["d"] += SC * DVE_PC
    C["bv"] += n2["bv"]
    C["rl"] += SC * 16
    C["sp"] += SC * 4 * 16
    C["bp"] += n2["bp"]
    C["cc"] += n2["cc"]
    C["g"] += n2["g"]
    free("apst", "OT_sb", "A_sb", "qt_sb", "W4_sb", "q2cand")

    if stop == 9:
        return finish_early()

    wstB = [enter(f"wstB{i}", [128, 1024], F32) for i in range(2)]
    w11, d11 = C["w"], C["d"]
    with nc.Block() as blk:
        @blk.sync
        def _(sync):
            for ch in range(HC):
                if ch >= 2:
                    sync.wait_ge(s_d, d11 + ch - 1)
                sync.dma_start(out=wstB[ch % 2][:, :],
                               in_=w2T_d[ch * 128:(ch + 1) * 128, :]).then_inc(s_w, 16)

        @blk.vector
        def _(v):
            for ch in range(HC):
                v.wait_ge(s_w, w11 + 16 * (ch + 1))
                v.tensor_scalar(jk_f[:, :], wstB[ch % 2][:, :], sc1[:, 5:6], None,
                                op0=ALU.mult)
                v.tensor_scalar(jk_f[:, :], jk_f[:, :], MAGIC, MAGIC,
                                op0=ALU.add, op1=ALU.subtract)
                v.tensor_scalar(w2q[:, ch * D:(ch + 1) * D], jk_f[:, :],
                                1.0, -1.0, op0=ALU.min, op1=ALU.max).then_inc(s_d, 1)
    C["w"] += HC * 16
    C["d"] += HC

    if stop == 10:
        return finish_early()

    free("wstB1", "wstB0")
    gload = enter("gload", [128, HID], BF16)
    gq = [enter(f"gq{i}", [128, HID], BF16) for i in range(2)]
    gs1_f = enter("gs1_f", [128, HID // 2], F32)
    xt2 = [enter(f"xt2{i}", [128, D], F32) for i in range(2)]
    apt = [enter(f"apt{i}", [128, D], BF16) for i in range(2)]
    outst = [enter(f"outst{i}", [128, D], F32) for i in range(2)]
    rl12, d12, p12, st12, sp12b = C["rl"], C["d"], C["p"], C["st"], C["sp"]
    wg12 = C["wg"]
    with nc.Block() as blk:
        @blk.sync
        def _(sync):
            sync.wait_ge(s_sp, sp12b)
            sync.wait_ge(s_sp, C["sp"])
            sync.wait_ge(s_d, d12)
            for tt in range(NT):
                if tt >= 1:
                    sync.wait_ge(s_d, d12 + (tt - 1) * 4 + 1)
                sync.dma_start(out=gload[:, :].rearrange("p (hc t) -> p hc t", hc=HC),
                               in_=gT_d[tt, :, :].rearrange("(hc p) t -> p hc t", p=128)
                               ).then_inc(s_rl, 16)
                if tt >= 2:
                    sync.wait_ge(s_d, d12 + (tt - 2) * 4 + 4)
                sync.dma_start(out=xt2[tt % 2][:, :],
                               in_=x_d[tt * 128:(tt + 1) * 128, :]).then_inc(s_rl, 16)
                sync.dma_start(out=apt[tt % 2][:, :],
                               in_=AP_d[tt * 128:(tt + 1) * 128, :]).then_inc(s_rl, 16)


        @blk.vector
        def _(v):
            for tt in range(NT):
                v.wait_ge(s_rl, rl12 + 16 * (3 * tt + 1))
                if tt >= 2:
                    v.wait_ge(s_p, p12 + 2 * (tt - 1))
                for hh in range(2):
                    sl = slice(hh * (HID // 2), (hh + 1) * (HID // 2))
                    v.tensor_scalar(gs1_f[:, :], gload[:, sl], sc1[:, 2:3], None,
                                    op0=ALU.mult)
                    v.tensor_scalar(gs1_f[:, :], gs1_f[:, :], MAGIC, MAGIC,
                                    op0=ALU.add, op1=ALU.subtract)
                    mm = v.tensor_scalar(gq[tt % 2][:, sl], gs1_f[:, :], 127.0, -128.0,
                                         op0=ALU.min, op1=ALU.max)
                mm.then_inc(s_d, 1)
                if tt >= 2:
                    v.wait_ge(s_st, st12 + 16 * (tt - 1))
                for dh in range(2):
                    v.wait_ge(s_p, p12 + tt * 2 + dh + 1)
                    psum = (pA, pB, pO1, pO2)[(tt * 2 + dh) % 4]
                    v.tensor_scalar(outst[tt % 2][:, dh * 512:(dh + 1) * 512],
                                    psum[:, :], sc1[:, 3:4], None, op0=ALU.mult).then_inc(s_d, 1)
                v.wait_ge(s_rl, rl12 + 16 * (3 * tt + 3))
                v.tensor_tensor(outst[tt % 2][:, :], outst[tt % 2][:, :],
                                xt2[tt % 2][:, :], op=ALU.add)
                v.tensor_tensor(outst[tt % 2][:, :], outst[tt % 2][:, :],
                                apt[tt % 2][:, :], op=ALU.add).then_inc(s_d, 1)

        @blk.tensor
        def _(pe):
            pe.wait_ge(s_d, d12)
            for tt in range(NT):
                pe.wait_ge(s_d, d12 + tt * 4 + 1)
                for dh in range(2):
                    psum = (pA, pB, pO1, pO2)[(tt * 2 + dh) % 4]
                    if tt >= 2:
                        pe.wait_ge(s_d, d12 + (tt - 2) * 4 + 2 + dh)
                    mm = None
                    for hc in range(HC):
                        mm = pe.matmul(psum[:, :],
                                       gq[tt % 2][:, hc * 128:(hc + 1) * 128],
                                       w2q[:, hc * D + dh * 512:hc * D + (dh + 1) * 512],
                                       start=(hc == 0), stop=(hc == HC - 1))
                    mm.then_inc(s_p, 1)

        @blk.scalar
        def _(act):
            for tt in range(NT):
                act.wait_ge(s_d, d12 + tt * 4 + 4)
                act.dma_start(out=out_d[tt * 128:(tt + 1) * 128, :],
                              in_=outst[tt % 2][:, :]).then_inc(s_st, 16)
            act.wait_ge(s_st, st12 + NT * 16)
    C["rl"] += NT * 3 * 16
    C["d"] += NT * 4
    C["p"] += NT * 2
    C["st"] += NT * 16

    es.close()
    return nc


_NC_CACHE = {}


def kernel(**inputs):
    x = np.ascontiguousarray(np.asarray(inputs["L0_attn_x"], dtype=np.float32))
    w1T = np.ascontiguousarray(np.asarray(inputs["L0_ffn_w1"], dtype=np.float32).T)
    w2T = np.ascontiguousarray(np.asarray(inputs["L0_ffn_w2"], dtype=np.float32).T)
    common = {
        "w1T": w1T,
        "w2T": w2T,
        "L0_ffn_ln_g": np.ascontiguousarray(np.asarray(inputs["L0_ffn_ln_g"], np.float32)),
        "L0_ffn_ln_b": np.ascontiguousarray(np.asarray(inputs["L0_ffn_ln_b"], np.float32)),
    }
    for i in (1, 2, 3, 4):
        common[f"L0_attn_W{i}"] = np.ascontiguousarray(
            np.asarray(inputs[f"L0_attn_W{i}"], np.float32))
        common[f"L0_attn_b{i}"] = np.ascontiguousarray(
            np.asarray(inputs[f"L0_attn_b{i}"], np.float32))

    if "nc" not in _NC_CACHE:
        _NC_CACHE["nc"] = build_program()
    nc = _NC_CACHE["nc"]

    in_maps = []
    for c in range(8):
        m = dict(common)
        m["L0_attn_x"] = np.ascontiguousarray(x[c])
        in_maps.append(m)
    res = run_bass_kernel_spmd(nc, in_maps, core_ids=list(range(8)))
    return np.stack([r["out"] for r in res.results], axis=0)

